# version 16
# baseline (speedup 1.0000x reference)
import numpy as np
import ml_dtypes
import concourse.bass as bass
import concourse.mybir as mybir
from concourse.bass_utils import run_bass_kernel_spmd

F32 = mybir.dt.float32
BF16 = mybir.dt.bfloat16
AF = mybir.ActivationFunctionType
ALU = mybir.AluOpType
AX = mybir.AxisListType

D = 2048
H = 16
EH = 128
CH = 64
SEQ = 2048
BPC = 2
TA = 256
NTT = TA // 128
NCH = TA // CH
APR = SEQ // TA
NATG = BPC * APR
NTR = SEQ // 512
SLOT = [[0, 1, 2, 3], [4, 5, 0, 1]]
TB = 1024
NBT = (BPC * SEQ) // TB
HC = 44
FH = 5632
KC = 16
NCORE = 8

DEV_SCRATCH_OUT = False
DEV_TAPS = ()
import os as _os
DEV_NBT = int(_os.environ.get("KMK_NBT", NBT))
DEV_PREP = int(_os.environ.get("KMK_PREP", 1))
DEV_A2 = int(_os.environ.get("KMK_A2", 1))
DEV_NROW = int(_os.environ.get("KMK_NROW", BPC))


class Sched:
    STREAMS = ("pe", "dv", "ac", "qs", "qg")
    ESEMS = ("pe", "dv", "ac", "qg")

    def __init__(self, nc):
        self.nc = nc
        self.ops = {s: [] for s in self.STREAMS}
        self.sem = {n: nc.alloc_semaphore(f"S_{n}") for n in self.ESEMS}
        self.cnt = {n: 0 for n in self.ESEMS}
        self.lastw = {s: {} for s in self.STREAMS}

    def _dmasem(self, ch):
        if ch not in self.sem:
            self.sem[ch] = self.nc.alloc_semaphore(f"S_{ch}")
            self.cnt[ch] = 0
        return self.sem[ch]

    def _flat(self, deps, out):
        for d in deps:
            if d is None:
                continue
            if isinstance(d, list):
                self._flat(d, out)
            else:
                out.append(d)

    def emit(self, stream, fn, deps=(), sig=None, dma_ch=None):
        flat = []
        self._flat(list(deps), flat)
        need = {}
        for (sname, val) in flat:
            if sname == stream and stream == "pe":
                continue
            if val > need.get(sname, 0):
                need[sname] = val
        waits = []
        for sname, val in need.items():
            cur = self.lastw[stream].get(sname, 0)
            if val > cur:
                waits.append((self.sem[sname], val))
                self.lastw[stream][sname] = val
        tok = None
        inc = None
        if dma_ch is not None:
            self._dmasem(dma_ch)
            self.cnt[dma_ch] += 16
            tok = (dma_ch, self.cnt[dma_ch])
            inc = (self.sem[dma_ch], 16)
        elif sig:
            self.cnt[sig] += 1
            tok = (sig, self.cnt[sig])
            inc = (self.sem[sig], 1)
        self.ops[stream].append((waits, fn, inc))
        return tok

    def pe(self, fn, deps=(), sig=False):
        return self.emit("pe", fn, deps, "pe" if sig else None)

    def dv(self, fn, deps=(), sig=False):
        return self.emit("dv", fn, deps, "dv" if sig else None)

    def ac(self, fn, deps=(), sig=False):
        return self.emit("ac", fn, deps, "ac" if sig else None)

    def qs(self, fn, deps=(), ch="g_qs"):
        return self.emit("qs", fn, deps, dma_ch=ch)

    def qg(self, fn, deps=(), ch="g_qg"):
        return self.emit("qg", fn, deps, dma_ch=ch)

    def qa(self, fn, deps=(), ch="g_qa"):
        return self.emit("ac", fn, deps, dma_ch=ch)

    def fence(self):
        return [(n, self.cnt[n]) for n in self.cnt]

    def replay(self, block):
        S = self

        def mk(stream):
            def run(eng):
                for waits, fn, inc in S.ops[stream]:
                    for (sem, val) in waits:
                        eng.wait_ge(sem, val)
                    ins = fn(eng)
                    if inc is not None:
                        ins.then_inc(*inc)
            return run

        block.tensor(mk("pe"))
        block.vector(mk("dv"))
        block.scalar(mk("ac"))
        block.sync(mk("qs"))
        block.gpsimd(mk("qg"))


class Bank:
    def __init__(self, ap):
        self.ap = ap
        self.ready = None
        self.free = []

    def wdeps(self):
        return list(self.free)

    def wrote(self, tok):
        if tok is not None:
            self.ready = tok
        self.free = []

    def rdeps(self):
        return [self.ready]

    def read(self, tok):
        if tok is not None:
            self.free.append(tok)


def _host_layouts(inp):
    wq = np.asarray(inp["L0_attn_Wq"], np.float32)
    wk = np.asarray(inp["L0_attn_Wk"], np.float32)
    wv = np.asarray(inp["L0_attn_Wv"], np.float32)
    wo = np.asarray(inp["L0_attn_Wo"], np.float32)
    wf = np.asarray(inp["L0_attn_Wf"], np.float32)
    nw = np.asarray(inp["L0_attn_norm_w"], np.float32)
    w1 = np.asarray(inp["L0_ffn_w1"], np.float32)
    w2 = np.asarray(inp["L0_ffn_w2"], np.float32)
    w3 = np.asarray(inp["L0_ffn_w3"], np.float32)

    def fc_major(w):
        return np.ascontiguousarray(
            w.reshape(KC, 128, KC, 128).transpose(2, 1, 0, 3).reshape(KC, 128, D))

    return {
        "wq_l": fc_major(wq),
        "wk_l": fc_major(wk),
        "wv_l": fc_major(wv),
        "wo_l": fc_major(wo),
        "wf_l": np.ascontiguousarray(
            wf.reshape(KC, 128, H).transpose(1, 0, 2).reshape(128, KC * H)),
        "nw_col": np.ascontiguousarray(nw.reshape(KC, 128).T),
        "w1_l": np.ascontiguousarray(
            w1.reshape(HC, 128, KC, 128).transpose(0, 3, 2, 1).reshape(HC, 128, D)),
        "w2_l": np.ascontiguousarray(
            w2.reshape(HC, 128, KC, 128).transpose(0, 3, 2, 1).reshape(HC, 128, D)),
        "w3_l": np.ascontiguousarray(
            w3.reshape(KC, 128, HC, 128).transpose(0, 3, 2, 1).reshape(KC, 128, FH)),
    }


def build_program():
    nc = bass.Bass("TRN2", target_bir_lowering=False, detect_race_conditions=False)

    x_hbm = nc.dram_tensor("x", [BPC, SEQ, D], F32, kind="ExternalInput")
    wq_l = nc.dram_tensor("wq_l", [KC, 128, D], F32, kind="ExternalInput")
    wk_l = nc.dram_tensor("wk_l", [KC, 128, D], F32, kind="ExternalInput")
    wv_l = nc.dram_tensor("wv_l", [KC, 128, D], F32, kind="ExternalInput")
    wo_l = nc.dram_tensor("wo_l", [KC, 128, D], F32, kind="ExternalInput")
    wf_l = nc.dram_tensor("wf_l", [128, KC * H], F32, kind="ExternalInput")
    nw_hbm = nc.dram_tensor("nw_col", [128, KC], F32, kind="ExternalInput")
    w1_l = nc.dram_tensor("w1_l", [HC, 128, D], F32, kind="ExternalInput")
    w2_l = nc.dram_tensor("w2_l", [HC, 128, D], F32, kind="ExternalInput")
    w3_l = nc.dram_tensor("w3_l", [KC, 128, FH], F32, kind="ExternalInput")
    out_hbm = nc.dram_tensor("out", [BPC, SEQ, D], F32, kind="ExternalOutput")

    skind = "ExternalOutput" if DEV_SCRATCH_OUT else "Internal"
    q1t = nc.dram_tensor("q1t", [HC, 128, D], BF16, kind=skind)
    q2t = nc.dram_tensor("q2t", [HC, 128, D], BF16, kind=skind)
    q3t = nc.dram_tensor("q3t", [KC, 128, FH], BF16, kind=skind)
    wo_b = nc.dram_tensor("wo_b", [KC, 128, D], BF16, kind=skind)
    q_hbm = nc.dram_tensor("q_hbm", [BPC * NTR, 128, H * 512], BF16, kind=skind)
    k_hbm = nc.dram_tensor("k_hbm", [BPC * NTR, 128, H * 512], BF16, kind=skind)
    v_hbm = nc.dram_tensor("v_hbm", [BPC * NTR, 128, H * 512], BF16, kind=skind)
    on_hbm = nc.dram_tensor("on_hbm", [NATG, 128, KC * TA], BF16, kind=skind)

    TAP_SPECS = {
        "hid0": ([128, HC * TB], BF16), "xnb0": ([128, KC * TB], BF16),
    }
    tap_hbm = {tn: nc.dram_tensor(f"tap_{tn}", list(TAP_SPECS[tn][0]),
                                  TAP_SPECS[tn][1], kind="ExternalOutput")
               for tn in DEV_TAPS}

    i128b_h = nc.inline_tensor(np.eye(128, dtype=ml_dtypes.bfloat16), "c_i128b")
    i128f_h = nc.inline_tensor(np.eye(128, dtype=np.float32), "c_i128f")
    ones_rf_h = nc.inline_tensor(np.ones((1, 128), np.float32), "c_ones_rf")
    ones_rb_h = nc.inline_tensor(np.ones((1, 128), ml_dtypes.bfloat16), "c_ones_rb")
    ones_cf_h = nc.inline_tensor(np.ones((128, 1), np.float32), "c_ones_cf")
    ones_cb_h = nc.inline_tensor(np.ones((128, 1), ml_dtypes.bfloat16), "c_ones_cb")
    m01 = np.tril(np.ones((CH, CH), np.float32)).T
    mask_np = np.tile(np.tile(m01, (1, H)), (2, 1)).astype(ml_dtypes.bfloat16)
    mask_h = nc.inline_tensor(mask_np, "c_mask")
    half2_np = np.zeros((128, 2), np.float32)
    half2_np[63, 0] = 1.0
    half2_np[127, 1] = 1.0
    half2_h = nc.inline_tensor(half2_np, "c_half2")
    sel2_np = np.zeros((2, 256), np.float32)
    sel2_np[0, 0:64] = 1.0
    sel2_np[1, 64:128] = 1.0
    sel2_np[1, 128:256] = 1.0
    sel2_h = nc.inline_tensor(sel2_np, "c_sel2")
    CVALS = [-0.5, 0.5, 1.0, 1e-6, 1e-5, 1e-8, 0.0]
    cval_h = nc.inline_tensor(
        np.repeat(np.array([CVALS], np.float32), 128, axis=0), "c_cvals")

    SA = nc.alloc_sbuf_tensor
    c_i128b = SA("s_i128b", [128, 128], BF16)
    c_i128f = SA("s_i128f", [128, 128], F32)
    c_ones_rf = SA("s_ones_rf", [1, 128], F32)
    c_ones_rb = SA("s_ones_rb", [1, 128], BF16)
    c_ones_cf = SA("s_ones_cf", [128, 1], F32)
    c_ones_cb = SA("s_ones_cb", [128, 1], BF16)
    c_mask = SA("s_mask", [128, H * CH], BF16)
    c_half2 = SA("s_half2", [128, 2], F32)
    c_sel2 = SA("s_sel2", [2, 256], F32)
    c_cval = SA("s_cvals", [128, len(CVALS)], F32)
    nw_col = SA("s_nw", [128, KC], F32)
    wf_sb = SA("s_wf", [128, KC * H], BF16)
    st_scale = SA("st_scale", [128, 32], F32)
    st_bias = SA("st_bias", [128, 32], F32)
    st_rstd2 = SA("st_rstd2", [128, 32], F32)
    p_acc = SA("p_acc", [128, 4], F32)
    p_tmp = SA("p_tmp", [128, 4], F32)
    p_inv = SA("p_inv", [128, 8], F32)
    p_sc = SA("p_sc", [1, 8], F32)
    a_st = SA("a_st", [128, 8], F32)

    for i, cv in enumerate(CVALS):
        nc.const_aps.register(F32, cv, c_cval[:, i:i + 1])

    base = nc.sbuf_base
    limit = nc.sbuf_top
    cur = [base]

    def place(name, shape, dtype):
        nbytes = int(np.prod(shape[1:])) * mybir.dt.size(dtype)
        off = (cur[0] + 31) & ~31
        cur[0] = off + nbytes
        assert cur[0] <= limit, f"SBUF overflow at {name}: {cur[0]} > {limit}"
        return nc.alloc_sbuf_tensor_at(name, list(shape), dtype, offset=off)

    p_w = [place(f"p_w{i}", [128, 1024], F32) for i in range(3)]
    p_g1 = [place(f"p_g1{i}", [128, 1024], BF16) for i in range(2)]
    p_q = [place(f"p_q{i}", [128, 1024], BF16) for i in range(2)]
    prep_end = cur[0]
    a_lstr = [place(f"a_lstr{i}", [128, 16 * H], F32) for i in range(2)]
    a_cfmr = [place(f"a_cfmr{i}", [16, SEQ], BF16) for i in range(2)]

    SLOTW = KC * 512
    r1 = (cur[0] + 31) & ~31
    xn2 = nc.alloc_sbuf_tensor_at("xn2", [128, 6 * SLOTW], BF16, offset=r1)
    a_qn = [nc.alloc_sbuf_tensor_at(f"a_qn{i}", [128, H * 512], BF16,
                                    offset=r1 + (0 + i) * SLOTW * 2) for i in range(2)]
    a_kn = [nc.alloc_sbuf_tensor_at(f"a_kn{i}", [128, H * 512], BF16,
                                    offset=r1 + (2 + i) * SLOTW * 2) for i in range(2)]
    a_vn = [nc.alloc_sbuf_tensor_at(f"a_vn{i}", [128, H * 512], BF16,
                                    offset=r1 + (4 + i) * SLOTW * 2) for i in range(2)]
    cur[0] = r1 + 6 * SLOTW * 2

    r2 = (cur[0] + 31) & ~31
    cur[0] = r2
    a_x = [place(f"a_x{i}", [128, D], F32) for i in range(2)]
    a_xt = [place(f"a_xt{i}", [128, D], BF16) for i in range(2)]
    w_f32 = [place(f"w_f32{i}", [128, D], F32) for i in range(2)]
    w_bf = [place(f"w_bf{i}", [128, D], BF16) for i in range(2)]
    stq = [place(f"stq{i}", [128, 512], BF16) for i in range(2)]
    a_ld = [place(f"a_ld{i}", [16, TA], F32) for i in range(2)]
    a_cum = [place(f"a_cum{i}", [16, TA], F32) for i in range(2)]
    a_lsg = place("a_lsg", [16, 2 * TA], F32)
    a1_end = cur[0]
    cur[0] = r2
    a_kt = [place(f"a_kt{i}", [128, D], BF16) for i in range(2)]
    a_vt = [place(f"a_vt{i}", [128, D], BF16) for i in range(2)]
    a_vd = [place(f"a_vd{i}", [128, D], BF16) for i in range(2)]
    a_S = [place(f"a_S{i}", [128, H * EH], BF16) for i in range(2)]
    a_q1 = [place(f"a_q1_{i}", [128, H * CH], BF16) for i in range(2)]
    a_cq = [place(f"a_cq{i}", [128, H * CH], BF16) for i in range(2)]
    a_sc = [place(f"a_sc{i}", [128, H * CH], BF16) for i in range(2)]
    a_crow = [place(f"a_crow{i}", [1, H * 128], BF16) for i in range(2)]
    a_ict = [place(f"a_ict{i}", [128, H], F32) for i in range(2)]
    a_dd = [place(f"a_dd{i}", [128, H], F32) for i in range(2)]
    a_tot = place("a_tot", [2, 4 * H], F32)
    a_eto = place("a_eto", [128, 4 * H], F32)
    a_sq = place("a_sq", [128, 2 * TA], BF16)
    a_r1 = place("a_r1", [1, 2 * TA], F32)
    a_o = [place(f"a_o{i}", [128, KC * TA], BF16) for i in range(2)]
    a2_end = cur[0]
    cur[0] = max(a1_end, a2_end)
    a_end = cur[0]

    cur[0] = prep_end
    b_xn = place("b_xn", [128, KC * TB], BF16)
    b_on = nc.alloc_sbuf_tensor_at("b_on", [128, 4 * KC * TA], BF16,
                                   offset=(prep_end + 31) & ~31)
    b_hid = place("b_hid", [128, HC * TB], BF16)
    b_q1s = [place(f"b_q1s{i}", [128, D], BF16) for i in range(2)]
    b_q2s = [place(f"b_q2s{i}", [128, D], BF16) for i in range(2)]
    b_q3s = [place(f"b_q3s{i}", [128, FH // 2], BF16) for i in range(2)]
    b_wob = [place("b_wob0", [128, D], BF16)]
    _sig_off = (cur[0] + 31) & ~31
    b_sig = [place(f"b_sig{i}", [128, TB], BF16) for i in range(2)]
    b_s = [nc.alloc_sbuf_tensor_at(f"b_s{i}", [128, TB], BF16,
                                   offset=_sig_off + i * TB * 2) for i in range(2)]
    b_x = [place("b_x0", [128, D], F32)]
    _bxt_off = (cur[0] + 31) & ~31
    b_xt = [place("b_xt0", [128, D], BF16)]
    b_g = [nc.alloc_sbuf_tensor_at(f"b_g{i}", [128, TB], BF16,
                                   offset=_bxt_off + i * TB * 2) for i in range(2)]
    b_xc = [place(f"b_xc{i}", [128, TB], BF16) for i in range(2)]
    b_os = [place(f"b_os{i}", [128, TB], F32) for i in range(2)]
    b_end = cur[0]
    assert max(a_end, b_end) <= limit, f"SBUF: a={a_end} b={b_end} limit={limit}"

    S = Sched(nc)
    PB = [Bank(nc.alloc_psum_tensor(f"pb{i}", [128, 512], F32)) for i in range(8)]

    def view(hd, p0, pn, off, dims):
        rl = hd.shape[-1]
        return bass.AP(hd, p0 * rl + off, [[rl, pn]] + dims)

    CONST = []
    for dst, src in [(c_i128b, i128b_h), (c_i128f, i128f_h), (c_ones_rf, ones_rf_h),
                     (c_ones_rb, ones_rb_h), (c_ones_cf, ones_cf_h),
                     (c_ones_cb, ones_cb_h), (c_mask, mask_h),
                     (c_half2, half2_h), (c_sel2, sel2_h),
                     (c_cval, cval_h), (nw_col, nw_hbm)]:
        CONST.append(S.qs((lambda d, s_: lambda e: e.dma_start(out=d[:, :], in_=s_[:, :]))(dst, src), ch="const"))
    CONST.append(S.qg(lambda e: e.dma_start(out=wf_sb[:, :], in_=wf_l[:, :]), ch="constg"))

    def _mk_units(srcs):
        units = []
        for mi, src, nch, width, dst in srcs:
            for ci in range(nch):
                if width == D:
                    chunks = ((0, 1024), (1024, 1024))
                else:
                    chunks = tuple((o * 1024, 1024) for o in range(5)) + ((5120, 512),)
                for (o, w_) in chunks:
                    units.append((mi, src, dst, ci, o, w_))
        return units

    G_UNITS = _mk_units([(0, w1_l, HC, D, None), (1, w2_l, HC, D, None),
                         (2, w3_l, KC, FH, None)])
    Q_UNITS = []
    for ci in range(HC):
        for (mi, src_, dst_) in ((0, w1_l, q1t), (1, w2_l, q2t)):
            for o in (0, 1024):
                Q_UNITS.append((mi, src_, dst_, ci, o, 1024))
    for ci in range(KC):
        for (o, w_) in tuple((j * 1024, 1024) for j in range(5)) + ((5120, 512),):
            Q_UNITS.append((2, w3_l, q3t, ci, o, w_))
    WO_UNITS = [(fc, o) for fc in range(KC) for o in (0, 1024)]
    PS = {"gi": 0, "qi": 0, "wi": 0, "lc": 0,
          "bfree": [list(CONST) for _ in range(3)],
          "qfree": [None] * 2, "wofree": [None], "accin": [False] * 3,
          "gdone": None, "inv": None, "s1free": [None, None], "qsdone": {}}

    def prep_wo_step():
        if PS["wi"] >= len(WO_UNITS):
            return False
        fc, off = WO_UNITS[PS["wi"]]
        PS["wi"] += 1
        bi = PS["lc"] % 3
        PS["lc"] += 1
        t_in = S.qs(
            (lambda b_, c_, o_: lambda e: e.dma_start(
                out=p_w[b_][:, :], in_=wo_l[c_, :, o_:o_ + 1024]))(bi, fc, off),
            deps=PS["bfree"][bi], ch=f"pw{bi}")
        t_c = S.dv(
            (lambda b_: lambda e: e.tensor_copy(p_g1[0][:, :], p_w[b_][:, :]))(bi),
            deps=[t_in] + PS["wofree"], sig=True)
        PS["bfree"][bi] = [t_c]
        t_out = S.qs(
            (lambda c_, o_: lambda e: e.dma_start(
                out=wo_b[c_, :, o_:o_ + 1024], in_=p_g1[0][:, :]))(fc, off),
            deps=[t_c], ch="pwo")
        PS["wofree"] = [t_out]
        if PS["wi"] == len(WO_UNITS):
            PS["s1free"][0] = [t_out]
        return True

    def prep_gamma_step():
        if PS["gi"] >= len(G_UNITS):
            return
        mi, src, _, ci, off, w_ = G_UNITS[PS["gi"]]
        bi = PS["lc"] % 3
        PS["lc"] += 1
        PS["gi"] += 1
        buf = p_w[bi]
        t_in = S.qs(
            (lambda bb, s_, c_, o_, ww: lambda e: e.dma_start(
                out=bb[:, 0:ww], in_=s_[c_, :, o_:o_ + ww]))(buf, src, ci, off, w_),
            deps=PS["bfree"][bi], ch=f"pw{bi}")
        t_red = S.dv(
            (lambda bb, ww: lambda e: e.tensor_reduce(
                p_tmp[:, 3:4], bb[:, 0:ww], AX.X, ALU.add, apply_absolute_value=True))(buf, w_),
            deps=[t_in], sig=True)
        PS["bfree"][bi] = [t_red]
        if not PS["accin"][mi]:
            PS["accin"][mi] = True
            t_acc = S.dv(
                (lambda m_: lambda e: e.tensor_copy(p_acc[:, m_:m_ + 1], p_tmp[:, 3:4]))(mi),
                deps=[t_red], sig=True)
        else:
            t_acc = S.dv(
                (lambda m_: lambda e: e.tensor_add(
                    p_acc[:, m_:m_ + 1], p_acc[:, m_:m_ + 1], p_tmp[:, 3:4]))(mi),
                deps=[t_red], sig=True)
        PS["gdone"] = t_acc

    def prep_gamma_finalize():
        bank = PB[7]
        toks = []
        for mi in range(3):
            nelem = FH * D
            t_mm = S.pe(
                (lambda m_: lambda e: e.matmul(
                    bank.ap[0:1, 4 * m_:4 * m_ + 1], p_acc[:, m_:m_ + 1],
                    c_ones_cf[:, :], start=True, stop=True))(mi),
                deps=[PS["gdone"]] + bank.wdeps(), sig=True)
            bank.wrote(t_mm)
            t_c = S.dv(
                (lambda m_, n_: lambda e: e.tensor_scalar(
                    p_sc[0:1, m_:m_ + 1], bank.ap[0:1, 4 * m_:4 * m_ + 1],
                    1.0 / n_, 1e-8, ALU.mult, ALU.add))(mi, nelem),
                deps=bank.rdeps(), sig=True)
            bank.read(t_c)
            t_r = S.dv(
                (lambda m_: lambda e: e.reciprocal(
                    p_sc[0:1, 4 + m_:5 + m_], p_sc[0:1, m_:m_ + 1]))(mi),
                deps=[t_c], sig=True)
            t_b = S.pe(
                (lambda m_: lambda e: e.matmul(
                    bank.ap[:, 8 + m_:9 + m_], c_ones_rf[:, :],
                    p_sc[0:1, 4 + m_:5 + m_], start=True, stop=True))(mi),
                deps=[t_r] + bank.wdeps(), sig=True)
            bank.wrote(t_b)
            t_e = S.dv(
                (lambda m_: lambda e: e.tensor_copy(
                    p_inv[:, m_:m_ + 1], bank.ap[:, 8 + m_:9 + m_]))(mi),
                deps=bank.rdeps(), sig=True)
            bank.read(t_e)
            t_n = S.dv(
                (lambda m_: lambda e: e.tensor_scalar(
                    p_inv[:, 4 + m_:5 + m_], p_inv[:, m_:m_ + 1], -1.0, None, ALU.mult))(mi),
                deps=[t_e], sig=True)
            toks.append(t_n)
        PS["inv"] = toks

    def prep_quant_step():
        if PS["qi"] >= len(Q_UNITS):
            return
        if PS["inv"] is None:
            while PS["gi"] < len(G_UNITS):
                prep_gamma_step()
            prep_gamma_finalize()
        mi, src, dst, ci, off, w_ = Q_UNITS[PS["qi"]]
        bi = PS["lc"] % 3
        PS["lc"] += 1
        qbi = PS["qi"] % 2
        PS["qi"] += 1
        buf = p_w[bi]
        qb = p_q[qbi]
        t_in = S.qs(
            (lambda bb, s_, c_, o_, ww: lambda e: e.dma_start(
                out=bb[:, 0:ww], in_=s_[c_, :, o_:o_ + ww]))(buf, src, ci, off, w_),
            deps=PS["bfree"][bi] + PS["inv"], ch=f"pw{bi}")
        gi2 = PS["qi"] % 2
        invc = p_inv[:, mi:mi + 1]
        ninvc = p_inv[:, 4 + mi:5 + mi]
        t_s1 = S.dv(
            (lambda bb, ww, iv, g_: lambda e: e.tensor_scalar(
                g_[:, 0:ww], bb[:, 0:ww], iv, 0.5, ALU.mult, ALU.is_ge))(
                buf, w_, invc, p_g1[gi2]),
            deps=[t_in, PS["s1free"][gi2]], sig=True)
        t_s2 = S.dv(
            (lambda bb, ww, iv, q_: lambda e: e.tensor_scalar(
                q_[:, 0:ww], bb[:, 0:ww], iv, 0.5, ALU.mult, ALU.is_ge))(
                buf, w_, ninvc, qb),
            deps=[t_in, PS["qfree"][qbi]], sig=True)
        PS["bfree"][bi] = [t_s2]
        t_add = S.dv(
            (lambda ww, q_, g_: lambda e: e.tensor_sub(
                q_[:, 0:ww], g_[:, 0:ww], q_[:, 0:ww]))(w_, qb, p_g1[gi2]),
            deps=[t_s1, t_s2], sig=True)
        PS["s1free"][gi2] = t_add
        t_out = S.qs(
            (lambda q_, d_, c_, o_, ww: lambda e: e.dma_start(
                out=d_[c_, :, o_:o_ + ww], in_=q_[:, 0:ww]))(qb, dst, ci, off, w_),
            deps=[t_add], ch=f"pq{qbi}")
        PS["qfree"][qbi] = t_out
        PS["qsdone"].setdefault((mi, ci), []).append(t_out)

    def prep_steps(n, quant):
        if not DEV_PREP:
            return
        for _ in range(n):
            if prep_wo_step():
                continue
            (prep_quant_step if quant else prep_gamma_step)()

    def mm(out_ap, l_ap, r_ap, start, stop, deps=(), sig=False):
        return S.pe(
            (lambda o_, l_, r_, s0, s1: lambda e: e.matmul(o_, l_, r_, start=s0, stop=s1))(
                out_ap, l_ap, r_ap, start, stop),
            deps, sig)

    x_free = [[], []]
    xt_free = [[], []]
    cum_free = [[], []]
    lsg_free = [[]]
    xn_evs = {}
    cfm_tok = {}
    lstr_tok = {}
    slot_readers = [[] for _ in range(6)]
    pb01 = [0]
    xblk = [0]

    def emit_ln_xtile(br, xt, evac_deps):
        slot = SLOT[br][xt // 2]
        xp = xt % 2
        evs = []
        for blk in range(2):
            gtt = xblk[0]
            xblk[0] += 1
            xb = gtt % 2
            t_x = S.qs(
                (lambda b_, bb, r0: lambda e: e.dma_start(
                    out=a_x[b_][:, :], in_=x_hbm[bb, r0:r0 + 128, :]))(
                    xb, br, xt * TA + blk * 128),
                deps=x_free[xb] + CONST, ch=f"x{xb}")
            t_sq = S.ac(
                (lambda b_, xc: lambda e: e.activation(
                    xc[:, :], a_x[b_][:, :], AF.Square, accum_out=a_st[:, 1:2]))(
                    xb, a_xt[blk]),
                deps=[t_x] + xt_free[blk], sig=True)
            t_sm = S.dv(
                (lambda b_, xc: lambda e: e.tensor_scalar(
                    xc[:, :], a_x[b_][:, :], 1.0, 0.0, ALU.mult, ALU.add,
                    accum_out=a_st[:, 0:1]))(xb, a_xt[blk]),
                deps=[t_sq], sig=True)
            t_mu = S.dv(
                lambda e: e.tensor_scalar(a_st[:, 3:4], a_st[:, 0:1], 1.0 / D, None, ALU.mult),
                deps=[t_sm], sig=True)
            t_e2 = S.dv(
                lambda e: e.tensor_scalar(a_st[:, 1:2], a_st[:, 1:2], 1.0 / D, None, ALU.mult),
                deps=[t_sq], sig=True)
            t_nv = S.dv(
                lambda e: e.scalar_tensor_tensor(
                    a_st[:, 2:3], a_st[:, 3:4], a_st[:, 3:4], a_st[:, 1:2],
                    ALU.mult, ALU.subtract),
                deps=[t_mu, t_e2], sig=True)
            t_ln = S.ac(
                lambda e: e.activation(a_st[:, 4:5], a_st[:, 2:3], AF.Ln,
                                       scale=-1.0, bias=1e-5),
                deps=[t_nv], sig=True)
            t_rs = S.ac(
                (lambda c: lambda e: e.activation(
                    st_scale[:, c:c + 1], a_st[:, 4:5], AF.Exp, scale=-0.5))(gtt),
                deps=[t_ln], sig=True)
            t_nb = S.dv(
                (lambda c: lambda e: e.tensor_scalar(
                    st_bias[:, c:c + 1], st_scale[:, c:c + 1], a_st[:, 3:4], -1.0,
                    ALU.mult, ALU.mult))(gtt),
                deps=[t_rs], sig=True)
            t_ap = S.ac(
                (lambda b_, xc, c: lambda e: e.activation(
                    xc[:, :], a_x[b_][:, :], AF.Identity,
                    scale=st_scale[:, c:c + 1], bias=st_bias[:, c:c + 1]))(
                    xb, a_xt[blk], gtt),
                deps=[t_nb, t_sm], sig=True)
            x_free[xb] = [t_ap, t_sm]
            bk7 = PB[7]
            for g in range(4):
                t_last = None
                for j in range(4):
                    fcb = g * 4 + j
                    t_last = mm(bk7.ap[:, j * 128:(j + 1) * 128],
                                a_xt[blk][:, fcb * 128:(fcb + 1) * 128],
                                c_i128b[:, :], True, True,
                                deps=[t_ap] + (bk7.wdeps() if j == 0 else []),
                                sig=(j == 3))
                bk7.wrote(t_last)
                colbase = slot * SLOTW + (g * 4) * 512 + (xt % 2) * 256 + blk * 128
                t_ev = S.ac(
                    (lambda bkap, cb: lambda e: e.copy(
                        view(xn2, 0, 128, cb, [[512, 4], [1, 128]]),
                        bkap[:, 0:512].rearrange("q (a b) -> q a b", a=4)))(
                        bk7.ap, colbase),
                    deps=bk7.rdeps() + evac_deps, sig=True)
                bk7.read(t_ev)
                evs.append(t_ev)
            xt_free[blk] = evs[-4:]
        xn_evs[(br, xt)] = evs
        bk7 = PB[7]
        t_last = None
        for kc in range(KC):
            t_last = mm(bk7.ap[0:16, 0:TA], wf_sb[:, kc * H:(kc + 1) * H],
                        xn2[:, slot * SLOTW + kc * 512 + (xt % 2) * 256:
                            slot * SLOTW + kc * 512 + (xt % 2) * 256 + TA],
                        kc == 0, kc == KC - 1,
                        deps=evs + (bk7.wdeps() if kc == 0 else []) + CONST,
                        sig=(kc == KC - 1))
        bk7.wrote(t_last)
        t_abs = S.ac(
            lambda e: e.activation(a_lsg[:, 0:TA], bk7.ap[0:16, 0:TA], AF.Abs),
            deps=bk7.rdeps() + lsg_free[0], sig=True)
        t_exp = S.ac(lambda e: e.activation(a_lsg[:, TA:2 * TA], a_lsg[:, 0:TA],
                                            AF.Exp, scale=-1.0), deps=[t_abs], sig=True)
        t_ln1 = S.ac(lambda e: e.activation(a_lsg[:, 0:TA], a_lsg[:, TA:2 * TA],
                                            AF.Ln, bias=c_cval[0:16, 2:3]),
                     deps=[t_exp] + CONST, sig=True)
        t_min = S.dv(
            (lambda p_: lambda e: e.tensor_scalar(a_ld[p_][:, :], bk7.ap[0:16, 0:TA],
                                                  0.0, None, ALU.min))(xp),
            deps=bk7.rdeps() + cum_free[xp] + [t_abs], sig=True)
        bk7.read(t_min)
        bk7.read(t_abs)
        t_ld = S.dv(
            (lambda p_: lambda e: e.tensor_sub(a_ld[p_][:, :], a_ld[p_][:, :],
                                               a_lsg[:, 0:TA]))(xp),
            deps=[t_ln1, t_min], sig=True)
        lsg_free[0] = [t_ld]
        t_cums = []
        for c in range(NCH):
            t_cums.append(S.dv(
                (lambda p_, c_: lambda e: e.tensor_tensor_scan(
                    a_cum[p_][:, c_ * CH:(c_ + 1) * CH], a_ld[p_][:, c_ * CH:(c_ + 1) * CH],
                    a_ld[p_][:, c_ * CH:(c_ + 1) * CH], 0.0, ALU.add, ALU.bypass))(xp, c),
                deps=[t_ld] + cum_free[xp], sig=True))
        t_cfm = S.ac(
            (lambda p_, b_, x_: lambda e: e.activation(
                a_cfmr[b_][:, x_ * TA:(x_ + 1) * TA], a_cum[p_][:, :], AF.Exp))(xp, br, xt),
            deps=t_cums, sig=True)
        cfm_tok[(br, xt)] = t_cfm
        t_tl = None
        for tp in range(2):
            t_tl = mm(bk7.ap[:, tp * 16:(tp + 1) * 16],
                      a_cum[xp][:, tp * 128:(tp + 1) * 128],
                      c_i128f[0:16, 0:16], True, True,
                      deps=t_cums + (bk7.wdeps() if tp == 0 else []),
                      sig=(tp == 1))
        bk7.wrote(t_tl)
        t_le = S.ac(
            (lambda bkap, b_, x_: lambda e: e.copy(
                a_lstr[b_][:, x_ * 32:(x_ + 1) * 32], bkap[:, 0:32]))(bk7.ap, br, xt),
            deps=bk7.rdeps(), sig=True)
        bk7.read(t_le)
        lstr_tok[(br, xt)] = t_le
        cum_free[xp] = [t_cfm, t_le]

    wf_free = [list(CONST), list(CONST)]
    wbf_free = [[], []]
    stq_free = [[], []]
    wcnt = [0]
    scnt = [0]
    W_SRC = [wq_l, wk_l, wv_l]
    W_DST = [q_hbm, k_hbm, v_hbm]

    def emit_w_unit(br, wi, fc):
        wj = wcnt[0] % 2
        wb = wcnt[0] % 2
        wcnt[0] += 1
        t_wl = S.qs(
            (lambda j_, s_, f_: lambda e: e.dma_start(
                out=w_f32[j_][:, :], in_=s_[f_, :, :]))(wj, W_SRC[wi], fc),
            deps=wf_free[wj], ch=f"wf{wj}")
        t_wc = S.dv(
            (lambda j_, b_: lambda e: e.tensor_copy(w_bf[b_][:, :], w_f32[j_][:, :]))(wj, wb),
            deps=[t_wl] + wbf_free[wb], sig=True)
        wf_free[wj] = [t_wc]
        t_last = None
        for nt in range(NTR):
            slot = SLOT[br][nt]
            bk = PB[pb01[0] % 6]
            pb01[0] += 1
            ndeps = xn_evs[(br, 2 * nt)] + xn_evs[(br, 2 * nt + 1)]
            for kc in range(KC):
                t_last = mm(bk.ap[:, 0:512],
                            w_bf[wb][:, kc * 128:(kc + 1) * 128],
                            xn2[:, slot * SLOTW + kc * 512:slot * SLOTW + (kc + 1) * 512],
                            kc == 0, kc == KC - 1,
                            deps=([t_wc] + ndeps + (bk.wdeps() if kc == 0 else [])),
                            sig=(kc == KC - 1))
            bk.wrote(t_last)
            slot_readers[slot].append(t_last)
            si = scnt[0] % 2
            scnt[0] += 1
            if wi == 2:
                t_ev = S.ac(
                    (lambda bkap, s_: lambda e: e.copy(stq[s_][:, :], bkap[:, 0:512]))(
                        bk.ap, si),
                    deps=bk.rdeps() + stq_free[si], sig=True)
            else:
                t_ev = S.ac(
                    (lambda bkap, s_: lambda e: e.activation(
                        stq[s_][:, :], bkap[:, 0:512], AF.Silu))(bk.ap, si),
                    deps=bk.rdeps() + stq_free[si], sig=True)
            bk.read(t_ev)
            t_st = S.qa(
                (lambda s_, d_, g_, f_: lambda e: e.dma_start(
                    out=d_[g_, :, f_ * 512:(f_ + 1) * 512], in_=stq[s_][:, :]))(
                    si, W_DST[wi], br * NTR + nt, fc),
                deps=[t_ev], ch=f"st{si}")
            stq_free[si] = [t_st]
        wbf_free[wb] = [t_last]

    for xt in range(8):
        emit_ln_xtile(0, xt, [])
    ucnt = 0
    for wi in range(3):
        for fc in range(KC):
            emit_w_unit(0, wi, fc)
            prep_steps(4, quant=(PS["gi"] >= len(G_UNITS)))
            if DEV_NROW > 1 and 12 <= ucnt < 16:
                emit_ln_xtile(1, ucnt - 12, [])
            ucnt += 1
    if DEV_NROW > 1:
        for xt in range(4, 8):
            slot = SLOT[1][xt // 2]
            emit_ln_xtile(1, xt, list(slot_readers[slot]))
        for wi in range(3):
            for fc in range(KC):
                emit_w_unit(1, wi, fc)
                prep_steps(4, quant=(PS["gi"] >= len(G_UNITS)))

    FBA = S.fence()
    qn_free = [[FBA], [FBA]]
    kn_free = [[FBA], [FBA]]
    vn_free = [[FBA], [FBA]]
    kt_free = [[FBA], [FBA]]
    vt_free = [[FBA], [FBA]]
    vd_free = [[FBA], [FBA]]
    o_free = [[FBA], [FBA]]
    q1_free = [[FBA], [FBA]]
    cq_free = [[FBA], [FBA]]
    crow_free = [[FBA], [FBA]]
    ict_free = [[FBA], [FBA]]
    eto_free_r = [[FBA], [FBA]]
    dd_free = [[FBA], [FBA]]
    sc_free = [[FBA], [FBA]]
    sq_free = [[FBA], [FBA]]
    r1_free = [[FBA]]
    S_write = [[None] * H, [None] * H]
    S_first = [[FBA], [FBA]]
    nt_load = {}
    ttcnt = [0]
    ccnt = [0]
    acnt = [0]
    tile_o = {}
    rowst = [{}, {}]
    R_SC = [[4, 5], [0, 1]]
    R_OB = [6, 3]
    R_UB = [2, 7]

    def a2_loads(br, nt):
        pn = (2 * nt + br) % 2
        g_ = br * NTR + nt

        def ld(dst, src, ch, free):
            return S.qs(
                (lambda d_, s_, gg: lambda e: e.dma_start(
                    out=d_[:, :], in_=s_[gg, :, :]))(dst, src, g_),
                deps=free, ch=ch)
        t_q = ld(a_qn[pn], q_hbm, f"qn{pn}", qn_free[pn])
        t_k = ld(a_kn[pn], k_hbm, f"kn{pn}", kn_free[pn])
        t_v = ld(a_vn[pn], v_hbm, f"vn{pn}", vn_free[pn])
        nt_load[(br, nt)] = (t_q, t_k, t_v)

    def emit_trans(src_buf, tok_dep, dst, dst_free, bank, base):
        evs = []
        for g in range(4):
            t_tl = None
            for j in range(4):
                h = g * 4 + j
                t_tl = mm(bank.ap[:, j * 128:(j + 1) * 128],
                          src_buf[:, h * 512 + base:h * 512 + base + 128],
                          c_i128b[:, :], True, True,
                          deps=[tok_dep] + (bank.wdeps() if j == 0 else []),
                          sig=(j == 3))
            bank.wrote(t_tl)
            t_ev = S.dv(
                (lambda bkap, d_, g_: lambda e: e.tensor_copy(
                    d_[:, g_ * 512:(g_ + 1) * 512], bkap[:, 0:512]))(bank.ap, dst, g),
                deps=bank.rdeps() + dst_free, sig=True)
            bank.read(t_ev)
            evs.append(t_ev)
        return evs

    def emit_pre_a(br, pn, ah, tt):
        st = rowst[br]
        tpg = ttcnt[0] % 2
        ttcnt[0] += 1
        st["tpg"] = tpg
        st["kt_evs"] = emit_trans(a_kn[pn], nt_load[(br, st["nt"])][1], a_kt[tpg],
                                  kt_free[tpg], PB[R_UB[br]], ah * 256 + tt * 128)
        kt_free[tpg] = []

    def emit_pre_b(br, pn, ah, tt):
        st = rowst[br]
        tpg = st["tpg"]
        st["vt_evs"] = emit_trans(a_vn[pn], nt_load[(br, st["nt"])][2], a_vt[tpg],
                                  vt_free[tpg], PB[R_SC[br][0]], ah * 256 + tt * 128)
        vt_free[tpg] = []

    def emit_pre_c(br, pn, ah, tt, tg_row):
        st = rowst[br]
        tpg = st["tpg"]
        atl = tg_row // 2
        lstr_sl = a_lstr[br][:, tg_row * 16:(tg_row + 1) * 16]
        dep_l = [lstr_tok[(br, atl)]]
        t_ic = S.ac(
            (lambda t_, sl: lambda e: e.activation(
                a_ict[t_][:, :], sl, AF.Exp, scale=-1.0))(tpg, lstr_sl),
            deps=dep_l + ict_free[tpg], sig=True)
        ict_free[tpg] = []
        t_crows = [S.qg(
            (lambda t_, b_, tg: lambda e: e.dma_start(
                out=bass.AP(a_crow[t_], 0, [[2048, 1], [128, H], [1, 128]]),
                in_=view(a_cfmr[b_], 0, H, tg * 128, [[1, 128]])))(tpg, br, tg_row),
            deps=[cfm_tok[(br, atl)]] + crow_free[tpg], ch=f"crowg{tpg}")]
        crow_free[tpg] = []
        bko = PB[R_OB[br]]
        bo = br * 32
        t_ta = mm(bko.ap[0:2, 0:16], c_half2[:, :], lstr_sl,
                  True, True, deps=dep_l + bko.wdeps(), sig=True)
        bko.wrote(t_ta)
        t_ee = S.ac(
            (lambda bkap, b_: lambda e: e.activation(
                a_tot[0:2, b_ + 16:b_ + 32], bkap[0:2, 0:16], AF.Exp))(bko.ap, bo),
            deps=bko.rdeps() + eto_free_r[br], sig=True)
        bko.read(t_ee)
        e1 = mm(bko.ap[:, 48:64], c_ones_rf[:, :], a_tot[0:1, bo + 16:bo + 32],
                True, True, deps=[t_ee] + bko.wdeps(), sig=False)
        e2 = mm(bko.ap[:, 64:80], c_sel2[0:2, 128:256], a_tot[0:2, bo + 16:bo + 32],
                True, True, deps=[t_ee], sig=True)
        bko.wrote(e2)
        t_ev = S.ac(
            (lambda bkap, b_: lambda e: e.copy(a_eto[:, b_:b_ + 32], bkap[:, 48:80]))(
                bko.ap, bo),
            deps=bko.rdeps() + eto_free_r[br], sig=True)
        bko.read(t_ev)
        eto_free_r[br] = []
        t_dd1 = S.dv(
            (lambda t_, b_: lambda e: e.tensor_mul(
                a_dd[t_][0:64, :], a_eto[0:64, b_:b_ + 16], a_ict[t_][0:64, :]))(tpg, bo),
            deps=[t_ev, t_ic] + dd_free[tpg], sig=True)
        t_dd2 = S.dv(
            (lambda t_, b_: lambda e: e.tensor_mul(
                a_dd[t_][64:128, :], a_eto[64:128, b_ + 16:b_ + 32],
                a_ict[t_][64:128, :]))(tpg, bo),
            deps=[t_ev, t_ic], sig=True)
        t_vd = S.dv(
            (lambda t_: lambda e: e.tensor_mul(
                view(a_vd[t_], 0, 128, 0, [[128, H], [1, 128]]),
                view(a_vt[t_], 0, 128, 0, [[128, H], [1, 128]]),
                view(a_dd[t_], 0, 128, 0, [[1, H], [0, 128]])))(tpg),
            deps=[t_dd1, t_dd2] + st["vt_evs"] + vd_free[tpg], sig=True)
        vd_free[tpg] = []
        dd_free[tpg] = [t_vd]
        st["pre"] = {"t_ic": t_ic, "t_ev": t_ev, "t_vd": t_vd, "t_crows": t_crows}

    def emit_cpre(br, pn, ah, c):
        st = rowst[br]
        tpg = st["tpg"]
        pre = st["pre"]
        half = c % 2
        pr = ccnt[0] % 2
        st["pr"] = pr
        cq_toks = []
        for rnd in range(2):
            bkc = PB[R_OB[br]] if rnd == 0 else PB[R_UB[br]]
            t_cb = mm(bkc.ap[:, 0:512], c_ones_rb[:, :],
                      bass.AP(a_crow[tpg], rnd * 1024 + half * 64,
                              [[2048, 1], [128, 8], [1, 64]]),
                      True, True,
                      deps=pre["t_crows"] + bkc.wdeps(), sig=True)
            bkc.wrote(t_cb)
            t_ce = S.dv(
                (lambda bkap, pr_, r_: lambda e: e.tensor_copy(
                    a_cq[pr_][:, r_ * 512:(r_ + 1) * 512], bkap[:, 0:512]))(
                    bkc.ap, pr, rnd),
                deps=bkc.rdeps() + cq_free[pr], sig=True)
            bkc.read(t_ce)
            cq_toks.append(t_ce)
        cq_free[pr] = []
        t_q1 = S.dv(
            (lambda pr_, p_, off: lambda e: e.tensor_mul(
                view(a_q1[pr_], 0, 128, 0, [[CH, H], [1, CH]]),
                view(a_qn[p_], 0, 128, off, [[512, H], [1, CH]]),
                view(a_cq[pr_], 0, 128, 0, [[CH, H], [1, CH]])))(pr, pn, ah * 256 + c * CH),
            deps=cq_toks + [nt_load[(br, st["nt"])][0]] + q1_free[pr], sig=True)
        q1_free[pr] = []
        st["t_q1"] = t_q1

    def emit_scores(br, pn, ah, c):
        st = rowst[br]
        tpg = st["tpg"]
        pr = st["pr"]
        half = c % 2
        p0 = 64 * half
        t_q1 = st["t_q1"]
        kbase = ah * 256 + c * CH
        t_kn = nt_load[(br, st["nt"])][1]
        sc_toks = []
        for g in range(2):
            bk = PB[R_SC[br][g]]
            for j in range(8):
                h = g * 8 + j
                mm(bk.ap[p0:p0 + 64, j * 64:(j + 1) * 64],
                   a_kn[pn][:, h * 512 + kbase: h * 512 + kbase + CH],
                   a_q1[pr][:, h * CH:(h + 1) * CH],
                   True, True,
                   deps=([t_q1, t_kn] + (bk.wdeps() if j == 0 else [])),
                   sig=(j == 7))
            bk.wrote(("pe", S.cnt["pe"]))
        for g in range(2):
            bk = PB[R_SC[br][g]]
            t_s1 = S.dv(
                (lambda bkap, q0_, g_, t_, pr_: lambda e: e.tensor_mul(
                    view(a_sc[pr_], q0_, 64, g_ * 512, [[CH, 8], [1, CH]]),
                    bkap[q0_:q0_ + 64, :].rearrange("q (h i) -> q h i", h=8),
                    view(a_ict[t_], q0_, 64, g_ * 8, [[1, 8], [0, CH]])))(
                    bk.ap, p0, g, tpg, pr),
                deps=bk.rdeps() + [st["pre"]["t_ic"]] + sc_free[pr], sig=True)
            bk.read(t_s1)
            t_s2 = S.dv(
                (lambda q0_, g_, pr_: lambda e: e.tensor_mul(
                    a_sc[pr_][q0_:q0_ + 64, g_ * 512:(g_ + 1) * 512],
                    a_sc[pr_][q0_:q0_ + 64, g_ * 512:(g_ + 1) * 512],
                    c_mask[q0_:q0_ + 64, g_ * 512:(g_ + 1) * 512]))(p0, g, pr),
                deps=[t_s1], sig=True)
            sc_toks.append(t_s2)
            ict_free[tpg].append(t_s1)
        sc_free[pr] = []
        st["sc_toks"] = sc_toks

    def emit_omm(br, pn, ah, atl, c):
        st = rowst[br]
        tpg = st["tpg"]
        pr = st["pr"]
        half = c % 2
        p0 = 64 * half
        po = st["po"]
        skip_inter = (atl == 0 and c == 0)
        t_q1 = st["t_q1"]
        sc_toks = st["sc_toks"]
        o_evs = []
        bko = PB[R_OB[br]]
        for g in range(2):
            for j in range(8):
                h = g * 8 + j
                deps_o = ([sc_toks[g], st["vt_evs"][h // 4]] +
                          (bko.wdeps() if j == 0 else []))
                mm(bko.ap[:, j * 64:(j + 1) * 64],
                   view(a_vt[tpg], p0, 64, h * 128, [[1, 128]]),
                   a_sc[pr][p0:p0 + 64, h * 64:(h + 1) * 64],
                   True, skip_inter, deps=deps_o,
                   sig=(skip_inter and j == 7))
                if not skip_inter:
                    mm(bko.ap[:, j * 64:(j + 1) * 64],
                       a_S[br][:, h * 128:(h + 1) * 128],
                       a_q1[pr][:, h * CH:(h + 1) * CH],
                       False, True, deps=[S_write[br][h], t_q1],
                       sig=(j == 7))
            bko.wrote(("pe", S.cnt["pe"]))
            t_oe = S.ac(
                (lambda bkap, g_, c_, p_: lambda e: e.copy(
                    view(a_o[p_], 0, 128, (g_ * 8) * TA + c_ * CH, [[TA, 8], [1, CH]]),
                    bkap[:, 0:512].rearrange("q (h i) -> q h i", h=8)))(bko.ap, g, c, po),
                deps=bko.rdeps() + (o_free[po] if (g == 0 and c == 0) else []), sig=True)
            bko.read(t_oe)
            o_evs.append(t_oe)
        if c == 0:
            o_free[po] = []
        tile_o[(br, atl)].extend(o_evs)
        st["o_evs"] = o_evs
        st["o_pe"] = ("pe", S.cnt["pe"])

    def emit_ustt(br, c):
        st = rowst[br]
        tpg = st["tpg"]
        pr = st["pr"]
        half = c % 2
        p0 = 64 * half
        pre = st["pre"]
        skip_inter = (st["atl"] == 0 and c == 0)
        bku = PB[R_UB[br]]
        for g4 in range(4):
            t_uu = None
            for j in range(4):
                h = g4 * 4 + j
                t_uu = mm(bku.ap[:, j * 128:(j + 1) * 128],
                          view(a_kt[tpg], p0, 64, h * 128, [[1, 128]]),
                          view(a_vd[tpg], p0, 64, h * 128, [[1, 128]]),
                          True, True,
                          deps=([st["kt_evs"][h // 4], pre["t_vd"]] +
                                (bku.wdeps() if j == 0 else [])),
                          sig=(j == 3))
            bku.wrote(t_uu)
            for j in range(4):
                h = g4 * 4 + j
                if skip_inter:
                    t_su = S.dv(
                        (lambda bkap, b_, h_, j_: lambda e: e.tensor_copy(
                            a_S[b_][:, h_ * 128:(h_ + 1) * 128],
                            bkap[:, j_ * 128:(j_ + 1) * 128]))(bku.ap, br, h, j),
                        deps=bku.rdeps() + [st["o_pe"]] + S_first[br], sig=True)
                else:
                    t_su = S.dv(
                        (lambda bkap, b_, h_, j_, pr2: lambda e: e.scalar_tensor_tensor(
                            a_S[b_][:, h_ * 128:(h_ + 1) * 128],
                            a_S[b_][:, h_ * 128:(h_ + 1) * 128],
                            a_eto[:, b_ * 32 + pr2 * 16 + h_:b_ * 32 + pr2 * 16 + h_ + 1],
                            bkap[:, j_ * 128:(j_ + 1) * 128],
                            ALU.mult, ALU.add))(bku.ap, br, h, j, half),
                        deps=bku.rdeps() + [pre["t_ev"], st["o_pe"]], sig=True)
                S_write[br][h] = t_su
                bku.read(t_su)
                eto_free_r[br].append(t_su)
        q1_free[pr] = [("pe", S.cnt["pe"])]
        cq_free[pr] = [st["t_q1"]]
        sc_free[pr] = [("pe", S.cnt["pe"])]
        vd_free[tpg] = [S_write[br][H - 1]]
        kt_free[tpg] = [S_write[br][H - 1]]
        vt_free[tpg] = [S_write[br][H - 1], ("pe", S.cnt["pe"])]
        crow_free[tpg] = [("pe", S.cnt["pe"])]
        ict_free[tpg].append(("dv", S.cnt["dv"]))

    def emit_u1(br, atl, fc0, fc1):
        st = rowst[br]
        po = st["po"]
        bko = PB[R_OB[br]]
        for fc in range(fc0, fc1):
            sqh = fc % 2
            t_sq = S.ac(
                (lambda f_, s_, p_: lambda e: e.activation(
                    a_sq[:, s_ * TA:(s_ + 1) * TA],
                    a_o[p_][:, f_ * TA:(f_ + 1) * TA], AF.Square))(fc, sqh, po),
                deps=tile_o[(br, atl)] + sq_free[sqh], sig=True)
            t_mmq = mm(bko.ap[0:1, 0:TA], c_ones_cb[:, :],
                       a_sq[:, sqh * TA:(sqh + 1) * TA],
                       fc == 0, fc == KC - 1,
                       deps=[t_sq] + (bko.wdeps() if fc == 0 else []),
                       sig=True)
            sq_free[sqh] = [t_mmq]
            t_nw = S.dv(
                (lambda f_, p_: lambda e: e.tensor_scalar(
                    a_o[p_][:, f_ * TA:(f_ + 1) * TA],
                    a_o[p_][:, f_ * TA:(f_ + 1) * TA],
                    nw_col[:, f_:f_ + 1], None, ALU.mult))(fc, po),
                deps=[t_sq], sig=True)
            st["nw_toks"].append(t_nw)
            if fc == KC - 1:
                bko.wrote(t_mmq)
                t_lnr = S.ac(
                    (lambda bkap: lambda e: e.activation(
                        a_r1[0:1, 0:TA], bkap[0:1, 0:TA],
                        AF.Ln, scale=1.0 / D, bias=c_cval[0:1, 3:4]))(bko.ap),
                    deps=bko.rdeps() + r1_free[0], sig=True)
                bko.read(t_lnr)
                t_rr = S.ac(
                    lambda e: e.activation(a_r1[0:1, TA:2 * TA], a_r1[0:1, 0:TA],
                                           AF.Exp, scale=-0.5), deps=[t_lnr], sig=True)
                bk7 = PB[R_UB[br]]
                t_rc = None
                for k4 in range(NTT):
                    t_rc = mm(bk7.ap[:, k4:k4 + 1],
                              a_r1[0:1, TA + k4 * 128:TA + (k4 + 1) * 128],
                              c_ones_rf[0:1, 0:1], True, True,
                              deps=[t_rr] + (bk7.wdeps() if k4 == 0 else []),
                              sig=(k4 == NTT - 1))
                bk7.wrote(t_rc)
                gtok = br * 16 + atl * 2
                t_rce = S.dv(
                    (lambda c_: lambda e: e.tensor_copy(
                        st_rstd2[:, c_:c_ + NTT], bk7.ap[:, 0:NTT]))(gtok),
                    deps=bk7.rdeps(), sig=True)
                bk7.read(t_rce)
                r1_free[0] = [t_rce, t_rr]
                t_sp = S.qs(
                    (lambda p_, g_: lambda e: e.dma_start(
                        out=on_hbm[g_, :, :], in_=a_o[p_][:, :]))(po, br * 8 + atl),
                    deps=st["nw_toks"], ch=f"on{po % 2}")
                o_free[po] = [t_sp]

    def row_stream(br):
        st = rowst[br]
        for nt in range(NTR):
            st["nt"] = nt
            pn = (2 * nt + br) % 2
            a2_loads(br, nt)
            for ah in range(2):
                atl = nt * 2 + ah
                st["atl"] = atl
                st["po"] = acnt[0] % 2
                acnt[0] += 1
                st["nw_toks"] = []
                tile_o[(br, atl)] = []
                for tt in range(NTT):
                    tg_row = atl * 2 + tt
                    emit_pre_a(br, pn, ah, tt)
                    yield
                    emit_pre_b(br, pn, ah, tt)
                    yield
                    emit_pre_c(br, pn, ah, tt, tg_row)
                    yield
                    for half in range(2):
                        c = 2 * tt + half
                        emit_cpre(br, pn, ah, c)
                        ccnt[0] += 1
                        prep_steps(1, quant=True)
                        yield
                        emit_scores(br, pn, ah, c)
                        prep_steps(1, quant=True)
                        yield
                        emit_omm(br, pn, ah, atl, c)
                        prep_steps(1, quant=True)
                        yield
                        emit_ustt(br, c)
                        yield
                emit_u1(br, atl, 0, 8)
                prep_steps(3, quant=True)
                yield
                emit_u1(br, atl, 8, KC)
                prep_steps(3, quant=True)
                yield
            qn_free[pn] = [("dv", S.cnt["dv"])]
            kn_free[pn] = [("pe", S.cnt["pe"])]
            vn_free[pn] = [("pe", S.cnt["pe"])]

    if DEV_A2:
        streams = [row_stream(br) for br in range(DEV_NROW)]
        alive = list(streams)
        while alive:
            for g in list(alive):
                try:
                    next(g)
                except StopIteration:
                    alive.remove(g)


    if DEV_PREP:
        while PS["wi"] < len(WO_UNITS):
            prep_wo_step()
        while PS["gi"] < len(G_UNITS):
            prep_gamma_step()
        if PS["inv"] is None:
            prep_gamma_finalize()

    def qdep(mi, ci):
        if not DEV_PREP:
            return []
        need = 2 if mi < 2 else 6
        while len(PS["qsdone"].get((mi, ci), ())) < need and PS["qi"] < len(Q_UNITS):
            prep_quant_step()
        return PS["qsdone"].get((mi, ci), [])

    FB = S.fence()
    bxn_free = []
    bhid_free = []
    bon_free = [FB]
    bq1_free = [FB, FB]
    bq2_free = [FB, FB]
    bq3_free = [FB, FB]
    bwo_free = [FB, FB]
    bsig_free = [[], []]
    bg_free = [[], []]
    bs_free = [FB, FB]
    bsg_free = [[], []]
    bx_free = [FB]
    bxt_free = [FB]
    bxc_free = [FB, FB]
    bos_free = [[], []]

    for bt in range(DEV_NBT):
        b = bt // (NBT // BPC)
        t0 = (bt % (NBT // BPC)) * TB
        g0 = bt * TB
        gtt0 = g0 // 128

        xn_evs2 = []
        for tt in range(TB // 128):
            bi = 0
            ttg = gtt0 + tt
            t_x = S.qs(
                (lambda b_, bb, r0: lambda e: e.dma_start(
                    out=b_x[b_][:, :], in_=x_hbm[bb, r0:r0 + 128, :]))(bi, b, t0 + tt * 128),
                deps=[bx_free[bi]] + FB, ch="bx")
            t_ap = S.ac(
                (lambda b_, c: lambda e: e.activation(
                    b_xt[b_][:, :], b_x[b_][:, :], AF.Identity,
                    scale=st_scale[:, c:c + 1], bias=st_bias[:, c:c + 1]))(bi, ttg),
                deps=[t_x, bxt_free[bi]] + FB, sig=True)
            bx_free[bi] = t_ap
            grp = []
            for g in range(4):
                bk = PB[6 + g % 2]
                t_tl = None
                for j in range(4):
                    fcb = g * 4 + j
                    t_tl = mm(bk.ap[:, j * 128:(j + 1) * 128],
                              b_xt[bi][:, fcb * 128:(fcb + 1) * 128],
                              c_i128b[:, :], True, True,
                              deps=[t_ap] + (bk.wdeps() if j == 0 else []) + FB,
                              sig=(j == 3))
                bk.wrote(t_tl)
                t_ev = S.ac(
                    (lambda bkap, g_, tt_: lambda e: e.copy(
                        view(b_xn, 0, 128, (g_ * 4) * TB + tt_ * 128, [[TB, 4], [1, 128]]),
                        bkap[:, :].rearrange("p (a b) -> p a b", a=4)))(bk.ap, g, tt),
                    deps=bk.rdeps() + bxn_free, sig=True)
                bk.read(t_ev)
                grp.append(t_ev)
            bxt_free[bi] = grp
            xn_evs2.extend(grp)
        bxn_free = []

        if bt == 0 and "xnb0" in tap_hbm:
            S.qa(lambda e: e.dma_start(out=tap_hbm["xnb0"][:, :], in_=b_xn[:, :]),
                 deps=xn_evs2, ch='tap')

        hid_toks = []
        for hc in range(HC):
            sbi = hc % 2
            if bt == 0:
                prep_steps(3, quant=True)
            t_w1 = S.qs(
                (lambda bb, h_: lambda e: e.dma_start(out=bb[:, :], in_=q1t[h_, :, :]))(
                    b_q1s[sbi], hc),
                deps=[bq1_free[sbi]] + FB + qdep(0, hc), ch=f"bq1_{sbi}")
            t_w2 = S.qs(
                (lambda bb, h_: lambda e: e.dma_start(out=bb[:, :], in_=q2t[h_, :, :]))(
                    b_q2s[sbi], hc),
                deps=[bq2_free[sbi]] + FB + qdep(1, hc), ch=f"bq2_{sbi}")
            pbase1 = 0 if hc % 2 == 0 else 4
            pbase2 = 2 if hc % 2 == 0 else 6
            sig_toks = []
            for half in range(2):
                bk1 = PB[pbase1 + half]
                t_l1 = None
                for kc in range(KC):
                    t_l1 = mm(bk1.ap[:, 0:512],
                              b_q1s[sbi][:, kc * 128:(kc + 1) * 128],
                              b_xn[:, kc * TB + half * 512: kc * TB + (half + 1) * 512],
                              kc == 0, kc == KC - 1,
                              deps=([t_w1] + xn_evs2 + (bk1.wdeps() if kc == 0 else [])),
                              sig=(kc == KC - 1))
                bk1.wrote(t_l1)
                t_sg = S.ac(
                    (lambda bkap, s_, h_: lambda e: e.activation(
                        b_sig[s_][:, h_ * 512:(h_ + 1) * 512], bkap[:, 0:512],
                        AF.Sigmoid))(bk1.ap, sbi, half),
                    deps=bk1.rdeps() + bsig_free[sbi], sig=True)
                bk1.read(t_sg)
                sig_toks.append(t_sg)
            bsig_free[sbi] = []
            t_w1_last = t_l1
            for half in range(2):
                bk2 = PB[pbase2 + half]
                t_l2 = None
                for kc in range(KC):
                    t_l2 = mm(bk2.ap[:, 0:512],
                              b_q2s[sbi][:, kc * 128:(kc + 1) * 128],
                              b_xn[:, kc * TB + half * 512: kc * TB + (half + 1) * 512],
                              kc == 0, kc == KC - 1,
                              deps=([t_w2] + xn_evs2 + (bk2.wdeps() if kc == 0 else [])),
                              sig=(kc == KC - 1))
                bk2.wrote(t_l2)
                t_h = S.dv(
                    (lambda bkap, s_, h_, hc_: lambda e: e.tensor_mul(
                        b_hid[:, hc_ * TB + h_ * 512: hc_ * TB + (h_ + 1) * 512],
                        b_sig[s_][:, h_ * 512:(h_ + 1) * 512],
                        bkap[:, 0:512]))(bk2.ap, sbi, half, hc),
                    deps=bk2.rdeps() + [sig_toks[half]] + bhid_free, sig=True)
                bk2.read(t_h)
                hid_toks.append(t_h)
                bsig_free[sbi].append(t_h)
            bq1_free[sbi] = t_w1_last
            bq2_free[sbi] = t_l2

        xn_done = [("pe", S.cnt["pe"])]
        on_evs = []
        for aq in range(4):
            on_evs.append(S.qs(
                (lambda a_, g_: lambda e: e.dma_start(
                    out=b_on[:, a_ * KC * TA:(a_ + 1) * KC * TA],
                    in_=on_hbm[g_, :, :]))(aq, bt * 4 + aq),
                deps=[bon_free[0]] + xn_done + FB, ch="bon"))

        if bt == 0 and "hid0" in tap_hbm:
            S.qa(lambda e: e.dma_start(out=tap_hbm["hid0"][:, :], in_=b_hid[:, :]),
                 deps=hid_toks, ch='tap')

        last_x3 = None
        for fc in range(KC):
            sbi = fc % 2
            bqa = fc % 2
            bqb = 1 - bqa
            if bt == 0:
                prep_steps(4, quant=True)
            t_w3 = S.qs(
                (lambda bb, f_: lambda e: e.dma_start(
                    out=bb[:, :], in_=q3t[f_, :, 0:FH // 2]))(b_q3s[bqa], fc),
                deps=[bq3_free[bqa]] + FB + qdep(2, fc), ch=f"bq3_{bqa}")
            t_w3b = S.qs(
                (lambda bb, f_: lambda e: e.dma_start(
                    out=bb[:, :], in_=q3t[f_, :, FH // 2:FH]))(b_q3s[bqb], fc),
                deps=[bq3_free[bqb]] + FB + qdep(2, fc), ch=f"bq3_{bqb}")
            t_wo = S.qs(
                (lambda bb, f_: lambda e: e.dma_start(out=bb[:, :], in_=wo_b[f_, :, :]))(
                    b_wob[0], fc),
                deps=[bwo_free[0]] + FB, ch="bwo0")
            t_xc = S.qg(
                (lambda bb, bb_, r0, f_: lambda e: e.dma_start(
                    out=bb[:, :].rearrange("p (tb f) -> p tb f", tb=8),
                    in_=x_hbm[bb_, r0:r0 + TB, f_ * 128:(f_ + 1) * 128].rearrange(
                        "(tb p) f -> p tb f", p=128)))(b_xc[sbi], b, t0, fc),
                deps=[bxc_free[sbi]] + FB, ch=f"bxcg{sbi}")
            s_toks = []
            t_sl = None
            for aq in range(4):
                bks = PB[6 + aq % 2]
                t_ls = None
                for kc in range(KC):
                    t_ls = mm(bks.ap[:, 0:TA],
                              b_wob[0][:, kc * 128:(kc + 1) * 128],
                              b_on[:, aq * KC * TA + kc * TA:aq * KC * TA + (kc + 1) * TA],
                              kc == 0, kc == KC - 1,
                              deps=([t_wo, on_evs[aq]] + (bks.wdeps() if kc == 0 else [])),
                              sig=(kc == KC - 1))
                bks.wrote(t_ls)
                t_sl = t_ls
                t_se = S.ac(
                    (lambda bkap, s_, a_: lambda e: e.copy(
                        b_s[s_][:, a_ * TA:(a_ + 1) * TA], bkap[:, 0:TA]))(bks.ap, sbi, aq),
                    deps=bks.rdeps() + ([bs_free[sbi]] + hid_toks[-1:]
                                        if aq == 0 else []), sig=True)
                bks.read(t_se)
                s_toks.append(t_se)
            bwo_free[0] = t_sl
            pb0 = 0 if fc % 2 == 0 else 2
            g_toks = []
            tok_lo = None
            for half in range(2):
                bk = PB[pb0 + half]
                t_l3 = None
                for hcp in range(HC):
                    hhalf = 0 if hcp < 22 else 1
                    hcl = hcp - 22 * hhalf
                    t_l3 = mm(bk.ap[:, 0:512],
                              b_q3s[bqa if hhalf == 0 else bqb][:, hcl * 128:(hcl + 1) * 128],
                              b_hid[:, hcp * TB + half * 512: hcp * TB + (half + 1) * 512],
                              hcp == 0, hcp == HC - 1,
                              deps=([t_w3, t_w3b] + hid_toks + (bk.wdeps() if hcp == 0 else [])),
                              sig=(hcp == 21 or hcp == HC - 1))
                    if hcp == 21 and half == 1:
                        tok_lo = t_l3
                bk.wrote(t_l3)
                last_x3 = t_l3
                t_g = S.ac(
                    (lambda bkap, s_, h_: lambda e: e.activation(
                        b_g[s_][:, h_ * 512:(h_ + 1) * 512], bkap[:, 0:512],
                        AF.Silu))(bk.ap, sbi, half),
                    deps=bk.rdeps() + bg_free[sbi] + xn_done, sig=True)
                bk.read(t_g)
                g_toks.append(t_g)
            bq3_free[bqa] = tok_lo
            bq3_free[bqb] = last_x3
            t_sgm = S.dv(
                (lambda s_: lambda e: e.tensor_mul(
                    b_g[s_][:, :], b_g[s_][:, :], b_s[s_][:, :]))(sbi),
                deps=g_toks + s_toks + bsg_free[sbi], sig=True)
            bg_free[sbi] = [t_sgm]
            bs_free[sbi] = t_sgm
            bsig_free[sbi].append(t_sgm)
            bxt_free[0] = (bxt_free[0] if isinstance(bxt_free[0], list)
                           else [bxt_free[0]]) + [t_sgm]
            st_toks = []
            for tb in range(8):
                bk = PB[4 + tb % 2]
                t_tp = mm(bk.ap[:, 0:128],
                          b_g[sbi][:, tb * 128:(tb + 1) * 128],
                          c_i128b[:, :], True, True,
                          deps=[t_sgm] + bk.wdeps(), sig=True)
                bk.wrote(t_tp)
                t_st = S.dv(
                    (lambda bkap, s_, tb_, c: lambda e: e.scalar_tensor_tensor(
                        b_os[s_][:, tb_ * 128:(tb_ + 1) * 128],
                        bkap[:, 0:128], st_rstd2[:, c:c + 1],
                        b_xc[s_][:, tb_ * 128:(tb_ + 1) * 128],
                        ALU.mult, ALU.add))(bk.ap, sbi, tb, gtt0 + tb),
                    deps=bk.rdeps() + [t_xc] + bos_free[sbi], sig=True)
                bk.read(t_st)
                st_toks.append(t_st)
            bos_free[sbi] = []
            bsg_free[sbi] = [("pe", S.cnt["pe"])]
            bxc_free[sbi] = st_toks[-1]
            t_od = S.qa(
                (lambda s_, bb_, r0, f_: lambda e: e.dma_start(
                    out=out_hbm[bb_, r0:r0 + TB, f_ * 128:(f_ + 1) * 128].rearrange(
                        "(tb p) f -> p tb f", p=128),
                    in_=b_os[s_][:, :].rearrange("p (tb f) -> p tb f", tb=8)))(
                    sbi, b, t0, fc),
                deps=st_toks, ch=f"out{sbi}")
            bos_free[sbi] = [t_od]
        bhid_free = [last_x3]
        bon_free = [t_sl]
        bxn_free = [t_sl]

    fin = [(ch, S.cnt[ch]) for ch in S.cnt if ch.startswith("out")]
    S.emit("qg", lambda e: e.nop(), deps=fin)

    with nc.Block() as block:
        S.replay(block)

    return nc


def _get_nc():
    if "nc" not in _CACHE:
        _CACHE["nc"] = build_program()
    return _CACHE["nc"]


_CACHE = {}


def kernel(**inputs):
    nc = _get_nc()
    lay = _host_layouts(inputs)
    x = np.asarray(inputs["L0_attn_x"], np.float32)
    in_maps = []
    for k in range(NCORE):
        m = {"x": np.ascontiguousarray(x[BPC * k:BPC * (k + 1)])}
        m.update(lay)
        in_maps.append(m)
    res = run_bass_kernel_spmd(nc, in_maps, core_ids=list(range(NCORE)))
    out = np.concatenate([r["out"] for r in res.results], axis=0)
    return out.astype(np.float32)


# version 17
# speedup vs baseline: 1.0388x; 1.0388x over previous
import numpy as np
import ml_dtypes
import concourse.bass as bass
import concourse.mybir as mybir
from concourse.bass_utils import run_bass_kernel_spmd

F32 = mybir.dt.float32
BF16 = mybir.dt.bfloat16
AF = mybir.ActivationFunctionType
ALU = mybir.AluOpType
AX = mybir.AxisListType

D = 2048
H = 16
EH = 128
CH = 64
SEQ = 2048
BPC = 2
TA = 256
NTT = TA // 128
NCH = TA // CH
APR = SEQ // TA
NATG = BPC * APR
NTR = SEQ // 512
SLOT = [[0, 1, 2, 3], [4, 5, 0, 1]]
TB = 1024
NBT = (BPC * SEQ) // TB
HC = 44
FH = 5632
KC = 16
NCORE = 8

DEV_SCRATCH_OUT = False
DEV_TAPS = ()
import os as _os
DEV_NBT = int(_os.environ.get("KMK_NBT", NBT))
DEV_PREP = int(_os.environ.get("KMK_PREP", 1))
DEV_A2 = int(_os.environ.get("KMK_A2", 1))
DEV_NROW = int(_os.environ.get("KMK_NROW", BPC))


class Sched:
    STREAMS = ("pe", "dv", "ac", "qs", "qg")
    ESEMS = ("pe", "dv", "ac", "qg")

    def __init__(self, nc):
        self.nc = nc
        self.ops = {s: [] for s in self.STREAMS}
        self.sem = {n: nc.alloc_semaphore(f"S_{n}") for n in self.ESEMS}
        self.cnt = {n: 0 for n in self.ESEMS}
        self.lastw = {s: {} for s in self.STREAMS}

    def _dmasem(self, ch):
        if ch not in self.sem:
            self.sem[ch] = self.nc.alloc_semaphore(f"S_{ch}")
            self.cnt[ch] = 0
        return self.sem[ch]

    def _flat(self, deps, out):
        for d in deps:
            if d is None:
                continue
            if isinstance(d, list):
                self._flat(d, out)
            else:
                out.append(d)

    def emit(self, stream, fn, deps=(), sig=None, dma_ch=None):
        flat = []
        self._flat(list(deps), flat)
        need = {}
        for (sname, val) in flat:
            if sname == stream and stream == "pe":
                continue
            if val > need.get(sname, 0):
                need[sname] = val
        waits = []
        for sname, val in need.items():
            cur = self.lastw[stream].get(sname, 0)
            if val > cur:
                waits.append((self.sem[sname], val))
                self.lastw[stream][sname] = val
        tok = None
        inc = None
        if dma_ch is not None:
            self._dmasem(dma_ch)
            self.cnt[dma_ch] += 16
            tok = (dma_ch, self.cnt[dma_ch])
            inc = (self.sem[dma_ch], 16)
        elif sig:
            self.cnt[sig] += 1
            tok = (sig, self.cnt[sig])
            inc = (self.sem[sig], 1)
        self.ops[stream].append((waits, fn, inc))
        return tok

    def pe(self, fn, deps=(), sig=False):
        return self.emit("pe", fn, deps, "pe" if sig else None)

    def dv(self, fn, deps=(), sig=False):
        return self.emit("dv", fn, deps, "dv" if sig else None)

    def ac(self, fn, deps=(), sig=False):
        return self.emit("ac", fn, deps, "ac" if sig else None)

    def qs(self, fn, deps=(), ch="g_qs"):
        return self.emit("qs", fn, deps, dma_ch=ch)

    def qg(self, fn, deps=(), ch="g_qg"):
        return self.emit("qg", fn, deps, dma_ch=ch)

    def qa(self, fn, deps=(), ch="g_qa"):
        return self.emit("ac", fn, deps, dma_ch=ch)

    def fence(self):
        return [(n, self.cnt[n]) for n in self.cnt]

    def replay(self, block):
        S = self

        def mk(stream):
            def run(eng):
                for waits, fn, inc in S.ops[stream]:
                    for (sem, val) in waits:
                        eng.wait_ge(sem, val)
                    ins = fn(eng)
                    if inc is not None:
                        ins.then_inc(*inc)
            return run

        block.tensor(mk("pe"))
        block.vector(mk("dv"))
        block.scalar(mk("ac"))
        block.sync(mk("qs"))
        block.gpsimd(mk("qg"))


class Bank:
    def __init__(self, ap):
        self.ap = ap
        self.ready = None
        self.free = []

    def wdeps(self):
        return list(self.free)

    def wrote(self, tok):
        if tok is not None:
            self.ready = tok
        self.free = []

    def rdeps(self):
        return [self.ready]

    def read(self, tok):
        if tok is not None:
            self.free.append(tok)


def _host_layouts(inp):
    wq = np.asarray(inp["L0_attn_Wq"], np.float32)
    wk = np.asarray(inp["L0_attn_Wk"], np.float32)
    wv = np.asarray(inp["L0_attn_Wv"], np.float32)
    wo = np.asarray(inp["L0_attn_Wo"], np.float32)
    wf = np.asarray(inp["L0_attn_Wf"], np.float32)
    nw = np.asarray(inp["L0_attn_norm_w"], np.float32)
    w1 = np.asarray(inp["L0_ffn_w1"], np.float32)
    w2 = np.asarray(inp["L0_ffn_w2"], np.float32)
    w3 = np.asarray(inp["L0_ffn_w3"], np.float32)

    def fc_major(w):
        return np.ascontiguousarray(
            w.reshape(KC, 128, KC, 128).transpose(2, 1, 0, 3).reshape(KC, 128, D))

    return {
        "wq_l": fc_major(wq),
        "wk_l": fc_major(wk),
        "wv_l": fc_major(wv),
        "wo_l": fc_major(wo),
        "wf_l": np.ascontiguousarray(
            wf.reshape(KC, 128, H).transpose(1, 0, 2).reshape(128, KC * H)),
        "nw_col": np.ascontiguousarray(nw.reshape(KC, 128).T),
        "w1_l": np.ascontiguousarray(
            w1.reshape(HC, 128, KC, 128).transpose(0, 3, 2, 1).reshape(HC, 128, D)),
        "w2_l": np.ascontiguousarray(
            w2.reshape(HC, 128, KC, 128).transpose(0, 3, 2, 1).reshape(HC, 128, D)),
        "w3_l": np.ascontiguousarray(
            w3.reshape(KC, 128, HC, 128).transpose(0, 3, 2, 1).reshape(KC, 128, FH)),
    }


def build_program():
    nc = bass.Bass("TRN2", target_bir_lowering=False, detect_race_conditions=False)

    x_hbm = nc.dram_tensor("x", [BPC, SEQ, D], F32, kind="ExternalInput")
    wq_l = nc.dram_tensor("wq_l", [KC, 128, D], F32, kind="ExternalInput")
    wk_l = nc.dram_tensor("wk_l", [KC, 128, D], F32, kind="ExternalInput")
    wv_l = nc.dram_tensor("wv_l", [KC, 128, D], F32, kind="ExternalInput")
    wo_l = nc.dram_tensor("wo_l", [KC, 128, D], F32, kind="ExternalInput")
    wf_l = nc.dram_tensor("wf_l", [128, KC * H], F32, kind="ExternalInput")
    nw_hbm = nc.dram_tensor("nw_col", [128, KC], F32, kind="ExternalInput")
    w1_l = nc.dram_tensor("w1_l", [HC, 128, D], F32, kind="ExternalInput")
    w2_l = nc.dram_tensor("w2_l", [HC, 128, D], F32, kind="ExternalInput")
    w3_l = nc.dram_tensor("w3_l", [KC, 128, FH], F32, kind="ExternalInput")
    out_hbm = nc.dram_tensor("out", [BPC, SEQ, D], F32, kind="ExternalOutput")

    skind = "ExternalOutput" if DEV_SCRATCH_OUT else "Internal"
    q1t = nc.dram_tensor("q1t", [HC, 128, D], BF16, kind=skind)
    q2t = nc.dram_tensor("q2t", [HC, 128, D], BF16, kind=skind)
    q3t = nc.dram_tensor("q3t", [KC, 128, FH], BF16, kind=skind)
    wo_b = nc.dram_tensor("wo_b", [KC, 128, D], BF16, kind=skind)
    q_hbm = nc.dram_tensor("q_hbm", [BPC * NTR, 128, H * 512], BF16, kind=skind)
    k_hbm = nc.dram_tensor("k_hbm", [BPC * NTR, 128, H * 512], BF16, kind=skind)
    v_hbm = nc.dram_tensor("v_hbm", [BPC * NTR, 128, H * 512], BF16, kind=skind)
    on_hbm = nc.dram_tensor("on_hbm", [NATG, 128, KC * TA], BF16, kind=skind)

    TAP_SPECS = {
        "hid0": ([128, HC * TB], BF16), "xnb0": ([128, KC * TB], BF16),
    }
    tap_hbm = {tn: nc.dram_tensor(f"tap_{tn}", list(TAP_SPECS[tn][0]),
                                  TAP_SPECS[tn][1], kind="ExternalOutput")
               for tn in DEV_TAPS}

    i128b_h = nc.inline_tensor(np.eye(128, dtype=ml_dtypes.bfloat16), "c_i128b")
    i128f_h = nc.inline_tensor(np.eye(128, dtype=np.float32), "c_i128f")
    ones_rf_h = nc.inline_tensor(np.ones((1, 128), np.float32), "c_ones_rf")
    ones_rb_h = nc.inline_tensor(np.ones((1, 128), ml_dtypes.bfloat16), "c_ones_rb")
    ones_cf_h = nc.inline_tensor(np.ones((128, 1), np.float32), "c_ones_cf")
    ones_cb_h = nc.inline_tensor(np.ones((128, 1), ml_dtypes.bfloat16), "c_ones_cb")
    m01 = np.tril(np.ones((CH, CH), np.float32)).T
    mask_np = np.tile(np.tile(m01, (1, H)), (2, 1)).astype(ml_dtypes.bfloat16)
    mask_h = nc.inline_tensor(mask_np, "c_mask")
    half2_np = np.zeros((128, 2), np.float32)
    half2_np[63, 0] = 1.0
    half2_np[127, 1] = 1.0
    half2_h = nc.inline_tensor(half2_np, "c_half2")
    sel2_np = np.zeros((2, 256), np.float32)
    sel2_np[0, 0:64] = 1.0
    sel2_np[1, 64:128] = 1.0
    sel2_np[1, 128:256] = 1.0
    sel2_h = nc.inline_tensor(sel2_np, "c_sel2")
    CVALS = [-0.5, 0.5, 1.0, 1e-6, 1e-5, 1e-8, 0.0]
    cval_h = nc.inline_tensor(
        np.repeat(np.array([CVALS], np.float32), 128, axis=0), "c_cvals")

    SA = nc.alloc_sbuf_tensor
    c_i128b = SA("s_i128b", [128, 128], BF16)
    c_i128f = SA("s_i128f", [128, 128], F32)
    c_ones_rf = SA("s_ones_rf", [1, 128], F32)
    c_ones_rb = SA("s_ones_rb", [1, 128], BF16)
    c_ones_cf = SA("s_ones_cf", [128, 1], F32)
    c_ones_cb = SA("s_ones_cb", [128, 1], BF16)
    c_mask = SA("s_mask", [128, H * CH], BF16)
    c_half2 = SA("s_half2", [128, 2], F32)
    c_sel2 = SA("s_sel2", [2, 256], F32)
    c_cval = SA("s_cvals", [128, len(CVALS)], F32)
    nw_col = SA("s_nw", [128, KC], F32)
    wf_sb = SA("s_wf", [128, KC * H], BF16)
    st_scale = SA("st_scale", [128, 32], F32)
    st_bias = SA("st_bias", [128, 32], F32)
    st_rstd2 = SA("st_rstd2", [128, 32], F32)
    p_acc = SA("p_acc", [128, 4], F32)
    p_tmp = SA("p_tmp", [128, 4], F32)
    p_inv = SA("p_inv", [128, 8], F32)
    p_sc = SA("p_sc", [1, 8], F32)
    a_st = SA("a_st", [128, 8], F32)

    for i, cv in enumerate(CVALS):
        nc.const_aps.register(F32, cv, c_cval[:, i:i + 1])

    base = nc.sbuf_base
    limit = nc.sbuf_top
    cur = [base]

    def place(name, shape, dtype):
        nbytes = int(np.prod(shape[1:])) * mybir.dt.size(dtype)
        off = (cur[0] + 31) & ~31
        cur[0] = off + nbytes
        assert cur[0] <= limit, f"SBUF overflow at {name}: {cur[0]} > {limit}"
        return nc.alloc_sbuf_tensor_at(name, list(shape), dtype, offset=off)

    p_w = [place(f"p_w{i}", [128, 1024], F32) for i in range(3)]
    p_g1 = [place(f"p_g1{i}", [128, 1024], BF16) for i in range(2)]
    p_q = [place(f"p_q{i}", [128, 1024], BF16) for i in range(2)]
    prep_end = cur[0]
    a_lstr = [place(f"a_lstr{i}", [128, 16 * H], F32) for i in range(2)]
    a_cfmr = [place(f"a_cfmr{i}", [16, SEQ], BF16) for i in range(2)]

    SLOTW = KC * 512
    r1 = (cur[0] + 31) & ~31
    xn2 = nc.alloc_sbuf_tensor_at("xn2", [128, 6 * SLOTW], BF16, offset=r1)
    a_qn = [nc.alloc_sbuf_tensor_at(f"a_qn{i}", [128, H * 512], BF16,
                                    offset=r1 + (0 + i) * SLOTW * 2) for i in range(2)]
    a_kn = [nc.alloc_sbuf_tensor_at(f"a_kn{i}", [128, H * 512], BF16,
                                    offset=r1 + (2 + i) * SLOTW * 2) for i in range(2)]
    a_vn = [nc.alloc_sbuf_tensor_at(f"a_vn{i}", [128, H * 512], BF16,
                                    offset=r1 + (4 + i) * SLOTW * 2) for i in range(2)]
    cur[0] = r1 + 6 * SLOTW * 2

    r2 = (cur[0] + 31) & ~31
    cur[0] = r2
    a_x = [place(f"a_x{i}", [128, D], F32) for i in range(2)]
    a_xt = [place(f"a_xt{i}", [128, D], BF16) for i in range(2)]
    w_f32 = [place(f"w_f32{i}", [128, D], F32) for i in range(2)]
    w_bf = [place(f"w_bf{i}", [128, D], BF16) for i in range(2)]
    stq = [place(f"stq{i}", [128, 512], BF16) for i in range(2)]
    a_ld = [place(f"a_ld{i}", [16, TA], F32) for i in range(2)]
    a_cum = [place(f"a_cum{i}", [16, TA], F32) for i in range(2)]
    a_lsg = place("a_lsg", [16, 2 * TA], F32)
    a1_end = cur[0]
    cur[0] = r2
    a_kt = [place(f"a_kt{i}", [128, D], BF16) for i in range(2)]
    a_vt = [place(f"a_vt{i}", [128, D], BF16) for i in range(2)]
    a_vd = [place(f"a_vd{i}", [128, D], BF16) for i in range(2)]
    a_S = [place(f"a_S{i}", [128, H * EH], BF16) for i in range(2)]
    a_q1 = [place(f"a_q1_{i}", [128, H * CH], BF16) for i in range(2)]
    a_cq = [place(f"a_cq{i}", [128, H * CH], BF16) for i in range(2)]
    a_sc = [place(f"a_sc{i}", [128, H * CH], BF16) for i in range(2)]
    a_crow = [place(f"a_crow{i}", [1, H * 128], BF16) for i in range(2)]
    a_ict = [place(f"a_ict{i}", [128, H], F32) for i in range(2)]
    a_dd = [place(f"a_dd{i}", [128, H], F32) for i in range(2)]
    a_tot = place("a_tot", [2, 4 * H], F32)
    a_eto = place("a_eto", [128, 4 * H], F32)
    a_sq = place("a_sq", [128, 2 * TA], BF16)
    a_r1 = place("a_r1", [1, 2 * TA], F32)
    a_o = [place(f"a_o{i}", [128, KC * TA], BF16) for i in range(2)]
    a2_end = cur[0]
    cur[0] = max(a1_end, a2_end)
    a_end = cur[0]

    cur[0] = prep_end
    b_xn = place("b_xn", [128, KC * TB], BF16)
    b_on = nc.alloc_sbuf_tensor_at("b_on", [128, 4 * KC * TA], BF16,
                                   offset=(prep_end + 31) & ~31)
    b_hid = place("b_hid", [128, HC * TB], BF16)
    b_q1s = [place(f"b_q1s{i}", [128, D], BF16) for i in range(2)]
    b_q2s = [place(f"b_q2s{i}", [128, D], BF16) for i in range(2)]
    b_q3s = [place(f"b_q3s{i}", [128, FH // 2], BF16) for i in range(2)]
    b_wob = [place("b_wob0", [128, D], BF16)]
    _sig_off = (cur[0] + 31) & ~31
    b_sig = [place(f"b_sig{i}", [128, TB], BF16) for i in range(2)]
    b_s = [nc.alloc_sbuf_tensor_at(f"b_s{i}", [128, TB], BF16,
                                   offset=_sig_off + i * TB * 2) for i in range(2)]
    b_x = [place("b_x0", [128, D], F32)]
    _bxt_off = (cur[0] + 31) & ~31
    b_xt = [place("b_xt0", [128, D], BF16)]
    b_g = [nc.alloc_sbuf_tensor_at(f"b_g{i}", [128, TB], BF16,
                                   offset=_bxt_off + i * TB * 2) for i in range(2)]
    b_xc = [place(f"b_xc{i}", [128, TB], BF16) for i in range(2)]
    b_os = [place(f"b_os{i}", [128, TB], F32) for i in range(2)]
    b_end = cur[0]
    assert max(a_end, b_end) <= limit, f"SBUF: a={a_end} b={b_end} limit={limit}"

    S = Sched(nc)
    PB = [Bank(nc.alloc_psum_tensor(f"pb{i}", [128, 512], F32)) for i in range(8)]

    def view(hd, p0, pn, off, dims):
        rl = hd.shape[-1]
        return bass.AP(hd, p0 * rl + off, [[rl, pn]] + dims)

    CONST = []
    for dst, src in [(c_i128b, i128b_h), (c_i128f, i128f_h), (c_ones_rf, ones_rf_h),
                     (c_ones_rb, ones_rb_h), (c_ones_cf, ones_cf_h),
                     (c_ones_cb, ones_cb_h), (c_mask, mask_h),
                     (c_half2, half2_h), (c_sel2, sel2_h),
                     (c_cval, cval_h), (nw_col, nw_hbm)]:
        CONST.append(S.qs((lambda d, s_: lambda e: e.dma_start(out=d[:, :], in_=s_[:, :]))(dst, src), ch="const"))
    CONST.append(S.qg(lambda e: e.dma_start(out=wf_sb[:, :], in_=wf_l[:, :]), ch="constg"))

    def _mk_units(srcs):
        units = []
        for mi, src, nch, width, dst in srcs:
            for ci in range(nch):
                if width == D:
                    chunks = ((0, 1024), (1024, 1024))
                else:
                    chunks = tuple((o * 1024, 1024) for o in range(5)) + ((5120, 512),)
                for (o, w_) in chunks:
                    units.append((mi, src, dst, ci, o, w_))
        return units

    G_UNITS = _mk_units([(0, w1_l, HC, D, None), (1, w2_l, HC, D, None),
                         (2, w3_l, KC, FH, None)])
    Q_UNITS = []
    for ci in range(HC):
        for (mi, src_, dst_) in ((0, w1_l, q1t), (1, w2_l, q2t)):
            for o in (0, 1024):
                Q_UNITS.append((mi, src_, dst_, ci, o, 1024))
    for ci in range(KC):
        for (o, w_) in tuple((j * 1024, 1024) for j in range(5)) + ((5120, 512),):
            Q_UNITS.append((2, w3_l, q3t, ci, o, w_))
    WO_UNITS = [(fc, o) for fc in range(KC) for o in (0, 1024)]
    PS = {"gi": 0, "qi": 0, "wi": 0, "lc": 0,
          "bfree": [list(CONST) for _ in range(3)],
          "qfree": [None] * 2, "wofree": [None], "accin": [False] * 3,
          "gdone": None, "inv": None, "s1free": [None, None], "qsdone": {}}

    def prep_wo_step():
        if PS["wi"] >= len(WO_UNITS):
            return False
        fc, off = WO_UNITS[PS["wi"]]
        PS["wi"] += 1
        bi = PS["lc"] % 3
        PS["lc"] += 1
        t_in = S.qs(
            (lambda b_, c_, o_: lambda e: e.dma_start(
                out=p_w[b_][:, :], in_=wo_l[c_, :, o_:o_ + 1024]))(bi, fc, off),
            deps=PS["bfree"][bi], ch=f"pw{bi}")
        t_c = S.dv(
            (lambda b_: lambda e: e.tensor_copy(p_g1[0][:, :], p_w[b_][:, :]))(bi),
            deps=[t_in] + PS["wofree"], sig=True)
        PS["bfree"][bi] = [t_c]
        t_out = S.qs(
            (lambda c_, o_: lambda e: e.dma_start(
                out=wo_b[c_, :, o_:o_ + 1024], in_=p_g1[0][:, :]))(fc, off),
            deps=[t_c], ch="pwo")
        PS["wofree"] = [t_out]
        if PS["wi"] == len(WO_UNITS):
            PS["s1free"][0] = [t_out]
        return True

    def prep_gamma_step():
        if PS["gi"] >= len(G_UNITS):
            return
        mi, src, _, ci, off, w_ = G_UNITS[PS["gi"]]
        bi = PS["lc"] % 3
        PS["lc"] += 1
        PS["gi"] += 1
        buf = p_w[bi]
        t_in = S.qs(
            (lambda bb, s_, c_, o_, ww: lambda e: e.dma_start(
                out=bb[:, 0:ww], in_=s_[c_, :, o_:o_ + ww]))(buf, src, ci, off, w_),
            deps=PS["bfree"][bi], ch=f"pw{bi}")
        t_red = S.dv(
            (lambda bb, ww: lambda e: e.tensor_reduce(
                p_tmp[:, 3:4], bb[:, 0:ww], AX.X, ALU.add, apply_absolute_value=True))(buf, w_),
            deps=[t_in], sig=True)
        PS["bfree"][bi] = [t_red]
        if not PS["accin"][mi]:
            PS["accin"][mi] = True
            t_acc = S.dv(
                (lambda m_: lambda e: e.tensor_copy(p_acc[:, m_:m_ + 1], p_tmp[:, 3:4]))(mi),
                deps=[t_red], sig=True)
        else:
            t_acc = S.dv(
                (lambda m_: lambda e: e.tensor_add(
                    p_acc[:, m_:m_ + 1], p_acc[:, m_:m_ + 1], p_tmp[:, 3:4]))(mi),
                deps=[t_red], sig=True)
        PS["gdone"] = t_acc

    def prep_gamma_finalize():
        bank = PB[7]
        toks = []
        for mi in range(3):
            nelem = FH * D
            t_mm = S.pe(
                (lambda m_: lambda e: e.matmul(
                    bank.ap[0:1, 4 * m_:4 * m_ + 1], p_acc[:, m_:m_ + 1],
                    c_ones_cf[:, :], start=True, stop=True))(mi),
                deps=[PS["gdone"]] + bank.wdeps(), sig=True)
            bank.wrote(t_mm)
            t_c = S.dv(
                (lambda m_, n_: lambda e: e.tensor_scalar(
                    p_sc[0:1, m_:m_ + 1], bank.ap[0:1, 4 * m_:4 * m_ + 1],
                    1.0 / n_, 1e-8, ALU.mult, ALU.add))(mi, nelem),
                deps=bank.rdeps(), sig=True)
            bank.read(t_c)
            t_r = S.dv(
                (lambda m_: lambda e: e.reciprocal(
                    p_sc[0:1, 4 + m_:5 + m_], p_sc[0:1, m_:m_ + 1]))(mi),
                deps=[t_c], sig=True)
            t_b = S.pe(
                (lambda m_: lambda e: e.matmul(
                    bank.ap[:, 8 + m_:9 + m_], c_ones_rf[:, :],
                    p_sc[0:1, 4 + m_:5 + m_], start=True, stop=True))(mi),
                deps=[t_r] + bank.wdeps(), sig=True)
            bank.wrote(t_b)
            t_e = S.dv(
                (lambda m_: lambda e: e.tensor_copy(
                    p_inv[:, m_:m_ + 1], bank.ap[:, 8 + m_:9 + m_]))(mi),
                deps=bank.rdeps(), sig=True)
            bank.read(t_e)
            t_n = S.dv(
                (lambda m_: lambda e: e.tensor_scalar(
                    p_inv[:, 4 + m_:5 + m_], p_inv[:, m_:m_ + 1], -1.0, None, ALU.mult))(mi),
                deps=[t_e], sig=True)
            toks.append(t_n)
        PS["inv"] = toks

    def prep_quant_step():
        if PS["qi"] >= len(Q_UNITS):
            return
        if PS["inv"] is None:
            while PS["gi"] < len(G_UNITS):
                prep_gamma_step()
            prep_gamma_finalize()
        mi, src, dst, ci, off, w_ = Q_UNITS[PS["qi"]]
        bi = PS["lc"] % 3
        PS["lc"] += 1
        qbi = PS["qi"] % 2
        PS["qi"] += 1
        buf = p_w[bi]
        qb = p_q[qbi]
        t_in = S.qs(
            (lambda bb, s_, c_, o_, ww: lambda e: e.dma_start(
                out=bb[:, 0:ww], in_=s_[c_, :, o_:o_ + ww]))(buf, src, ci, off, w_),
            deps=PS["bfree"][bi] + PS["inv"], ch=f"pw{bi}")
        gi2 = PS["qi"] % 2
        invc = p_inv[:, mi:mi + 1]
        ninvc = p_inv[:, 4 + mi:5 + mi]
        t_s1 = S.dv(
            (lambda bb, ww, iv, g_: lambda e: e.tensor_scalar(
                g_[:, 0:ww], bb[:, 0:ww], iv, 0.5, ALU.mult, ALU.is_ge))(
                buf, w_, invc, p_g1[gi2]),
            deps=[t_in, PS["s1free"][gi2]], sig=True)
        t_s2 = S.dv(
            (lambda bb, ww, iv, q_: lambda e: e.tensor_scalar(
                q_[:, 0:ww], bb[:, 0:ww], iv, 0.5, ALU.mult, ALU.is_ge))(
                buf, w_, ninvc, qb),
            deps=[t_in, PS["qfree"][qbi]], sig=True)
        PS["bfree"][bi] = [t_s2]
        t_add = S.dv(
            (lambda ww, q_, g_: lambda e: e.tensor_sub(
                q_[:, 0:ww], g_[:, 0:ww], q_[:, 0:ww]))(w_, qb, p_g1[gi2]),
            deps=[t_s1, t_s2], sig=True)
        PS["s1free"][gi2] = t_add
        t_out = S.qs(
            (lambda q_, d_, c_, o_, ww: lambda e: e.dma_start(
                out=d_[c_, :, o_:o_ + ww], in_=q_[:, 0:ww]))(qb, dst, ci, off, w_),
            deps=[t_add], ch=f"pq{qbi}")
        PS["qfree"][qbi] = t_out
        PS["qsdone"].setdefault((mi, ci), []).append(t_out)

    def prep_steps(n, quant):
        if not DEV_PREP:
            return
        for _ in range(n):
            if prep_wo_step():
                continue
            (prep_quant_step if quant else prep_gamma_step)()

    def mm(out_ap, l_ap, r_ap, start, stop, deps=(), sig=False):
        return S.pe(
            (lambda o_, l_, r_, s0, s1: lambda e: e.matmul(o_, l_, r_, start=s0, stop=s1))(
                out_ap, l_ap, r_ap, start, stop),
            deps, sig)

    x_free = [[], []]
    xt_free = [[], []]
    cum_free = [[], []]
    lsg_free = [[]]
    xn_evs = {}
    cfm_tok = {}
    lstr_tok = {}
    slot_readers = [[] for _ in range(6)]
    pb01 = [0]
    xblk = [0]

    def emit_ln_xtile(br, xt, evac_deps):
        slot = SLOT[br][xt // 2]
        xp = xt % 2
        evs = []
        for blk in range(2):
            gtt = xblk[0]
            xblk[0] += 1
            xb = gtt % 2
            t_x = S.qs(
                (lambda b_, bb, r0: lambda e: e.dma_start(
                    out=a_x[b_][:, :], in_=x_hbm[bb, r0:r0 + 128, :]))(
                    xb, br, xt * TA + blk * 128),
                deps=x_free[xb] + CONST, ch=f"x{xb}")
            t_sq = S.ac(
                (lambda b_, xc: lambda e: e.activation(
                    xc[:, :], a_x[b_][:, :], AF.Square, accum_out=a_st[:, 1:2]))(
                    xb, a_xt[blk]),
                deps=[t_x] + xt_free[blk], sig=True)
            t_sm = S.dv(
                (lambda b_, xc: lambda e: e.tensor_scalar(
                    xc[:, :], a_x[b_][:, :], 1.0, 0.0, ALU.mult, ALU.add,
                    accum_out=a_st[:, 0:1]))(xb, a_xt[blk]),
                deps=[t_sq], sig=True)
            t_mu = S.dv(
                lambda e: e.tensor_scalar(a_st[:, 3:4], a_st[:, 0:1], 1.0 / D, None, ALU.mult),
                deps=[t_sm], sig=True)
            t_e2 = S.dv(
                lambda e: e.tensor_scalar(a_st[:, 1:2], a_st[:, 1:2], 1.0 / D, None, ALU.mult),
                deps=[t_sq], sig=True)
            t_nv = S.dv(
                lambda e: e.scalar_tensor_tensor(
                    a_st[:, 2:3], a_st[:, 3:4], a_st[:, 3:4], a_st[:, 1:2],
                    ALU.mult, ALU.subtract),
                deps=[t_mu, t_e2], sig=True)
            t_ln = S.ac(
                lambda e: e.activation(a_st[:, 4:5], a_st[:, 2:3], AF.Ln,
                                       scale=-1.0, bias=1e-5),
                deps=[t_nv], sig=True)
            t_rs = S.ac(
                (lambda c: lambda e: e.activation(
                    st_scale[:, c:c + 1], a_st[:, 4:5], AF.Exp, scale=-0.5))(gtt),
                deps=[t_ln], sig=True)
            t_nb = S.dv(
                (lambda c: lambda e: e.tensor_scalar(
                    st_bias[:, c:c + 1], st_scale[:, c:c + 1], a_st[:, 3:4], -1.0,
                    ALU.mult, ALU.mult))(gtt),
                deps=[t_rs], sig=True)
            t_ap = S.ac(
                (lambda b_, xc, c: lambda e: e.activation(
                    xc[:, :], a_x[b_][:, :], AF.Identity,
                    scale=st_scale[:, c:c + 1], bias=st_bias[:, c:c + 1]))(
                    xb, a_xt[blk], gtt),
                deps=[t_nb, t_sm], sig=True)
            x_free[xb] = [t_ap, t_sm]
            bk7 = PB[7]
            for g in range(4):
                t_last = None
                for j in range(4):
                    fcb = g * 4 + j
                    t_last = mm(bk7.ap[:, j * 128:(j + 1) * 128],
                                a_xt[blk][:, fcb * 128:(fcb + 1) * 128],
                                c_i128b[:, :], True, True,
                                deps=[t_ap] + (bk7.wdeps() if j == 0 else []),
                                sig=(j == 3))
                bk7.wrote(t_last)
                colbase = slot * SLOTW + (g * 4) * 512 + (xt % 2) * 256 + blk * 128
                t_ev = S.ac(
                    (lambda bkap, cb: lambda e: e.copy(
                        view(xn2, 0, 128, cb, [[512, 4], [1, 128]]),
                        bkap[:, 0:512].rearrange("q (a b) -> q a b", a=4)))(
                        bk7.ap, colbase),
                    deps=bk7.rdeps() + evac_deps, sig=True)
                bk7.read(t_ev)
                evs.append(t_ev)
            xt_free[blk] = evs[-4:]
        xn_evs[(br, xt)] = evs
        bk7 = PB[7]
        t_last = None
        for kc in range(KC):
            t_last = mm(bk7.ap[0:16, 0:TA], wf_sb[:, kc * H:(kc + 1) * H],
                        xn2[:, slot * SLOTW + kc * 512 + (xt % 2) * 256:
                            slot * SLOTW + kc * 512 + (xt % 2) * 256 + TA],
                        kc == 0, kc == KC - 1,
                        deps=evs + (bk7.wdeps() if kc == 0 else []) + CONST,
                        sig=(kc == KC - 1))
        bk7.wrote(t_last)
        t_abs = S.ac(
            lambda e: e.activation(a_lsg[:, 0:TA], bk7.ap[0:16, 0:TA], AF.Abs),
            deps=bk7.rdeps() + lsg_free[0], sig=True)
        t_exp = S.ac(lambda e: e.activation(a_lsg[:, TA:2 * TA], a_lsg[:, 0:TA],
                                            AF.Exp, scale=-1.0), deps=[t_abs], sig=True)
        t_ln1 = S.ac(lambda e: e.activation(a_lsg[:, 0:TA], a_lsg[:, TA:2 * TA],
                                            AF.Ln, bias=c_cval[0:16, 2:3]),
                     deps=[t_exp] + CONST, sig=True)
        t_min = S.dv(
            (lambda p_: lambda e: e.tensor_scalar(a_ld[p_][:, :], bk7.ap[0:16, 0:TA],
                                                  0.0, None, ALU.min))(xp),
            deps=bk7.rdeps() + cum_free[xp] + [t_abs], sig=True)
        bk7.read(t_min)
        bk7.read(t_abs)
        t_ld = S.dv(
            (lambda p_: lambda e: e.tensor_sub(a_ld[p_][:, :], a_ld[p_][:, :],
                                               a_lsg[:, 0:TA]))(xp),
            deps=[t_ln1, t_min], sig=True)
        lsg_free[0] = [t_ld]
        t_cums = []
        for c in range(NCH):
            t_cums.append(S.dv(
                (lambda p_, c_: lambda e: e.tensor_tensor_scan(
                    a_cum[p_][:, c_ * CH:(c_ + 1) * CH], a_ld[p_][:, c_ * CH:(c_ + 1) * CH],
                    a_ld[p_][:, c_ * CH:(c_ + 1) * CH], 0.0, ALU.add, ALU.bypass))(xp, c),
                deps=[t_ld] + cum_free[xp], sig=True))
        t_cfm = S.ac(
            (lambda p_, b_, x_: lambda e: e.activation(
                a_cfmr[b_][:, x_ * TA:(x_ + 1) * TA], a_cum[p_][:, :], AF.Exp))(xp, br, xt),
            deps=t_cums, sig=True)
        cfm_tok[(br, xt)] = t_cfm
        t_tl = None
        for tp in range(2):
            t_tl = mm(bk7.ap[:, tp * 16:(tp + 1) * 16],
                      a_cum[xp][:, tp * 128:(tp + 1) * 128],
                      c_i128f[0:16, 0:16], True, True,
                      deps=t_cums + (bk7.wdeps() if tp == 0 else []),
                      sig=(tp == 1))
        bk7.wrote(t_tl)
        t_le = S.ac(
            (lambda bkap, b_, x_: lambda e: e.copy(
                a_lstr[b_][:, x_ * 32:(x_ + 1) * 32], bkap[:, 0:32]))(bk7.ap, br, xt),
            deps=bk7.rdeps(), sig=True)
        bk7.read(t_le)
        lstr_tok[(br, xt)] = t_le
        cum_free[xp] = [t_cfm, t_le]

    wf_free = [list(CONST), list(CONST)]
    wbf_free = [[], []]
    stq_free = [[], []]
    wcnt = [0]
    scnt = [0]
    W_SRC = [wq_l, wk_l, wv_l]
    W_DST = [q_hbm, k_hbm, v_hbm]

    def emit_w_unit(br, wi, fc):
        wj = wcnt[0] % 2
        wb = wcnt[0] % 2
        wcnt[0] += 1
        t_wl = S.qs(
            (lambda j_, s_, f_: lambda e: e.dma_start(
                out=w_f32[j_][:, :], in_=s_[f_, :, :]))(wj, W_SRC[wi], fc),
            deps=wf_free[wj], ch=f"wf{wj}")
        t_wc = S.dv(
            (lambda j_, b_: lambda e: e.tensor_copy(w_bf[b_][:, :], w_f32[j_][:, :]))(wj, wb),
            deps=[t_wl] + wbf_free[wb], sig=True)
        wf_free[wj] = [t_wc]
        t_last = None
        for nt in range(NTR):
            slot = SLOT[br][nt]
            bk = PB[pb01[0] % 6]
            pb01[0] += 1
            ndeps = xn_evs[(br, 2 * nt)] + xn_evs[(br, 2 * nt + 1)]
            for kc in range(KC):
                t_last = mm(bk.ap[:, 0:512],
                            w_bf[wb][:, kc * 128:(kc + 1) * 128],
                            xn2[:, slot * SLOTW + kc * 512:slot * SLOTW + (kc + 1) * 512],
                            kc == 0, kc == KC - 1,
                            deps=([t_wc] + ndeps + (bk.wdeps() if kc == 0 else [])),
                            sig=(kc == KC - 1))
            bk.wrote(t_last)
            slot_readers[slot].append(t_last)
            si = scnt[0] % 2
            scnt[0] += 1
            if wi == 2:
                t_ev = S.ac(
                    (lambda bkap, s_: lambda e: e.copy(stq[s_][:, :], bkap[:, 0:512]))(
                        bk.ap, si),
                    deps=bk.rdeps() + stq_free[si], sig=True)
            else:
                t_ev = S.ac(
                    (lambda bkap, s_: lambda e: e.activation(
                        stq[s_][:, :], bkap[:, 0:512], AF.Silu))(bk.ap, si),
                    deps=bk.rdeps() + stq_free[si], sig=True)
            bk.read(t_ev)
            t_st = S.qa(
                (lambda s_, d_, g_, f_: lambda e: e.dma_start(
                    out=d_[g_, :, f_ * 512:(f_ + 1) * 512], in_=stq[s_][:, :]))(
                    si, W_DST[wi], br * NTR + nt, fc),
                deps=[t_ev], ch=f"st{si}")
            stq_free[si] = [t_st]
        wbf_free[wb] = [t_last]

    for xt in range(8):
        emit_ln_xtile(0, xt, [])
    ucnt = 0
    for wi in range(3):
        for fc in range(KC):
            emit_w_unit(0, wi, fc)
            prep_steps(4, quant=(PS["gi"] >= len(G_UNITS)))
            if DEV_NROW > 1 and 12 <= ucnt < 16:
                emit_ln_xtile(1, ucnt - 12, [])
            ucnt += 1
    if DEV_NROW > 1:
        for xt in range(4, 8):
            slot = SLOT[1][xt // 2]
            emit_ln_xtile(1, xt, list(slot_readers[slot]))
        for wi in range(3):
            for fc in range(KC):
                emit_w_unit(1, wi, fc)
                prep_steps(4, quant=(PS["gi"] >= len(G_UNITS)))

    FBA = S.fence()
    qn_free = [[FBA], [FBA]]
    kn_free = [[FBA], [FBA]]
    vn_free = [[FBA], [FBA]]
    kt_free = [[FBA], [FBA]]
    vt_free = [[FBA], [FBA]]
    vd_free = [[FBA], [FBA]]
    o_free = [[FBA], [FBA]]
    q1_free = [[FBA], [FBA]]
    cq_free = [[FBA], [FBA]]
    crow_free = [[FBA], [FBA]]
    ict_free = [[FBA], [FBA]]
    eto_free_r = [[FBA], [FBA]]
    dd_free = [[FBA], [FBA]]
    sc_free = [[FBA], [FBA]]
    sq_free = [[FBA], [FBA]]
    r1_free = [[FBA]]
    S_write = [[None] * H, [None] * H]
    S_first = [[FBA], [FBA]]
    nt_load = {}
    ttcnt = [0]
    ccnt = [0]
    acnt = [0]
    tile_o = {}
    rowst = [{}, {}]
    R_SC = [[4, 5], [0, 1]]
    R_OB = [6, 3]
    R_UB = [2, 7]

    def a2_loads(br, nt):
        pn = (2 * nt + br) % 2
        g_ = br * NTR + nt

        def ld(dst, src, ch, free):
            return S.qs(
                (lambda d_, s_, gg: lambda e: e.dma_start(
                    out=d_[:, :], in_=s_[gg, :, :]))(dst, src, g_),
                deps=free, ch=ch)
        t_q = ld(a_qn[pn], q_hbm, f"qn{pn}", qn_free[pn])
        t_k = ld(a_kn[pn], k_hbm, f"kn{pn}", kn_free[pn])
        t_v = ld(a_vn[pn], v_hbm, f"vn{pn}", vn_free[pn])
        nt_load[(br, nt)] = (t_q, t_k, t_v)

    def emit_trans(src_buf, tok_dep, dst, dst_free, bank, base):
        evs = []
        for g in range(4):
            t_tl = None
            for j in range(4):
                h = g * 4 + j
                t_tl = mm(bank.ap[:, j * 128:(j + 1) * 128],
                          src_buf[:, h * 512 + base:h * 512 + base + 128],
                          c_i128b[:, :], True, True,
                          deps=[tok_dep] + (bank.wdeps() if j == 0 else []),
                          sig=(j == 3))
            bank.wrote(t_tl)
            t_ev = S.dv(
                (lambda bkap, d_, g_: lambda e: e.tensor_copy(
                    d_[:, g_ * 512:(g_ + 1) * 512], bkap[:, 0:512]))(bank.ap, dst, g),
                deps=bank.rdeps() + dst_free, sig=True)
            bank.read(t_ev)
            evs.append(t_ev)
        return evs

    def emit_pre_a(br, pn, ah, tt):
        st = rowst[br]
        tpg = ttcnt[0] % 2
        ttcnt[0] += 1
        st["tpg"] = tpg
        st["kt_evs"] = emit_trans(a_kn[pn], nt_load[(br, st["nt"])][1], a_kt[tpg],
                                  kt_free[tpg], PB[R_UB[br]], ah * 256 + tt * 128)
        kt_free[tpg] = []

    def emit_pre_b(br, pn, ah, tt):
        st = rowst[br]
        tpg = st["tpg"]
        st["vt_evs"] = emit_trans(a_vn[pn], nt_load[(br, st["nt"])][2], a_vt[tpg],
                                  vt_free[tpg], PB[R_SC[br][0]], ah * 256 + tt * 128)
        vt_free[tpg] = []

    def emit_pre_c(br, pn, ah, tt, tg_row):
        st = rowst[br]
        tpg = st["tpg"]
        atl = tg_row // 2
        lstr_sl = a_lstr[br][:, tg_row * 16:(tg_row + 1) * 16]
        dep_l = [lstr_tok[(br, atl)]]
        t_ic = S.ac(
            (lambda t_, sl: lambda e: e.activation(
                a_ict[t_][:, :], sl, AF.Exp, scale=-1.0))(tpg, lstr_sl),
            deps=dep_l + ict_free[tpg], sig=True)
        ict_free[tpg] = []
        t_crows = [S.qg(
            (lambda t_, b_, tg: lambda e: e.dma_start(
                out=bass.AP(a_crow[t_], 0, [[2048, 1], [128, H], [1, 128]]),
                in_=view(a_cfmr[b_], 0, H, tg * 128, [[1, 128]])))(tpg, br, tg_row),
            deps=[cfm_tok[(br, atl)]] + crow_free[tpg], ch=f"crowg{tpg}")]
        crow_free[tpg] = []
        bko = PB[R_OB[br]]
        bo = br * 32
        t_ta = mm(bko.ap[0:2, 0:16], c_half2[:, :], lstr_sl,
                  True, True, deps=dep_l + bko.wdeps(), sig=True)
        bko.wrote(t_ta)
        t_ee = S.ac(
            (lambda bkap, b_: lambda e: e.activation(
                a_tot[0:2, b_ + 16:b_ + 32], bkap[0:2, 0:16], AF.Exp))(bko.ap, bo),
            deps=bko.rdeps() + eto_free_r[br], sig=True)
        bko.read(t_ee)
        e1 = mm(bko.ap[:, 48:64], c_ones_rf[:, :], a_tot[0:1, bo + 16:bo + 32],
                True, True, deps=[t_ee] + bko.wdeps(), sig=False)
        e2 = mm(bko.ap[:, 64:80], c_sel2[0:2, 128:256], a_tot[0:2, bo + 16:bo + 32],
                True, True, deps=[t_ee], sig=True)
        bko.wrote(e2)
        t_ev = S.ac(
            (lambda bkap, b_: lambda e: e.copy(a_eto[:, b_:b_ + 32], bkap[:, 48:80]))(
                bko.ap, bo),
            deps=bko.rdeps() + eto_free_r[br], sig=True)
        bko.read(t_ev)
        eto_free_r[br] = []
        t_dd1 = S.dv(
            (lambda t_, b_: lambda e: e.tensor_mul(
                a_dd[t_][0:64, :], a_eto[0:64, b_:b_ + 16], a_ict[t_][0:64, :]))(tpg, bo),
            deps=[t_ev, t_ic] + dd_free[tpg], sig=True)
        t_dd2 = S.dv(
            (lambda t_, b_: lambda e: e.tensor_mul(
                a_dd[t_][64:128, :], a_eto[64:128, b_ + 16:b_ + 32],
                a_ict[t_][64:128, :]))(tpg, bo),
            deps=[t_ev, t_ic], sig=True)
        t_vd = S.dv(
            (lambda t_: lambda e: e.tensor_mul(
                view(a_vd[t_], 0, 128, 0, [[128, H], [1, 128]]),
                view(a_vt[t_], 0, 128, 0, [[128, H], [1, 128]]),
                view(a_dd[t_], 0, 128, 0, [[1, H], [0, 128]])))(tpg),
            deps=[t_dd1, t_dd2] + st["vt_evs"] + vd_free[tpg], sig=True)
        vd_free[tpg] = []
        dd_free[tpg] = [t_vd]
        st["pre"] = {"t_ic": t_ic, "t_ev": t_ev, "t_vd": t_vd, "t_crows": t_crows}

    def emit_cpre(br, pn, ah, c):
        st = rowst[br]
        tpg = st["tpg"]
        pre = st["pre"]
        half = c % 2
        pr = ccnt[0] % 2
        st["pr"] = pr
        cq_toks = []
        for rnd in range(2):
            bkc = PB[R_OB[br]] if rnd == 0 else PB[R_UB[br]]
            t_cb = mm(bkc.ap[:, 0:512], c_ones_rb[:, :],
                      bass.AP(a_crow[tpg], rnd * 1024 + half * 64,
                              [[2048, 1], [128, 8], [1, 64]]),
                      True, True,
                      deps=pre["t_crows"] + bkc.wdeps(), sig=True)
            bkc.wrote(t_cb)
            t_ce = S.ac(
                (lambda bkap, pr_, r_: lambda e: e.copy(
                    a_cq[pr_][:, r_ * 512:(r_ + 1) * 512], bkap[:, 0:512]))(
                    bkc.ap, pr, rnd),
                deps=bkc.rdeps() + cq_free[pr], sig=True)
            bkc.read(t_ce)
            cq_toks.append(t_ce)
        cq_free[pr] = []
        t_q1 = S.dv(
            (lambda pr_, p_, off: lambda e: e.tensor_mul(
                view(a_q1[pr_], 0, 128, 0, [[CH, H], [1, CH]]),
                view(a_qn[p_], 0, 128, off, [[512, H], [1, CH]]),
                view(a_cq[pr_], 0, 128, 0, [[CH, H], [1, CH]])))(pr, pn, ah * 256 + c * CH),
            deps=cq_toks + [nt_load[(br, st["nt"])][0]] + q1_free[pr], sig=True)
        q1_free[pr] = []
        st["t_q1"] = t_q1

    def emit_scores(br, pn, ah, c):
        st = rowst[br]
        tpg = st["tpg"]
        pr = st["pr"]
        half = c % 2
        p0 = 64 * half
        t_q1 = st["t_q1"]
        kbase = ah * 256 + c * CH
        t_kn = nt_load[(br, st["nt"])][1]
        sc_toks = []
        for g in range(2):
            bk = PB[R_SC[br][g]]
            for j in range(8):
                h = g * 8 + j
                mm(bk.ap[p0:p0 + 64, j * 64:(j + 1) * 64],
                   a_kn[pn][:, h * 512 + kbase: h * 512 + kbase + CH],
                   a_q1[pr][:, h * CH:(h + 1) * CH],
                   True, True,
                   deps=([t_q1, t_kn] + (bk.wdeps() if j == 0 else [])),
                   sig=(j == 7))
            bk.wrote(("pe", S.cnt["pe"]))
        for g in range(2):
            bk = PB[R_SC[br][g]]
            t_s1 = S.dv(
                (lambda bkap, q0_, g_, t_, pr_: lambda e: e.tensor_mul(
                    view(a_sc[pr_], q0_, 64, g_ * 512, [[CH, 8], [1, CH]]),
                    bkap[q0_:q0_ + 64, :].rearrange("q (h i) -> q h i", h=8),
                    view(a_ict[t_], q0_, 64, g_ * 8, [[1, 8], [0, CH]])))(
                    bk.ap, p0, g, tpg, pr),
                deps=bk.rdeps() + [st["pre"]["t_ic"]] + sc_free[pr], sig=True)
            bk.read(t_s1)
            t_s2 = S.dv(
                (lambda q0_, g_, pr_: lambda e: e.tensor_mul(
                    a_sc[pr_][q0_:q0_ + 64, g_ * 512:(g_ + 1) * 512],
                    a_sc[pr_][q0_:q0_ + 64, g_ * 512:(g_ + 1) * 512],
                    c_mask[q0_:q0_ + 64, g_ * 512:(g_ + 1) * 512]))(p0, g, pr),
                deps=[t_s1], sig=True)
            sc_toks.append(t_s2)
            ict_free[tpg].append(t_s1)
        sc_free[pr] = []
        st["sc_toks"] = sc_toks

    def emit_omm(br, pn, ah, atl, c):
        st = rowst[br]
        tpg = st["tpg"]
        pr = st["pr"]
        half = c % 2
        p0 = 64 * half
        po = st["po"]
        skip_inter = (atl == 0 and c == 0)
        t_q1 = st["t_q1"]
        sc_toks = st["sc_toks"]
        o_evs = []
        bko = PB[R_OB[br]]
        for g in range(2):
            for j in range(8):
                h = g * 8 + j
                deps_o = ([sc_toks[g], st["vt_evs"][h // 4]] +
                          (bko.wdeps() if j == 0 else []))
                mm(bko.ap[:, j * 64:(j + 1) * 64],
                   view(a_vt[tpg], p0, 64, h * 128, [[1, 128]]),
                   a_sc[pr][p0:p0 + 64, h * 64:(h + 1) * 64],
                   True, skip_inter, deps=deps_o,
                   sig=(skip_inter and j == 7))
                if not skip_inter:
                    mm(bko.ap[:, j * 64:(j + 1) * 64],
                       a_S[br][:, h * 128:(h + 1) * 128],
                       a_q1[pr][:, h * CH:(h + 1) * CH],
                       False, True, deps=[S_write[br][h], t_q1],
                       sig=(j == 7))
            bko.wrote(("pe", S.cnt["pe"]))
            t_oe = S.ac(
                (lambda bkap, g_, c_, p_: lambda e: e.copy(
                    view(a_o[p_], 0, 128, (g_ * 8) * TA + c_ * CH, [[TA, 8], [1, CH]]),
                    bkap[:, 0:512].rearrange("q (h i) -> q h i", h=8)))(bko.ap, g, c, po),
                deps=bko.rdeps() + (o_free[po] if (g == 0 and c == 0) else []), sig=True)
            bko.read(t_oe)
            o_evs.append(t_oe)
        if c == 0:
            o_free[po] = []
        tile_o[(br, atl)].extend(o_evs)
        st["o_evs"] = o_evs
        st["o_pe"] = ("pe", S.cnt["pe"])

    def emit_ustt(br, c):
        st = rowst[br]
        tpg = st["tpg"]
        pr = st["pr"]
        half = c % 2
        p0 = 64 * half
        pre = st["pre"]
        skip_inter = (st["atl"] == 0 and c == 0)
        bku = PB[R_UB[br]]
        for g4 in range(4):
            t_uu = None
            for j in range(4):
                h = g4 * 4 + j
                t_uu = mm(bku.ap[:, j * 128:(j + 1) * 128],
                          view(a_kt[tpg], p0, 64, h * 128, [[1, 128]]),
                          view(a_vd[tpg], p0, 64, h * 128, [[1, 128]]),
                          True, True,
                          deps=([st["kt_evs"][h // 4], pre["t_vd"]] +
                                (bku.wdeps() if j == 0 else [])),
                          sig=(j == 3))
            bku.wrote(t_uu)
            for j in range(4):
                h = g4 * 4 + j
                if skip_inter:
                    t_su = S.dv(
                        (lambda bkap, b_, h_, j_: lambda e: e.tensor_copy(
                            a_S[b_][:, h_ * 128:(h_ + 1) * 128],
                            bkap[:, j_ * 128:(j_ + 1) * 128]))(bku.ap, br, h, j),
                        deps=bku.rdeps() + [st["o_pe"]] + S_first[br], sig=True)
                else:
                    t_su = S.dv(
                        (lambda bkap, b_, h_, j_, pr2: lambda e: e.scalar_tensor_tensor(
                            a_S[b_][:, h_ * 128:(h_ + 1) * 128],
                            a_S[b_][:, h_ * 128:(h_ + 1) * 128],
                            a_eto[:, b_ * 32 + pr2 * 16 + h_:b_ * 32 + pr2 * 16 + h_ + 1],
                            bkap[:, j_ * 128:(j_ + 1) * 128],
                            ALU.mult, ALU.add))(bku.ap, br, h, j, half),
                        deps=bku.rdeps() + [pre["t_ev"], st["o_pe"]], sig=True)
                S_write[br][h] = t_su
                bku.read(t_su)
                eto_free_r[br].append(t_su)
        q1_free[pr] = [("pe", S.cnt["pe"])]
        cq_free[pr] = [st["t_q1"]]
        sc_free[pr] = [("pe", S.cnt["pe"])]
        vd_free[tpg] = [S_write[br][H - 1]]
        kt_free[tpg] = [S_write[br][H - 1]]
        vt_free[tpg] = [S_write[br][H - 1], ("pe", S.cnt["pe"])]
        crow_free[tpg] = [("pe", S.cnt["pe"])]
        ict_free[tpg].append(("dv", S.cnt["dv"]))

    def emit_u1(br, atl, fc0, fc1):
        st = rowst[br]
        po = st["po"]
        bko = PB[R_OB[br]]
        for fc in range(fc0, fc1):
            sqh = fc % 2
            t_sq = S.ac(
                (lambda f_, s_, p_: lambda e: e.activation(
                    a_sq[:, s_ * TA:(s_ + 1) * TA],
                    a_o[p_][:, f_ * TA:(f_ + 1) * TA], AF.Square))(fc, sqh, po),
                deps=tile_o[(br, atl)] + sq_free[sqh], sig=True)
            t_mmq = mm(bko.ap[0:1, 0:TA], c_ones_cb[:, :],
                       a_sq[:, sqh * TA:(sqh + 1) * TA],
                       fc == 0, fc == KC - 1,
                       deps=[t_sq] + (bko.wdeps() if fc == 0 else []),
                       sig=True)
            sq_free[sqh] = [t_mmq]
            t_nw = S.dv(
                (lambda f_, p_: lambda e: e.tensor_scalar(
                    a_o[p_][:, f_ * TA:(f_ + 1) * TA],
                    a_o[p_][:, f_ * TA:(f_ + 1) * TA],
                    nw_col[:, f_:f_ + 1], None, ALU.mult))(fc, po),
                deps=[t_sq], sig=True)
            st["nw_toks"].append(t_nw)
            if fc == KC - 1:
                bko.wrote(t_mmq)
                t_lnr = S.ac(
                    (lambda bkap: lambda e: e.activation(
                        a_r1[0:1, 0:TA], bkap[0:1, 0:TA],
                        AF.Ln, scale=1.0 / D, bias=c_cval[0:1, 3:4]))(bko.ap),
                    deps=bko.rdeps() + r1_free[0], sig=True)
                bko.read(t_lnr)
                t_rr = S.ac(
                    lambda e: e.activation(a_r1[0:1, TA:2 * TA], a_r1[0:1, 0:TA],
                                           AF.Exp, scale=-0.5), deps=[t_lnr], sig=True)
                bk7 = PB[R_UB[br]]
                t_rc = None
                for k4 in range(NTT):
                    t_rc = mm(bk7.ap[:, k4:k4 + 1],
                              a_r1[0:1, TA + k4 * 128:TA + (k4 + 1) * 128],
                              c_ones_rf[0:1, 0:1], True, True,
                              deps=[t_rr] + (bk7.wdeps() if k4 == 0 else []),
                              sig=(k4 == NTT - 1))
                bk7.wrote(t_rc)
                gtok = br * 16 + atl * 2
                t_rce = S.dv(
                    (lambda c_: lambda e: e.tensor_copy(
                        st_rstd2[:, c_:c_ + NTT], bk7.ap[:, 0:NTT]))(gtok),
                    deps=bk7.rdeps(), sig=True)
                bk7.read(t_rce)
                r1_free[0] = [t_rce, t_rr]
                t_sp = S.qs(
                    (lambda p_, g_: lambda e: e.dma_start(
                        out=on_hbm[g_, :, :], in_=a_o[p_][:, :]))(po, br * 8 + atl),
                    deps=st["nw_toks"], ch=f"on{po % 2}")
                o_free[po] = [t_sp]

    def row_stream(br):
        st = rowst[br]
        for nt in range(NTR):
            st["nt"] = nt
            pn = (2 * nt + br) % 2
            a2_loads(br, nt)
            for ah in range(2):
                atl = nt * 2 + ah
                st["atl"] = atl
                st["po"] = acnt[0] % 2
                acnt[0] += 1
                st["nw_toks"] = []
                tile_o[(br, atl)] = []
                for tt in range(NTT):
                    tg_row = atl * 2 + tt
                    emit_pre_a(br, pn, ah, tt)
                    yield
                    emit_pre_b(br, pn, ah, tt)
                    yield
                    emit_pre_c(br, pn, ah, tt, tg_row)
                    yield
                    for half in range(2):
                        c = 2 * tt + half
                        emit_cpre(br, pn, ah, c)
                        ccnt[0] += 1
                        prep_steps(1, quant=True)
                        yield
                        emit_scores(br, pn, ah, c)
                        prep_steps(1, quant=True)
                        yield
                        emit_omm(br, pn, ah, atl, c)
                        prep_steps(1, quant=True)
                        yield
                        emit_ustt(br, c)
                        yield
                emit_u1(br, atl, 0, 8)
                prep_steps(3, quant=True)
                yield
                emit_u1(br, atl, 8, KC)
                prep_steps(3, quant=True)
                yield
            qn_free[pn] = [("dv", S.cnt["dv"])]
            kn_free[pn] = [("pe", S.cnt["pe"])]
            vn_free[pn] = [("pe", S.cnt["pe"])]

    if DEV_A2:
        streams = [row_stream(br) for br in range(DEV_NROW)]
        alive = list(streams)
        while alive:
            for g in list(alive):
                try:
                    next(g)
                except StopIteration:
                    alive.remove(g)


    if DEV_PREP:
        while PS["wi"] < len(WO_UNITS):
            prep_wo_step()
        while PS["gi"] < len(G_UNITS):
            prep_gamma_step()
        if PS["inv"] is None:
            prep_gamma_finalize()

    def qdep(mi, ci):
        if not DEV_PREP:
            return []
        need = 2 if mi < 2 else 6
        while len(PS["qsdone"].get((mi, ci), ())) < need and PS["qi"] < len(Q_UNITS):
            prep_quant_step()
        return PS["qsdone"].get((mi, ci), [])

    FB = S.fence()
    bxn_free = []
    bhid_free = []
    bon_free = [FB]
    bq1_free = [FB, FB]
    bq2_free = [FB, FB]
    bq3_free = [FB, FB]
    bwo_free = [FB, FB]
    bsig_free = [[], []]
    bg_free = [[], []]
    bs_free = [FB, FB]
    bsg_free = [[], []]
    bx_free = [FB]
    bxt_free = [FB]
    bxc_free = [FB, FB]
    bos_free = [[], []]

    for bt in range(DEV_NBT):
        b = bt // (NBT // BPC)
        t0 = (bt % (NBT // BPC)) * TB
        g0 = bt * TB
        gtt0 = g0 // 128

        xn_evs2 = []
        for tt in range(TB // 128):
            bi = 0
            ttg = gtt0 + tt
            t_x = S.qs(
                (lambda b_, bb, r0: lambda e: e.dma_start(
                    out=b_x[b_][:, :], in_=x_hbm[bb, r0:r0 + 128, :]))(bi, b, t0 + tt * 128),
                deps=[bx_free[bi]] + FB, ch="bx")
            t_ap = S.ac(
                (lambda b_, c: lambda e: e.activation(
                    b_xt[b_][:, :], b_x[b_][:, :], AF.Identity,
                    scale=st_scale[:, c:c + 1], bias=st_bias[:, c:c + 1]))(bi, ttg),
                deps=[t_x, bxt_free[bi]] + FB, sig=True)
            bx_free[bi] = t_ap
            grp = []
            for g in range(4):
                bk = PB[6 + g % 2]
                t_tl = None
                for j in range(4):
                    fcb = g * 4 + j
                    t_tl = mm(bk.ap[:, j * 128:(j + 1) * 128],
                              b_xt[bi][:, fcb * 128:(fcb + 1) * 128],
                              c_i128b[:, :], True, True,
                              deps=[t_ap] + (bk.wdeps() if j == 0 else []) + FB,
                              sig=(j == 3))
                bk.wrote(t_tl)
                t_ev = S.ac(
                    (lambda bkap, g_, tt_: lambda e: e.copy(
                        view(b_xn, 0, 128, (g_ * 4) * TB + tt_ * 128, [[TB, 4], [1, 128]]),
                        bkap[:, :].rearrange("p (a b) -> p a b", a=4)))(bk.ap, g, tt),
                    deps=bk.rdeps() + bxn_free, sig=True)
                bk.read(t_ev)
                grp.append(t_ev)
            bxt_free[bi] = grp
            xn_evs2.extend(grp)
        bxn_free = []

        if bt == 0 and "xnb0" in tap_hbm:
            S.qa(lambda e: e.dma_start(out=tap_hbm["xnb0"][:, :], in_=b_xn[:, :]),
                 deps=xn_evs2, ch='tap')

        hid_toks = []
        for hc in range(HC):
            sbi = hc % 2
            if bt == 0:
                prep_steps(3, quant=True)
            t_w1 = S.qs(
                (lambda bb, h_: lambda e: e.dma_start(out=bb[:, :], in_=q1t[h_, :, :]))(
                    b_q1s[sbi], hc),
                deps=[bq1_free[sbi]] + FB + qdep(0, hc), ch=f"bq1_{sbi}")
            t_w2 = S.qs(
                (lambda bb, h_: lambda e: e.dma_start(out=bb[:, :], in_=q2t[h_, :, :]))(
                    b_q2s[sbi], hc),
                deps=[bq2_free[sbi]] + FB + qdep(1, hc), ch=f"bq2_{sbi}")
            pbase1 = 0 if hc % 2 == 0 else 4
            pbase2 = 2 if hc % 2 == 0 else 6
            sig_toks = []
            for half in range(2):
                bk1 = PB[pbase1 + half]
                t_l1 = None
                for kc in range(KC):
                    nds = [xn_evs2[tt_ * 4 + kc // 4] for tt_ in range(half * 4, half * 4 + 4)]
                    t_l1 = mm(bk1.ap[:, 0:512],
                              b_q1s[sbi][:, kc * 128:(kc + 1) * 128],
                              b_xn[:, kc * TB + half * 512: kc * TB + (half + 1) * 512],
                              kc == 0, kc == KC - 1,
                              deps=([t_w1] + nds + (bk1.wdeps() if kc == 0 else [])),
                              sig=(kc == KC - 1))
                bk1.wrote(t_l1)
                t_sg = S.ac(
                    (lambda bkap, s_, h_: lambda e: e.activation(
                        b_sig[s_][:, h_ * 512:(h_ + 1) * 512], bkap[:, 0:512],
                        AF.Sigmoid))(bk1.ap, sbi, half),
                    deps=bk1.rdeps() + bsig_free[sbi], sig=True)
                bk1.read(t_sg)
                sig_toks.append(t_sg)
            bsig_free[sbi] = []
            t_w1_last = t_l1
            for half in range(2):
                bk2 = PB[pbase2 + half]
                t_l2 = None
                for kc in range(KC):
                    nds = [xn_evs2[tt_ * 4 + kc // 4] for tt_ in range(half * 4, half * 4 + 4)]
                    t_l2 = mm(bk2.ap[:, 0:512],
                              b_q2s[sbi][:, kc * 128:(kc + 1) * 128],
                              b_xn[:, kc * TB + half * 512: kc * TB + (half + 1) * 512],
                              kc == 0, kc == KC - 1,
                              deps=([t_w2] + nds + (bk2.wdeps() if kc == 0 else [])),
                              sig=(kc == KC - 1))
                bk2.wrote(t_l2)
                t_h = S.dv(
                    (lambda bkap, s_, h_, hc_: lambda e: e.tensor_mul(
                        b_hid[:, hc_ * TB + h_ * 512: hc_ * TB + (h_ + 1) * 512],
                        b_sig[s_][:, h_ * 512:(h_ + 1) * 512],
                        bkap[:, 0:512]))(bk2.ap, sbi, half, hc),
                    deps=bk2.rdeps() + [sig_toks[half]] + bhid_free, sig=True)
                bk2.read(t_h)
                hid_toks.append(t_h)
                bsig_free[sbi].append(t_h)
            bq1_free[sbi] = t_w1_last
            bq2_free[sbi] = t_l2

        xn_done = [("pe", S.cnt["pe"])]
        on_evs = []
        for aq in range(4):
            on_evs.append(S.qs(
                (lambda a_, g_: lambda e: e.dma_start(
                    out=b_on[:, a_ * KC * TA:(a_ + 1) * KC * TA],
                    in_=on_hbm[g_, :, :]))(aq, bt * 4 + aq),
                deps=[bon_free[0]] + xn_done + FB, ch="bon"))

        if bt == 0 and "hid0" in tap_hbm:
            S.qa(lambda e: e.dma_start(out=tap_hbm["hid0"][:, :], in_=b_hid[:, :]),
                 deps=hid_toks, ch='tap')

        last_x3 = None
        for fc in range(KC):
            sbi = fc % 2
            bqa = fc % 2
            bqb = 1 - bqa
            if bt == 0:
                prep_steps(4, quant=True)
            t_w3 = S.qs(
                (lambda bb, f_: lambda e: e.dma_start(
                    out=bb[:, :], in_=q3t[f_, :, 0:FH // 2]))(b_q3s[bqa], fc),
                deps=[bq3_free[bqa]] + FB + qdep(2, fc), ch=f"bq3_{bqa}")
            t_w3b = S.qs(
                (lambda bb, f_: lambda e: e.dma_start(
                    out=bb[:, :], in_=q3t[f_, :, FH // 2:FH]))(b_q3s[bqb], fc),
                deps=[bq3_free[bqb]] + FB + qdep(2, fc), ch=f"bq3_{bqb}")
            t_wo = S.qs(
                (lambda bb, f_: lambda e: e.dma_start(out=bb[:, :], in_=wo_b[f_, :, :]))(
                    b_wob[0], fc),
                deps=[bwo_free[0]] + FB, ch="bwo0")
            t_xc = S.qg(
                (lambda bb, bb_, r0, f_: lambda e: e.dma_start(
                    out=bb[:, :].rearrange("p (tb f) -> p tb f", tb=8),
                    in_=x_hbm[bb_, r0:r0 + TB, f_ * 128:(f_ + 1) * 128].rearrange(
                        "(tb p) f -> p tb f", p=128)))(b_xc[sbi], b, t0, fc),
                deps=[bxc_free[sbi]] + FB, ch=f"bxcg{sbi}")
            s_toks = []
            t_sl = None
            for aq in range(4):
                bks = PB[6 + aq % 2]
                t_ls = None
                for kc in range(KC):
                    t_ls = mm(bks.ap[:, 0:TA],
                              b_wob[0][:, kc * 128:(kc + 1) * 128],
                              b_on[:, aq * KC * TA + kc * TA:aq * KC * TA + (kc + 1) * TA],
                              kc == 0, kc == KC - 1,
                              deps=([t_wo, on_evs[aq]] + (bks.wdeps() if kc == 0 else [])),
                              sig=(kc == KC - 1))
                bks.wrote(t_ls)
                t_sl = t_ls
                t_se = S.ac(
                    (lambda bkap, s_, a_: lambda e: e.copy(
                        b_s[s_][:, a_ * TA:(a_ + 1) * TA], bkap[:, 0:TA]))(bks.ap, sbi, aq),
                    deps=bks.rdeps() + ([bs_free[sbi]] + hid_toks[-1:]
                                        if aq == 0 else []), sig=True)
                bks.read(t_se)
                s_toks.append(t_se)
            bwo_free[0] = t_sl
            pb0 = 0 if fc % 2 == 0 else 2
            g_toks = []
            tok_lo = None
            for half in range(2):
                bk = PB[pb0 + half]
                t_l3 = None
                for hcp in range(HC):
                    hhalf = 0 if hcp < 22 else 1
                    hcl = hcp - 22 * hhalf
                    t_l3 = mm(bk.ap[:, 0:512],
                              b_q3s[bqa if hhalf == 0 else bqb][:, hcl * 128:(hcl + 1) * 128],
                              b_hid[:, hcp * TB + half * 512: hcp * TB + (half + 1) * 512],
                              hcp == 0, hcp == HC - 1,
                              deps=([t_w3, t_w3b] + hid_toks + (bk.wdeps() if hcp == 0 else [])),
                              sig=(hcp == 21 or hcp == HC - 1))
                    if hcp == 21 and half == 1:
                        tok_lo = t_l3
                bk.wrote(t_l3)
                last_x3 = t_l3
                t_g = S.ac(
                    (lambda bkap, s_, h_: lambda e: e.activation(
                        b_g[s_][:, h_ * 512:(h_ + 1) * 512], bkap[:, 0:512],
                        AF.Silu))(bk.ap, sbi, half),
                    deps=bk.rdeps() + bg_free[sbi] + xn_done, sig=True)
                bk.read(t_g)
                g_toks.append(t_g)
            bq3_free[bqa] = tok_lo
            bq3_free[bqb] = last_x3
            t_sgm = S.dv(
                (lambda s_: lambda e: e.tensor_mul(
                    b_g[s_][:, :], b_g[s_][:, :], b_s[s_][:, :]))(sbi),
                deps=g_toks + s_toks + bsg_free[sbi], sig=True)
            bg_free[sbi] = [t_sgm]
            bs_free[sbi] = t_sgm
            bsig_free[sbi].append(t_sgm)
            bxt_free[0] = (bxt_free[0] if isinstance(bxt_free[0], list)
                           else [bxt_free[0]]) + [t_sgm]
            st_toks = []
            for tb in range(8):
                bk = PB[4 + tb % 2]
                t_tp = mm(bk.ap[:, 0:128],
                          b_g[sbi][:, tb * 128:(tb + 1) * 128],
                          c_i128b[:, :], True, True,
                          deps=[t_sgm] + bk.wdeps(), sig=True)
                bk.wrote(t_tp)
                t_st = S.dv(
                    (lambda bkap, s_, tb_, c: lambda e: e.scalar_tensor_tensor(
                        b_os[s_][:, tb_ * 128:(tb_ + 1) * 128],
                        bkap[:, 0:128], st_rstd2[:, c:c + 1],
                        b_xc[s_][:, tb_ * 128:(tb_ + 1) * 128],
                        ALU.mult, ALU.add))(bk.ap, sbi, tb, gtt0 + tb),
                    deps=bk.rdeps() + [t_xc] + bos_free[sbi], sig=True)
                bk.read(t_st)
                st_toks.append(t_st)
            bos_free[sbi] = []
            bsg_free[sbi] = [("pe", S.cnt["pe"])]
            bxc_free[sbi] = st_toks[-1]
            t_od = S.qa(
                (lambda s_, bb_, r0, f_: lambda e: e.dma_start(
                    out=out_hbm[bb_, r0:r0 + TB, f_ * 128:(f_ + 1) * 128].rearrange(
                        "(tb p) f -> p tb f", p=128),
                    in_=b_os[s_][:, :].rearrange("p (tb f) -> p tb f", tb=8)))(
                    sbi, b, t0, fc),
                deps=st_toks, ch=f"out{sbi}")
            bos_free[sbi] = [t_od]
        bhid_free = [last_x3]
        bon_free = [t_sl]
        bxn_free = [t_sl]

    fin = [(ch, S.cnt[ch]) for ch in S.cnt if ch.startswith("out")]
    S.emit("qg", lambda e: e.nop(), deps=fin)

    with nc.Block() as block:
        S.replay(block)

    return nc


def _get_nc():
    if "nc" not in _CACHE:
        _CACHE["nc"] = build_program()
    return _CACHE["nc"]


_CACHE = {}


def kernel(**inputs):
    nc = _get_nc()
    lay = _host_layouts(inputs)
    x = np.asarray(inputs["L0_attn_x"], np.float32)
    in_maps = []
    for k in range(NCORE):
        m = {"x": np.ascontiguousarray(x[BPC * k:BPC * (k + 1)])}
        m.update(lay)
        in_maps.append(m)
    res = run_bass_kernel_spmd(nc, in_maps, core_ids=list(range(NCORE)))
    out = np.concatenate([r["out"] for r in res.results], axis=0)
    return out.astype(np.float32)


# version 18
# speedup vs baseline: 1.0510x; 1.0118x over previous
import numpy as np
import ml_dtypes
import concourse.bass as bass
import concourse.mybir as mybir
from concourse.bass_utils import run_bass_kernel_spmd

F32 = mybir.dt.float32
BF16 = mybir.dt.bfloat16
AF = mybir.ActivationFunctionType
ALU = mybir.AluOpType
AX = mybir.AxisListType

D = 2048
H = 16
EH = 128
CH = 64
SEQ = 2048
BPC = 2
TA = 256
NTT = TA // 128
NCH = TA // CH
APR = SEQ // TA
NATG = BPC * APR
NTR = SEQ // 512
SLOT = [[0, 1, 2, 3], [4, 5, 0, 1]]
TB = 1024
NBT = (BPC * SEQ) // TB
HC = 44
FH = 5632
KC = 16
NCORE = 8

DEV_SCRATCH_OUT = False
DEV_TAPS = ()
import os as _os
DEV_NBT = int(_os.environ.get("KMK_NBT", NBT))
DEV_PREP = int(_os.environ.get("KMK_PREP", 1))
DEV_A2 = int(_os.environ.get("KMK_A2", 1))
DEV_NROW = int(_os.environ.get("KMK_NROW", BPC))


class Sched:
    STREAMS = ("pe", "dv", "ac", "qs", "qg")
    ESEMS = ("pe", "dv", "ac", "qg")

    def __init__(self, nc):
        self.nc = nc
        self.ops = {s: [] for s in self.STREAMS}
        self.sem = {n: nc.alloc_semaphore(f"S_{n}") for n in self.ESEMS}
        self.cnt = {n: 0 for n in self.ESEMS}
        self.lastw = {s: {} for s in self.STREAMS}

    def _dmasem(self, ch):
        if ch not in self.sem:
            self.sem[ch] = self.nc.alloc_semaphore(f"S_{ch}")
            self.cnt[ch] = 0
        return self.sem[ch]

    def _flat(self, deps, out):
        for d in deps:
            if d is None:
                continue
            if isinstance(d, list):
                self._flat(d, out)
            else:
                out.append(d)

    def emit(self, stream, fn, deps=(), sig=None, dma_ch=None):
        flat = []
        self._flat(list(deps), flat)
        need = {}
        for (sname, val) in flat:
            if sname == stream and stream == "pe":
                continue
            if val > need.get(sname, 0):
                need[sname] = val
        waits = []
        for sname, val in need.items():
            cur = self.lastw[stream].get(sname, 0)
            if val > cur:
                waits.append((self.sem[sname], val))
                self.lastw[stream][sname] = val
        tok = None
        inc = None
        if dma_ch is not None:
            self._dmasem(dma_ch)
            self.cnt[dma_ch] += 16
            tok = (dma_ch, self.cnt[dma_ch])
            inc = (self.sem[dma_ch], 16)
        elif sig:
            self.cnt[sig] += 1
            tok = (sig, self.cnt[sig])
            inc = (self.sem[sig], 1)
        self.ops[stream].append((waits, fn, inc))
        return tok

    def pe(self, fn, deps=(), sig=False):
        return self.emit("pe", fn, deps, "pe" if sig else None)

    def dv(self, fn, deps=(), sig=False):
        return self.emit("dv", fn, deps, "dv" if sig else None)

    def ac(self, fn, deps=(), sig=False):
        return self.emit("ac", fn, deps, "ac" if sig else None)

    def qs(self, fn, deps=(), ch="g_qs"):
        return self.emit("qs", fn, deps, dma_ch=ch)

    def qg(self, fn, deps=(), ch="g_qg"):
        return self.emit("qg", fn, deps, dma_ch=ch)

    def qa(self, fn, deps=(), ch="g_qa"):
        return self.emit("ac", fn, deps, dma_ch=ch)

    def fence(self):
        return [(n, self.cnt[n]) for n in self.cnt]

    def replay(self, block):
        S = self

        def mk(stream):
            def run(eng):
                for waits, fn, inc in S.ops[stream]:
                    for (sem, val) in waits:
                        eng.wait_ge(sem, val)
                    ins = fn(eng)
                    if inc is not None:
                        ins.then_inc(*inc)
            return run

        block.tensor(mk("pe"))
        block.vector(mk("dv"))
        block.scalar(mk("ac"))
        block.sync(mk("qs"))
        block.gpsimd(mk("qg"))


class Bank:
    def __init__(self, ap):
        self.ap = ap
        self.ready = None
        self.free = []

    def wdeps(self):
        return list(self.free)

    def wrote(self, tok):
        if tok is not None:
            self.ready = tok
        self.free = []

    def rdeps(self):
        return [self.ready]

    def read(self, tok):
        if tok is not None:
            self.free.append(tok)


def _host_layouts(inp):
    wq = np.asarray(inp["L0_attn_Wq"], np.float32)
    wk = np.asarray(inp["L0_attn_Wk"], np.float32)
    wv = np.asarray(inp["L0_attn_Wv"], np.float32)
    wo = np.asarray(inp["L0_attn_Wo"], np.float32)
    wf = np.asarray(inp["L0_attn_Wf"], np.float32)
    nw = np.asarray(inp["L0_attn_norm_w"], np.float32)
    w1 = np.asarray(inp["L0_ffn_w1"], np.float32)
    w2 = np.asarray(inp["L0_ffn_w2"], np.float32)
    w3 = np.asarray(inp["L0_ffn_w3"], np.float32)

    def fc_major(w):
        return np.ascontiguousarray(
            w.reshape(KC, 128, KC, 128).transpose(2, 1, 0, 3).reshape(KC, 128, D))

    return {
        "wq_l": fc_major(wq),
        "wk_l": fc_major(wk),
        "wv_l": fc_major(wv),
        "wo_l": fc_major(wo),
        "wf_l": np.ascontiguousarray(
            wf.reshape(KC, 128, H).transpose(1, 0, 2).reshape(128, KC * H)),
        "nw_col": np.ascontiguousarray(nw.reshape(KC, 128).T),
        "w1_l": np.ascontiguousarray(
            w1.reshape(HC, 128, KC, 128).transpose(0, 3, 2, 1).reshape(HC, 128, D)),
        "w2_l": np.ascontiguousarray(
            w2.reshape(HC, 128, KC, 128).transpose(0, 3, 2, 1).reshape(HC, 128, D)),
        "w3_l": np.ascontiguousarray(
            w3.reshape(KC, 128, HC, 128).transpose(0, 3, 2, 1).reshape(KC, 128, FH)),
    }


def build_program():
    nc = bass.Bass("TRN2", target_bir_lowering=False, detect_race_conditions=False)

    x_hbm = nc.dram_tensor("x", [BPC, SEQ, D], F32, kind="ExternalInput")
    wq_l = nc.dram_tensor("wq_l", [KC, 128, D], F32, kind="ExternalInput")
    wk_l = nc.dram_tensor("wk_l", [KC, 128, D], F32, kind="ExternalInput")
    wv_l = nc.dram_tensor("wv_l", [KC, 128, D], F32, kind="ExternalInput")
    wo_l = nc.dram_tensor("wo_l", [KC, 128, D], F32, kind="ExternalInput")
    wf_l = nc.dram_tensor("wf_l", [128, KC * H], F32, kind="ExternalInput")
    nw_hbm = nc.dram_tensor("nw_col", [128, KC], F32, kind="ExternalInput")
    w1_l = nc.dram_tensor("w1_l", [HC, 128, D], F32, kind="ExternalInput")
    w2_l = nc.dram_tensor("w2_l", [HC, 128, D], F32, kind="ExternalInput")
    w3_l = nc.dram_tensor("w3_l", [KC, 128, FH], F32, kind="ExternalInput")
    out_hbm = nc.dram_tensor("out", [BPC, SEQ, D], F32, kind="ExternalOutput")

    skind = "ExternalOutput" if DEV_SCRATCH_OUT else "Internal"
    q1t = nc.dram_tensor("q1t", [HC, 128, D], BF16, kind=skind)
    q2t = nc.dram_tensor("q2t", [HC, 128, D], BF16, kind=skind)
    q3t = nc.dram_tensor("q3t", [KC, 128, FH], BF16, kind=skind)
    wo_b = nc.dram_tensor("wo_b", [KC, 128, D], BF16, kind=skind)
    q_hbm = nc.dram_tensor("q_hbm", [BPC * NTR, 128, H * 512], BF16, kind=skind)
    k_hbm = nc.dram_tensor("k_hbm", [BPC * NTR, 128, H * 512], BF16, kind=skind)
    v_hbm = nc.dram_tensor("v_hbm", [BPC * NTR, 128, H * 512], BF16, kind=skind)
    on_hbm = nc.dram_tensor("on_hbm", [NATG, 128, KC * TA], BF16, kind=skind)

    TAP_SPECS = {
        "hid0": ([128, HC * TB], BF16), "xnb0": ([128, KC * TB], BF16),
    }
    tap_hbm = {tn: nc.dram_tensor(f"tap_{tn}", list(TAP_SPECS[tn][0]),
                                  TAP_SPECS[tn][1], kind="ExternalOutput")
               for tn in DEV_TAPS}

    i128b_h = nc.inline_tensor(np.eye(128, dtype=ml_dtypes.bfloat16), "c_i128b")
    i128f_h = nc.inline_tensor(np.eye(128, dtype=np.float32), "c_i128f")
    ones_rf_h = nc.inline_tensor(np.ones((1, 128), np.float32), "c_ones_rf")
    ones_rb_h = nc.inline_tensor(np.ones((1, 128), ml_dtypes.bfloat16), "c_ones_rb")
    ones_cf_h = nc.inline_tensor(np.ones((128, 1), np.float32), "c_ones_cf")
    ones_cb_h = nc.inline_tensor(np.ones((128, 1), ml_dtypes.bfloat16), "c_ones_cb")
    m01 = np.tril(np.ones((CH, CH), np.float32)).T
    mask_np = np.tile(np.tile(m01, (1, H)), (2, 1)).astype(ml_dtypes.bfloat16)
    mask_h = nc.inline_tensor(mask_np, "c_mask")
    half2_np = np.zeros((128, 2), np.float32)
    half2_np[63, 0] = 1.0
    half2_np[127, 1] = 1.0
    half2_h = nc.inline_tensor(half2_np, "c_half2")
    sel2_np = np.zeros((2, 256), np.float32)
    sel2_np[0, 0:64] = 1.0
    sel2_np[1, 64:128] = 1.0
    sel2_np[1, 128:256] = 1.0
    sel2_h = nc.inline_tensor(sel2_np, "c_sel2")
    CVALS = [-0.5, 0.5, 1.0, 1e-6, 1e-5, 1e-8, 0.0]
    cval_h = nc.inline_tensor(
        np.repeat(np.array([CVALS], np.float32), 128, axis=0), "c_cvals")

    SA = nc.alloc_sbuf_tensor
    c_i128b = SA("s_i128b", [128, 128], BF16)
    c_i128f = SA("s_i128f", [128, 128], F32)
    c_ones_rf = SA("s_ones_rf", [1, 128], F32)
    c_ones_rb = SA("s_ones_rb", [1, 128], BF16)
    c_ones_cf = SA("s_ones_cf", [128, 1], F32)
    c_ones_cb = SA("s_ones_cb", [128, 1], BF16)
    c_mask = SA("s_mask", [128, H * CH], BF16)
    c_half2 = SA("s_half2", [128, 2], F32)
    c_sel2 = SA("s_sel2", [2, 256], F32)
    c_cval = SA("s_cvals", [128, len(CVALS)], F32)
    nw_col = SA("s_nw", [128, KC], F32)
    wf_sb = SA("s_wf", [128, KC * H], BF16)
    st_scale = SA("st_scale", [128, 32], F32)
    st_bias = SA("st_bias", [128, 32], F32)
    st_rstd2 = SA("st_rstd2", [128, 32], F32)
    p_acc = SA("p_acc", [128, 4], F32)
    p_tmp = SA("p_tmp", [128, 4], F32)
    p_inv = SA("p_inv", [128, 8], F32)
    p_sc = SA("p_sc", [1, 8], F32)
    a_st = SA("a_st", [128, 8], F32)

    for i, cv in enumerate(CVALS):
        nc.const_aps.register(F32, cv, c_cval[:, i:i + 1])

    base = nc.sbuf_base
    limit = nc.sbuf_top
    cur = [base]

    def place(name, shape, dtype):
        nbytes = int(np.prod(shape[1:])) * mybir.dt.size(dtype)
        off = (cur[0] + 31) & ~31
        cur[0] = off + nbytes
        assert cur[0] <= limit, f"SBUF overflow at {name}: {cur[0]} > {limit}"
        return nc.alloc_sbuf_tensor_at(name, list(shape), dtype, offset=off)

    p_w = [place(f"p_w{i}", [128, 1024], F32) for i in range(3)]
    p_g1 = [place(f"p_g1{i}", [128, 1024], BF16) for i in range(2)]
    p_q = [place(f"p_q{i}", [128, 1024], BF16) for i in range(2)]
    prep_end = cur[0]
    a_lstr = [place(f"a_lstr{i}", [128, 16 * H], F32) for i in range(2)]
    a_cfmr = [place(f"a_cfmr{i}", [16, SEQ], BF16) for i in range(2)]

    SLOTW = KC * 512
    r1 = (cur[0] + 31) & ~31
    xn2 = nc.alloc_sbuf_tensor_at("xn2", [128, 6 * SLOTW], BF16, offset=r1)
    a_qn = [nc.alloc_sbuf_tensor_at(f"a_qn{i}", [128, H * 512], BF16,
                                    offset=r1 + (0 + i) * SLOTW * 2) for i in range(2)]
    a_kn = [nc.alloc_sbuf_tensor_at(f"a_kn{i}", [128, H * 512], BF16,
                                    offset=r1 + (2 + i) * SLOTW * 2) for i in range(2)]
    a_vn = [nc.alloc_sbuf_tensor_at(f"a_vn{i}", [128, H * 512], BF16,
                                    offset=r1 + (4 + i) * SLOTW * 2) for i in range(2)]
    cur[0] = r1 + 6 * SLOTW * 2

    r2 = (cur[0] + 31) & ~31
    cur[0] = r2
    a_x = [place(f"a_x{i}", [128, D], F32) for i in range(2)]
    a_xt = [place(f"a_xt{i}", [128, D], BF16) for i in range(2)]
    w_f32 = [place(f"w_f32{i}", [128, D], F32) for i in range(2)]
    w_bf = [place(f"w_bf{i}", [128, D], BF16) for i in range(2)]
    stq = [place(f"stq{i}", [128, 512], BF16) for i in range(2)]
    a_ld = [place(f"a_ld{i}", [16, TA], F32) for i in range(2)]
    a_cum = [place(f"a_cum{i}", [16, TA], F32) for i in range(2)]
    a_lsg = place("a_lsg", [16, 2 * TA], F32)
    a1_end = cur[0]
    cur[0] = r2
    a_kt = [place(f"a_kt{i}", [128, D], BF16) for i in range(2)]
    a_vt = [place(f"a_vt{i}", [128, D], BF16) for i in range(2)]
    a_vd = [place(f"a_vd{i}", [128, D], BF16) for i in range(2)]
    a_S = [place(f"a_S{i}", [128, H * EH], BF16) for i in range(2)]
    a_q1 = [place(f"a_q1_{i}", [128, H * CH], BF16) for i in range(2)]
    a_cq = [place(f"a_cq{i}", [128, H * CH], BF16) for i in range(2)]
    a_sc = [place(f"a_sc{i}", [128, H * CH], BF16) for i in range(2)]
    a_crow = [place(f"a_crow{i}", [1, H * 128], BF16) for i in range(2)]
    a_ict = [place(f"a_ict{i}", [128, H], F32) for i in range(2)]
    a_dd = [place(f"a_dd{i}", [128, H], F32) for i in range(2)]
    a_tot = place("a_tot", [2, 4 * H], F32)
    a_eto = place("a_eto", [128, 4 * H], F32)
    a_sq = place("a_sq", [128, 2 * TA], BF16)
    a_r1 = place("a_r1", [1, 2 * TA], F32)
    a_o = [place(f"a_o{i}", [128, KC * TA], BF16) for i in range(2)]
    a2_end = cur[0]
    cur[0] = max(a1_end, a2_end)
    a_end = cur[0]

    cur[0] = prep_end
    b_xn = place("b_xn", [128, KC * TB], BF16)
    b_on = nc.alloc_sbuf_tensor_at("b_on", [128, 4 * KC * TA], BF16,
                                   offset=(prep_end + 31) & ~31)
    b_hid = place("b_hid", [128, HC * TB], BF16)
    b_q1s = [place(f"b_q1s{i}", [128, D], BF16) for i in range(2)]
    b_q2s = [place(f"b_q2s{i}", [128, D], BF16) for i in range(2)]
    b_q3s = [place(f"b_q3s{i}", [128, FH // 2], BF16) for i in range(2)]
    b_wob = [place("b_wob0", [128, D], BF16)]
    _sig_off = (cur[0] + 31) & ~31
    b_sig = [place(f"b_sig{i}", [128, TB], BF16) for i in range(2)]
    b_s = [nc.alloc_sbuf_tensor_at(f"b_s{i}", [128, TB], BF16,
                                   offset=_sig_off + i * TB * 2) for i in range(2)]
    b_x = [place("b_x0", [128, D], F32)]
    _bxt_off = (cur[0] + 31) & ~31
    b_xt = [place("b_xt0", [128, D], BF16)]
    b_g = [nc.alloc_sbuf_tensor_at(f"b_g{i}", [128, TB], BF16,
                                   offset=_bxt_off + i * TB * 2) for i in range(2)]
    b_xc = [place(f"b_xc{i}", [128, TB], BF16) for i in range(2)]
    b_os = [place(f"b_os{i}", [128, TB], F32) for i in range(2)]
    b_end = cur[0]
    assert max(a_end, b_end) <= limit, f"SBUF: a={a_end} b={b_end} limit={limit}"

    S = Sched(nc)
    PB = [Bank(nc.alloc_psum_tensor(f"pb{i}", [128, 512], F32)) for i in range(8)]

    def view(hd, p0, pn, off, dims):
        rl = hd.shape[-1]
        return bass.AP(hd, p0 * rl + off, [[rl, pn]] + dims)

    CONST = []
    for dst, src in [(c_i128b, i128b_h), (c_i128f, i128f_h), (c_ones_rf, ones_rf_h),
                     (c_ones_rb, ones_rb_h), (c_ones_cf, ones_cf_h),
                     (c_ones_cb, ones_cb_h), (c_mask, mask_h),
                     (c_half2, half2_h), (c_sel2, sel2_h),
                     (c_cval, cval_h), (nw_col, nw_hbm)]:
        CONST.append(S.qs((lambda d, s_: lambda e: e.dma_start(out=d[:, :], in_=s_[:, :]))(dst, src), ch="const"))
    CONST.append(S.qg(lambda e: e.dma_start(out=wf_sb[:, :], in_=wf_l[:, :]), ch="constg"))

    def _mk_units(srcs):
        units = []
        for mi, src, nch, width, dst in srcs:
            for ci in range(nch):
                if width == D:
                    chunks = ((0, 1024), (1024, 1024))
                else:
                    chunks = tuple((o * 1024, 1024) for o in range(5)) + ((5120, 512),)
                for (o, w_) in chunks:
                    units.append((mi, src, dst, ci, o, w_))
        return units

    G_UNITS = _mk_units([(0, w1_l, HC, D, None), (1, w2_l, HC, D, None),
                         (2, w3_l, KC, FH, None)])
    Q_UNITS = []
    for ci in range(HC):
        for (mi, src_, dst_) in ((0, w1_l, q1t), (1, w2_l, q2t)):
            for o in (0, 1024):
                Q_UNITS.append((mi, src_, dst_, ci, o, 1024))
    for ci in range(KC):
        for (o, w_) in tuple((j * 1024, 1024) for j in range(5)) + ((5120, 512),):
            Q_UNITS.append((2, w3_l, q3t, ci, o, w_))
    WO_UNITS = [(fc, o) for fc in range(KC) for o in (0, 1024)]
    PS = {"gi": 0, "qi": 0, "wi": 0, "lc": 0,
          "bfree": [list(CONST) for _ in range(3)],
          "qfree": [None] * 2, "wofree": [None], "accin": [False] * 3,
          "gdone": None, "inv": None, "s1free": [None, None], "qsdone": {}}

    def prep_wo_step():
        if PS["wi"] >= len(WO_UNITS):
            return False
        fc, off = WO_UNITS[PS["wi"]]
        PS["wi"] += 1
        bi = PS["lc"] % 3
        PS["lc"] += 1
        t_in = S.qs(
            (lambda b_, c_, o_: lambda e: e.dma_start(
                out=p_w[b_][:, :], in_=wo_l[c_, :, o_:o_ + 1024]))(bi, fc, off),
            deps=PS["bfree"][bi], ch=f"pw{bi}")
        t_c = S.dv(
            (lambda b_: lambda e: e.tensor_copy(p_g1[0][:, :], p_w[b_][:, :]))(bi),
            deps=[t_in] + PS["wofree"], sig=True)
        PS["bfree"][bi] = [t_c]
        t_out = S.qs(
            (lambda c_, o_: lambda e: e.dma_start(
                out=wo_b[c_, :, o_:o_ + 1024], in_=p_g1[0][:, :]))(fc, off),
            deps=[t_c], ch="pwo")
        PS["wofree"] = [t_out]
        if PS["wi"] == len(WO_UNITS):
            PS["s1free"][0] = [t_out]
        return True

    def prep_gamma_step():
        if PS["gi"] >= len(G_UNITS):
            return
        mi, src, _, ci, off, w_ = G_UNITS[PS["gi"]]
        bi = PS["lc"] % 3
        PS["lc"] += 1
        PS["gi"] += 1
        buf = p_w[bi]
        t_in = S.qs(
            (lambda bb, s_, c_, o_, ww: lambda e: e.dma_start(
                out=bb[:, 0:ww], in_=s_[c_, :, o_:o_ + ww]))(buf, src, ci, off, w_),
            deps=PS["bfree"][bi], ch=f"pw{bi}")
        t_red = S.dv(
            (lambda bb, ww: lambda e: e.tensor_reduce(
                p_tmp[:, 3:4], bb[:, 0:ww], AX.X, ALU.add, apply_absolute_value=True))(buf, w_),
            deps=[t_in], sig=True)
        PS["bfree"][bi] = [t_red]
        if not PS["accin"][mi]:
            PS["accin"][mi] = True
            t_acc = S.dv(
                (lambda m_: lambda e: e.tensor_copy(p_acc[:, m_:m_ + 1], p_tmp[:, 3:4]))(mi),
                deps=[t_red], sig=True)
        else:
            t_acc = S.dv(
                (lambda m_: lambda e: e.tensor_add(
                    p_acc[:, m_:m_ + 1], p_acc[:, m_:m_ + 1], p_tmp[:, 3:4]))(mi),
                deps=[t_red], sig=True)
        PS["gdone"] = t_acc

    def prep_gamma_finalize():
        bank = PB[7]
        toks = []
        for mi in range(3):
            nelem = FH * D
            t_mm = S.pe(
                (lambda m_: lambda e: e.matmul(
                    bank.ap[0:1, 4 * m_:4 * m_ + 1], p_acc[:, m_:m_ + 1],
                    c_ones_cf[:, :], start=True, stop=True))(mi),
                deps=[PS["gdone"]] + bank.wdeps(), sig=True)
            bank.wrote(t_mm)
            t_c = S.dv(
                (lambda m_, n_: lambda e: e.tensor_scalar(
                    p_sc[0:1, m_:m_ + 1], bank.ap[0:1, 4 * m_:4 * m_ + 1],
                    1.0 / n_, 1e-8, ALU.mult, ALU.add))(mi, nelem),
                deps=bank.rdeps(), sig=True)
            bank.read(t_c)
            t_r = S.dv(
                (lambda m_: lambda e: e.reciprocal(
                    p_sc[0:1, 4 + m_:5 + m_], p_sc[0:1, m_:m_ + 1]))(mi),
                deps=[t_c], sig=True)
            t_b = S.pe(
                (lambda m_: lambda e: e.matmul(
                    bank.ap[:, 8 + m_:9 + m_], c_ones_rf[:, :],
                    p_sc[0:1, 4 + m_:5 + m_], start=True, stop=True))(mi),
                deps=[t_r] + bank.wdeps(), sig=True)
            bank.wrote(t_b)
            t_e = S.dv(
                (lambda m_: lambda e: e.tensor_copy(
                    p_inv[:, m_:m_ + 1], bank.ap[:, 8 + m_:9 + m_]))(mi),
                deps=bank.rdeps(), sig=True)
            bank.read(t_e)
            t_n = S.dv(
                (lambda m_: lambda e: e.tensor_scalar(
                    p_inv[:, 4 + m_:5 + m_], p_inv[:, m_:m_ + 1], -1.0, None, ALU.mult))(mi),
                deps=[t_e], sig=True)
            toks.append(t_n)
        PS["inv"] = toks

    def prep_quant_step():
        if PS["qi"] >= len(Q_UNITS):
            return
        if PS["inv"] is None:
            while PS["gi"] < len(G_UNITS):
                prep_gamma_step()
            prep_gamma_finalize()
        mi, src, dst, ci, off, w_ = Q_UNITS[PS["qi"]]
        bi = PS["lc"] % 3
        PS["lc"] += 1
        qbi = PS["qi"] % 2
        PS["qi"] += 1
        buf = p_w[bi]
        qb = p_q[qbi]
        t_in = S.qs(
            (lambda bb, s_, c_, o_, ww: lambda e: e.dma_start(
                out=bb[:, 0:ww], in_=s_[c_, :, o_:o_ + ww]))(buf, src, ci, off, w_),
            deps=PS["bfree"][bi] + PS["inv"], ch=f"pw{bi}")
        gi2 = PS["qi"] % 2
        invc = p_inv[:, mi:mi + 1]
        ninvc = p_inv[:, 4 + mi:5 + mi]
        t_s1 = S.dv(
            (lambda bb, ww, iv, g_: lambda e: e.tensor_scalar(
                g_[:, 0:ww], bb[:, 0:ww], iv, 0.5, ALU.mult, ALU.is_ge))(
                buf, w_, invc, p_g1[gi2]),
            deps=[t_in, PS["s1free"][gi2]], sig=True)
        t_s2 = S.dv(
            (lambda bb, ww, iv, q_: lambda e: e.tensor_scalar(
                q_[:, 0:ww], bb[:, 0:ww], iv, 0.5, ALU.mult, ALU.is_ge))(
                buf, w_, ninvc, qb),
            deps=[t_in, PS["qfree"][qbi]], sig=True)
        PS["bfree"][bi] = [t_s2]
        t_add = S.dv(
            (lambda ww, q_, g_: lambda e: e.tensor_sub(
                q_[:, 0:ww], g_[:, 0:ww], q_[:, 0:ww]))(w_, qb, p_g1[gi2]),
            deps=[t_s1, t_s2], sig=True)
        PS["s1free"][gi2] = t_add
        t_out = S.qs(
            (lambda q_, d_, c_, o_, ww: lambda e: e.dma_start(
                out=d_[c_, :, o_:o_ + ww], in_=q_[:, 0:ww]))(qb, dst, ci, off, w_),
            deps=[t_add], ch=f"pq{qbi}")
        PS["qfree"][qbi] = t_out
        PS["qsdone"].setdefault((mi, ci), []).append(t_out)

    def prep_steps(n, quant):
        if not DEV_PREP:
            return
        for _ in range(n):
            if prep_wo_step():
                continue
            (prep_quant_step if quant else prep_gamma_step)()

    def mm(out_ap, l_ap, r_ap, start, stop, deps=(), sig=False):
        return S.pe(
            (lambda o_, l_, r_, s0, s1: lambda e: e.matmul(o_, l_, r_, start=s0, stop=s1))(
                out_ap, l_ap, r_ap, start, stop),
            deps, sig)

    x_free = [[], []]
    xt_free = [[], []]
    cum_free = [[], []]
    lsg_free = [[]]
    xn_evs = {}
    cfm_tok = {}
    lstr_tok = {}
    slot_readers = [[] for _ in range(6)]
    pb01 = [0]
    xblk = [0]

    def emit_ln_xtile(br, xt, evac_deps):
        slot = SLOT[br][xt // 2]
        xp = xt % 2
        evs = []
        for blk in range(2):
            gtt = xblk[0]
            xblk[0] += 1
            xb = gtt % 2
            t_x = S.qs(
                (lambda b_, bb, r0: lambda e: e.dma_start(
                    out=a_x[b_][:, :], in_=x_hbm[bb, r0:r0 + 128, :]))(
                    xb, br, xt * TA + blk * 128),
                deps=x_free[xb] + CONST, ch=f"x{xb}")
            t_sq = S.ac(
                (lambda b_, xc: lambda e: e.activation(
                    xc[:, :], a_x[b_][:, :], AF.Square, accum_out=a_st[:, 1:2]))(
                    xb, a_xt[blk]),
                deps=[t_x] + xt_free[blk], sig=True)
            t_sm = S.dv(
                (lambda b_, xc: lambda e: e.tensor_scalar(
                    xc[:, :], a_x[b_][:, :], 1.0, 0.0, ALU.mult, ALU.add,
                    accum_out=a_st[:, 0:1]))(xb, a_xt[blk]),
                deps=[t_sq], sig=True)
            t_mu = S.dv(
                lambda e: e.tensor_scalar(a_st[:, 3:4], a_st[:, 0:1], 1.0 / D, None, ALU.mult),
                deps=[t_sm], sig=True)
            t_e2 = S.dv(
                lambda e: e.tensor_scalar(a_st[:, 1:2], a_st[:, 1:2], 1.0 / D, None, ALU.mult),
                deps=[t_sq], sig=True)
            t_nv = S.dv(
                lambda e: e.scalar_tensor_tensor(
                    a_st[:, 2:3], a_st[:, 3:4], a_st[:, 3:4], a_st[:, 1:2],
                    ALU.mult, ALU.subtract),
                deps=[t_mu, t_e2], sig=True)
            t_ln = S.ac(
                lambda e: e.activation(a_st[:, 4:5], a_st[:, 2:3], AF.Ln,
                                       scale=-1.0, bias=1e-5),
                deps=[t_nv], sig=True)
            t_rs = S.ac(
                (lambda c: lambda e: e.activation(
                    st_scale[:, c:c + 1], a_st[:, 4:5], AF.Exp, scale=-0.5))(gtt),
                deps=[t_ln], sig=True)
            t_nb = S.dv(
                (lambda c: lambda e: e.tensor_scalar(
                    st_bias[:, c:c + 1], st_scale[:, c:c + 1], a_st[:, 3:4], -1.0,
                    ALU.mult, ALU.mult))(gtt),
                deps=[t_rs], sig=True)
            t_ap = S.ac(
                (lambda b_, xc, c: lambda e: e.activation(
                    xc[:, :], a_x[b_][:, :], AF.Identity,
                    scale=st_scale[:, c:c + 1], bias=st_bias[:, c:c + 1]))(
                    xb, a_xt[blk], gtt),
                deps=[t_nb, t_sm], sig=True)
            x_free[xb] = [t_ap, t_sm]
            bk7 = PB[7]
            for g in range(4):
                t_last = None
                for j in range(4):
                    fcb = g * 4 + j
                    t_last = mm(bk7.ap[:, j * 128:(j + 1) * 128],
                                a_xt[blk][:, fcb * 128:(fcb + 1) * 128],
                                c_i128b[:, :], True, True,
                                deps=[t_ap] + (bk7.wdeps() if j == 0 else []),
                                sig=(j == 3))
                bk7.wrote(t_last)
                colbase = slot * SLOTW + (g * 4) * 512 + (xt % 2) * 256 + blk * 128
                t_ev = S.ac(
                    (lambda bkap, cb: lambda e: e.copy(
                        view(xn2, 0, 128, cb, [[512, 4], [1, 128]]),
                        bkap[:, 0:512].rearrange("q (a b) -> q a b", a=4)))(
                        bk7.ap, colbase),
                    deps=bk7.rdeps() + evac_deps, sig=True)
                bk7.read(t_ev)
                evs.append(t_ev)
            xt_free[blk] = evs[-4:]
        xn_evs[(br, xt)] = evs
        bk7 = PB[7]
        t_last = None
        for kc in range(KC):
            t_last = mm(bk7.ap[0:16, 0:TA], wf_sb[:, kc * H:(kc + 1) * H],
                        xn2[:, slot * SLOTW + kc * 512 + (xt % 2) * 256:
                            slot * SLOTW + kc * 512 + (xt % 2) * 256 + TA],
                        kc == 0, kc == KC - 1,
                        deps=evs + (bk7.wdeps() if kc == 0 else []) + CONST,
                        sig=(kc == KC - 1))
        bk7.wrote(t_last)
        t_abs = S.ac(
            lambda e: e.activation(a_lsg[:, 0:TA], bk7.ap[0:16, 0:TA], AF.Abs),
            deps=bk7.rdeps() + lsg_free[0], sig=True)
        t_exp = S.ac(lambda e: e.activation(a_lsg[:, TA:2 * TA], a_lsg[:, 0:TA],
                                            AF.Exp, scale=-1.0), deps=[t_abs], sig=True)
        t_ln1 = S.ac(lambda e: e.activation(a_lsg[:, 0:TA], a_lsg[:, TA:2 * TA],
                                            AF.Ln, bias=c_cval[0:16, 2:3]),
                     deps=[t_exp] + CONST, sig=True)
        t_min = S.dv(
            (lambda p_: lambda e: e.tensor_scalar(a_ld[p_][:, :], bk7.ap[0:16, 0:TA],
                                                  0.0, None, ALU.min))(xp),
            deps=bk7.rdeps() + cum_free[xp] + [t_abs], sig=True)
        bk7.read(t_min)
        bk7.read(t_abs)
        t_ld = S.dv(
            (lambda p_: lambda e: e.tensor_sub(a_ld[p_][:, :], a_ld[p_][:, :],
                                               a_lsg[:, 0:TA]))(xp),
            deps=[t_ln1, t_min], sig=True)
        lsg_free[0] = [t_ld]
        t_cums = []
        for c in range(NCH):
            t_cums.append(S.dv(
                (lambda p_, c_: lambda e: e.tensor_tensor_scan(
                    a_cum[p_][:, c_ * CH:(c_ + 1) * CH], a_ld[p_][:, c_ * CH:(c_ + 1) * CH],
                    a_ld[p_][:, c_ * CH:(c_ + 1) * CH], 0.0, ALU.add, ALU.bypass))(xp, c),
                deps=[t_ld] + cum_free[xp], sig=True))
        t_cfm = S.ac(
            (lambda p_, b_, x_: lambda e: e.activation(
                a_cfmr[b_][:, x_ * TA:(x_ + 1) * TA], a_cum[p_][:, :], AF.Exp))(xp, br, xt),
            deps=t_cums, sig=True)
        cfm_tok[(br, xt)] = t_cfm
        t_tl = None
        for tp in range(2):
            t_tl = mm(bk7.ap[:, tp * 16:(tp + 1) * 16],
                      a_cum[xp][:, tp * 128:(tp + 1) * 128],
                      c_i128f[0:16, 0:16], True, True,
                      deps=t_cums + (bk7.wdeps() if tp == 0 else []),
                      sig=(tp == 1))
        bk7.wrote(t_tl)
        t_le = S.ac(
            (lambda bkap, b_, x_: lambda e: e.copy(
                a_lstr[b_][:, x_ * 32:(x_ + 1) * 32], bkap[:, 0:32]))(bk7.ap, br, xt),
            deps=bk7.rdeps(), sig=True)
        bk7.read(t_le)
        lstr_tok[(br, xt)] = t_le
        cum_free[xp] = [t_cfm, t_le]

    wf_free = [list(CONST), list(CONST)]
    wbf_free = [[], []]
    stq_free = [[], []]
    wcnt = [0]
    scnt = [0]
    W_SRC = [wq_l, wk_l, wv_l]
    W_DST = [q_hbm, k_hbm, v_hbm]

    def emit_w_unit(br, wi, fc):
        wj = wcnt[0] % 2
        wb = wcnt[0] % 2
        wcnt[0] += 1
        t_wl = S.qs(
            (lambda j_, s_, f_: lambda e: e.dma_start(
                out=w_f32[j_][:, :], in_=s_[f_, :, :]))(wj, W_SRC[wi], fc),
            deps=wf_free[wj], ch=f"wf{wj}")
        t_wc = S.dv(
            (lambda j_, b_: lambda e: e.tensor_copy(w_bf[b_][:, :], w_f32[j_][:, :]))(wj, wb),
            deps=[t_wl] + wbf_free[wb], sig=True)
        wf_free[wj] = [t_wc]
        t_last = None
        for nt in range(NTR):
            slot = SLOT[br][nt]
            bk = PB[pb01[0] % 6]
            pb01[0] += 1
            ndeps = xn_evs[(br, 2 * nt)] + xn_evs[(br, 2 * nt + 1)]
            for kc in range(KC):
                t_last = mm(bk.ap[:, 0:512],
                            w_bf[wb][:, kc * 128:(kc + 1) * 128],
                            xn2[:, slot * SLOTW + kc * 512:slot * SLOTW + (kc + 1) * 512],
                            kc == 0, kc == KC - 1,
                            deps=([t_wc] + ndeps + (bk.wdeps() if kc == 0 else [])),
                            sig=(kc == KC - 1))
            bk.wrote(t_last)
            slot_readers[slot].append(t_last)
            si = scnt[0] % 2
            scnt[0] += 1
            if wi == 2:
                t_ev = S.ac(
                    (lambda bkap, s_: lambda e: e.copy(stq[s_][:, :], bkap[:, 0:512]))(
                        bk.ap, si),
                    deps=bk.rdeps() + stq_free[si], sig=True)
            else:
                t_ev = S.ac(
                    (lambda bkap, s_: lambda e: e.activation(
                        stq[s_][:, :], bkap[:, 0:512], AF.Silu))(bk.ap, si),
                    deps=bk.rdeps() + stq_free[si], sig=True)
            bk.read(t_ev)
            t_st = S.qa(
                (lambda s_, d_, g_, f_: lambda e: e.dma_start(
                    out=d_[g_, :, f_ * 512:(f_ + 1) * 512], in_=stq[s_][:, :]))(
                    si, W_DST[wi], br * NTR + nt, fc),
                deps=[t_ev], ch=f"st{si}")
            stq_free[si] = [t_st]
        wbf_free[wb] = [t_last]

    for xt in range(8):
        emit_ln_xtile(0, xt, [])
    if DEV_NROW > 1:
        for xt in range(4):
            emit_ln_xtile(1, xt, [])
    for wi in range(3):
        for fc in range(KC):
            emit_w_unit(0, wi, fc)
            prep_steps(4, quant=(PS["gi"] >= len(G_UNITS)))
    if DEV_NROW > 1:
        for xt in range(4, 8):
            slot = SLOT[1][xt // 2]
            emit_ln_xtile(1, xt, list(slot_readers[slot]))
        for wi in range(3):
            for fc in range(KC):
                emit_w_unit(1, wi, fc)
                prep_steps(4, quant=(PS["gi"] >= len(G_UNITS)))

    FBA = S.fence()
    qn_free = [[FBA], [FBA]]
    kn_free = [[FBA], [FBA]]
    vn_free = [[FBA], [FBA]]
    kt_free = [[FBA], [FBA]]
    vt_free = [[FBA], [FBA]]
    vd_free = [[FBA], [FBA]]
    o_free = [[FBA], [FBA]]
    q1_free = [[FBA], [FBA]]
    cq_free = [[FBA], [FBA]]
    crow_free = [[FBA], [FBA]]
    ict_free = [[FBA], [FBA]]
    eto_free_r = [[FBA], [FBA]]
    dd_free = [[FBA], [FBA]]
    sc_free = [[FBA], [FBA]]
    sq_free = [[FBA], [FBA]]
    r1_free = [[FBA]]
    S_write = [[None] * H, [None] * H]
    S_first = [[FBA], [FBA]]
    nt_load = {}
    ttcnt = [0]
    ccnt = [0]
    acnt = [0]
    tile_o = {}
    rowst = [{}, {}]
    R_SC = [[4, 5], [0, 1]]
    R_OB = [6, 3]
    R_UB = [2, 7]

    def a2_loads(br, nt):
        pn = (2 * nt + br) % 2
        g_ = br * NTR + nt

        def ld(dst, src, ch, free):
            return S.qs(
                (lambda d_, s_, gg: lambda e: e.dma_start(
                    out=d_[:, :], in_=s_[gg, :, :]))(dst, src, g_),
                deps=free, ch=ch)
        t_q = ld(a_qn[pn], q_hbm, f"qn{pn}", qn_free[pn])
        t_k = ld(a_kn[pn], k_hbm, f"kn{pn}", kn_free[pn])
        t_v = ld(a_vn[pn], v_hbm, f"vn{pn}", vn_free[pn])
        nt_load[(br, nt)] = (t_q, t_k, t_v)

    def emit_trans(src_buf, tok_dep, dst, dst_free, bank, base):
        evs = []
        for g in range(4):
            t_tl = None
            for j in range(4):
                h = g * 4 + j
                t_tl = mm(bank.ap[:, j * 128:(j + 1) * 128],
                          src_buf[:, h * 512 + base:h * 512 + base + 128],
                          c_i128b[:, :], True, True,
                          deps=[tok_dep] + (bank.wdeps() if j == 0 else []),
                          sig=(j == 3))
            bank.wrote(t_tl)
            t_ev = S.dv(
                (lambda bkap, d_, g_: lambda e: e.tensor_copy(
                    d_[:, g_ * 512:(g_ + 1) * 512], bkap[:, 0:512]))(bank.ap, dst, g),
                deps=bank.rdeps() + dst_free, sig=True)
            bank.read(t_ev)
            evs.append(t_ev)
        return evs

    def emit_pre_a(br, pn, ah, tt):
        st = rowst[br]
        tpg = ttcnt[0] % 2
        ttcnt[0] += 1
        st["tpg"] = tpg
        st["kt_evs"] = emit_trans(a_kn[pn], nt_load[(br, st["nt"])][1], a_kt[tpg],
                                  kt_free[tpg], PB[R_UB[br]], ah * 256 + tt * 128)
        kt_free[tpg] = []

    def emit_pre_b(br, pn, ah, tt):
        st = rowst[br]
        tpg = st["tpg"]
        st["vt_evs"] = emit_trans(a_vn[pn], nt_load[(br, st["nt"])][2], a_vt[tpg],
                                  vt_free[tpg], PB[R_SC[br][0]], ah * 256 + tt * 128)
        vt_free[tpg] = []

    def emit_pre_c(br, pn, ah, tt, tg_row):
        st = rowst[br]
        tpg = st["tpg"]
        atl = tg_row // 2
        lstr_sl = a_lstr[br][:, tg_row * 16:(tg_row + 1) * 16]
        dep_l = [lstr_tok[(br, atl)]]
        t_ic = S.ac(
            (lambda t_, sl: lambda e: e.activation(
                a_ict[t_][:, :], sl, AF.Exp, scale=-1.0))(tpg, lstr_sl),
            deps=dep_l + ict_free[tpg], sig=True)
        ict_free[tpg] = []
        t_crows = [S.qg(
            (lambda t_, b_, tg: lambda e: e.dma_start(
                out=bass.AP(a_crow[t_], 0, [[2048, 1], [128, H], [1, 128]]),
                in_=view(a_cfmr[b_], 0, H, tg * 128, [[1, 128]])))(tpg, br, tg_row),
            deps=[cfm_tok[(br, atl)]] + crow_free[tpg], ch=f"crowg{tpg}")]
        crow_free[tpg] = []
        bko = PB[R_OB[br]]
        bo = br * 32
        t_ta = mm(bko.ap[0:2, 0:16], c_half2[:, :], lstr_sl,
                  True, True, deps=dep_l + bko.wdeps(), sig=True)
        bko.wrote(t_ta)
        t_ee = S.ac(
            (lambda bkap, b_: lambda e: e.activation(
                a_tot[0:2, b_ + 16:b_ + 32], bkap[0:2, 0:16], AF.Exp))(bko.ap, bo),
            deps=bko.rdeps() + eto_free_r[br], sig=True)
        bko.read(t_ee)
        e1 = mm(bko.ap[:, 48:64], c_ones_rf[:, :], a_tot[0:1, bo + 16:bo + 32],
                True, True, deps=[t_ee] + bko.wdeps(), sig=False)
        e2 = mm(bko.ap[:, 64:80], c_sel2[0:2, 128:256], a_tot[0:2, bo + 16:bo + 32],
                True, True, deps=[t_ee], sig=True)
        bko.wrote(e2)
        t_ev = S.ac(
            (lambda bkap, b_: lambda e: e.copy(a_eto[:, b_:b_ + 32], bkap[:, 48:80]))(
                bko.ap, bo),
            deps=bko.rdeps() + eto_free_r[br], sig=True)
        bko.read(t_ev)
        eto_free_r[br] = []
        t_dd1 = S.dv(
            (lambda t_, b_: lambda e: e.tensor_mul(
                a_dd[t_][0:64, :], a_eto[0:64, b_:b_ + 16], a_ict[t_][0:64, :]))(tpg, bo),
            deps=[t_ev, t_ic] + dd_free[tpg], sig=True)
        t_dd2 = S.dv(
            (lambda t_, b_: lambda e: e.tensor_mul(
                a_dd[t_][64:128, :], a_eto[64:128, b_ + 16:b_ + 32],
                a_ict[t_][64:128, :]))(tpg, bo),
            deps=[t_ev, t_ic], sig=True)
        t_vd = S.dv(
            (lambda t_: lambda e: e.tensor_mul(
                view(a_vd[t_], 0, 128, 0, [[128, H], [1, 128]]),
                view(a_vt[t_], 0, 128, 0, [[128, H], [1, 128]]),
                view(a_dd[t_], 0, 128, 0, [[1, H], [0, 128]])))(tpg),
            deps=[t_dd1, t_dd2] + st["vt_evs"] + vd_free[tpg], sig=True)
        vd_free[tpg] = []
        dd_free[tpg] = [t_vd]
        st["pre"] = {"t_ic": t_ic, "t_ev": t_ev, "t_vd": t_vd, "t_crows": t_crows}

    def emit_cpre(br, pn, ah, c):
        st = rowst[br]
        tpg = st["tpg"]
        pre = st["pre"]
        half = c % 2
        pr = ccnt[0] % 2
        st["pr"] = pr
        bko = PB[R_OB[br]]
        cq_toks = []
        for rnd in range(2):
            t_cb = mm(bko.ap[:, 0:512], c_ones_rb[:, :],
                      bass.AP(a_crow[tpg], rnd * 1024 + half * 64,
                              [[2048, 1], [128, 8], [1, 64]]),
                      True, True,
                      deps=pre["t_crows"] + bko.wdeps(), sig=True)
            bko.wrote(t_cb)
            t_ce = S.ac(
                (lambda bkap, pr_, r_: lambda e: e.copy(
                    a_cq[pr_][:, r_ * 512:(r_ + 1) * 512], bkap[:, 0:512]))(
                    bko.ap, pr, rnd),
                deps=bko.rdeps() + cq_free[pr], sig=True)
            bko.read(t_ce)
            cq_toks.append(t_ce)
        cq_free[pr] = []
        t_q1 = S.dv(
            (lambda pr_, p_, off: lambda e: e.tensor_mul(
                view(a_q1[pr_], 0, 128, 0, [[CH, H], [1, CH]]),
                view(a_qn[p_], 0, 128, off, [[512, H], [1, CH]]),
                view(a_cq[pr_], 0, 128, 0, [[CH, H], [1, CH]])))(pr, pn, ah * 256 + c * CH),
            deps=cq_toks + [nt_load[(br, st["nt"])][0]] + q1_free[pr], sig=True)
        q1_free[pr] = []
        st["t_q1"] = t_q1

    def emit_scores(br, pn, ah, c):
        st = rowst[br]
        tpg = st["tpg"]
        pr = st["pr"]
        half = c % 2
        p0 = 64 * half
        t_q1 = st["t_q1"]
        kbase = ah * 256 + c * CH
        t_kn = nt_load[(br, st["nt"])][1]
        sc_toks = []
        for g in range(2):
            bk = PB[R_SC[br][g]]
            for j in range(8):
                h = g * 8 + j
                mm(bk.ap[p0:p0 + 64, j * 64:(j + 1) * 64],
                   a_kn[pn][:, h * 512 + kbase: h * 512 + kbase + CH],
                   a_q1[pr][:, h * CH:(h + 1) * CH],
                   True, True,
                   deps=([t_q1, t_kn] + (bk.wdeps() if j == 0 else [])),
                   sig=(j == 7))
            bk.wrote(("pe", S.cnt["pe"]))
        for g in range(2):
            bk = PB[R_SC[br][g]]
            t_s1 = S.dv(
                (lambda bkap, q0_, g_, t_, pr_: lambda e: e.tensor_mul(
                    view(a_sc[pr_], q0_, 64, g_ * 512, [[CH, 8], [1, CH]]),
                    bkap[q0_:q0_ + 64, :].rearrange("q (h i) -> q h i", h=8),
                    view(a_ict[t_], q0_, 64, g_ * 8, [[1, 8], [0, CH]])))(
                    bk.ap, p0, g, tpg, pr),
                deps=bk.rdeps() + [st["pre"]["t_ic"]] + sc_free[pr], sig=True)
            bk.read(t_s1)
            t_s2 = S.dv(
                (lambda q0_, g_, pr_: lambda e: e.tensor_mul(
                    a_sc[pr_][q0_:q0_ + 64, g_ * 512:(g_ + 1) * 512],
                    a_sc[pr_][q0_:q0_ + 64, g_ * 512:(g_ + 1) * 512],
                    c_mask[q0_:q0_ + 64, g_ * 512:(g_ + 1) * 512]))(p0, g, pr),
                deps=[t_s1], sig=True)
            sc_toks.append(t_s2)
            ict_free[tpg].append(t_s1)
        sc_free[pr] = []
        st["sc_toks"] = sc_toks

    def emit_omm(br, pn, ah, atl, c):
        st = rowst[br]
        tpg = st["tpg"]
        pr = st["pr"]
        half = c % 2
        p0 = 64 * half
        po = st["po"]
        skip_inter = (atl == 0 and c == 0)
        t_q1 = st["t_q1"]
        sc_toks = st["sc_toks"]
        o_evs = []
        bko = PB[R_OB[br]]
        for g in range(2):
            for j in range(8):
                h = g * 8 + j
                deps_o = ([sc_toks[g], st["vt_evs"][h // 4]] +
                          (bko.wdeps() if j == 0 else []))
                mm(bko.ap[:, j * 64:(j + 1) * 64],
                   view(a_vt[tpg], p0, 64, h * 128, [[1, 128]]),
                   a_sc[pr][p0:p0 + 64, h * 64:(h + 1) * 64],
                   True, skip_inter, deps=deps_o,
                   sig=(skip_inter and j == 7))
                if not skip_inter:
                    mm(bko.ap[:, j * 64:(j + 1) * 64],
                       a_S[br][:, h * 128:(h + 1) * 128],
                       a_q1[pr][:, h * CH:(h + 1) * CH],
                       False, True, deps=[S_write[br][h], t_q1],
                       sig=(j == 7))
            bko.wrote(("pe", S.cnt["pe"]))
            t_oe = S.ac(
                (lambda bkap, g_, c_, p_: lambda e: e.copy(
                    view(a_o[p_], 0, 128, (g_ * 8) * TA + c_ * CH, [[TA, 8], [1, CH]]),
                    bkap[:, 0:512].rearrange("q (h i) -> q h i", h=8)))(bko.ap, g, c, po),
                deps=bko.rdeps() + (o_free[po] if (g == 0 and c == 0) else []), sig=True)
            bko.read(t_oe)
            o_evs.append(t_oe)
        if c == 0:
            o_free[po] = []
        tile_o[(br, atl)].extend(o_evs)
        st["o_evs"] = o_evs
        st["o_pe"] = ("pe", S.cnt["pe"])

    def emit_ustt(br, c):
        st = rowst[br]
        tpg = st["tpg"]
        pr = st["pr"]
        half = c % 2
        p0 = 64 * half
        pre = st["pre"]
        skip_inter = (st["atl"] == 0 and c == 0)
        bku = PB[R_UB[br]]
        for g4 in range(4):
            t_uu = None
            for j in range(4):
                h = g4 * 4 + j
                t_uu = mm(bku.ap[:, j * 128:(j + 1) * 128],
                          view(a_kt[tpg], p0, 64, h * 128, [[1, 128]]),
                          view(a_vd[tpg], p0, 64, h * 128, [[1, 128]]),
                          True, True,
                          deps=([st["kt_evs"][h // 4], pre["t_vd"]] +
                                (bku.wdeps() if j == 0 else [])),
                          sig=(j == 3))
            bku.wrote(t_uu)
            for j in range(4):
                h = g4 * 4 + j
                if skip_inter:
                    t_su = S.dv(
                        (lambda bkap, b_, h_, j_: lambda e: e.tensor_copy(
                            a_S[b_][:, h_ * 128:(h_ + 1) * 128],
                            bkap[:, j_ * 128:(j_ + 1) * 128]))(bku.ap, br, h, j),
                        deps=bku.rdeps() + [st["o_pe"]] + S_first[br], sig=True)
                else:
                    t_su = S.dv(
                        (lambda bkap, b_, h_, j_, pr2: lambda e: e.scalar_tensor_tensor(
                            a_S[b_][:, h_ * 128:(h_ + 1) * 128],
                            a_S[b_][:, h_ * 128:(h_ + 1) * 128],
                            a_eto[:, b_ * 32 + pr2 * 16 + h_:b_ * 32 + pr2 * 16 + h_ + 1],
                            bkap[:, j_ * 128:(j_ + 1) * 128],
                            ALU.mult, ALU.add))(bku.ap, br, h, j, half),
                        deps=bku.rdeps() + [pre["t_ev"], st["o_pe"]], sig=True)
                S_write[br][h] = t_su
                bku.read(t_su)
                eto_free_r[br].append(t_su)
        q1_free[pr] = [("pe", S.cnt["pe"])]
        cq_free[pr] = [st["t_q1"]]
        sc_free[pr] = [("pe", S.cnt["pe"])]
        vd_free[tpg] = [S_write[br][H - 1]]
        kt_free[tpg] = [S_write[br][H - 1]]
        vt_free[tpg] = [S_write[br][H - 1], ("pe", S.cnt["pe"])]
        crow_free[tpg] = [("pe", S.cnt["pe"])]
        ict_free[tpg].append(("dv", S.cnt["dv"]))

    def emit_u1(br, atl, fc0, fc1):
        st = rowst[br]
        po = st["po"]
        bko = PB[R_OB[br]]
        for fc in range(fc0, fc1):
            sqh = fc % 2
            t_sq = S.ac(
                (lambda f_, s_, p_: lambda e: e.activation(
                    a_sq[:, s_ * TA:(s_ + 1) * TA],
                    a_o[p_][:, f_ * TA:(f_ + 1) * TA], AF.Square))(fc, sqh, po),
                deps=tile_o[(br, atl)] + sq_free[sqh], sig=True)
            t_mmq = mm(bko.ap[0:1, 0:TA], c_ones_cb[:, :],
                       a_sq[:, sqh * TA:(sqh + 1) * TA],
                       fc == 0, fc == KC - 1,
                       deps=[t_sq] + (bko.wdeps() if fc == 0 else []),
                       sig=True)
            sq_free[sqh] = [t_mmq]
            t_nw = S.dv(
                (lambda f_, p_: lambda e: e.tensor_scalar(
                    a_o[p_][:, f_ * TA:(f_ + 1) * TA],
                    a_o[p_][:, f_ * TA:(f_ + 1) * TA],
                    nw_col[:, f_:f_ + 1], None, ALU.mult))(fc, po),
                deps=[t_sq], sig=True)
            st["nw_toks"].append(t_nw)
            if fc == KC - 1:
                bko.wrote(t_mmq)
                t_lnr = S.ac(
                    (lambda bkap: lambda e: e.activation(
                        a_r1[0:1, 0:TA], bkap[0:1, 0:TA],
                        AF.Ln, scale=1.0 / D, bias=c_cval[0:1, 3:4]))(bko.ap),
                    deps=bko.rdeps() + r1_free[0], sig=True)
                bko.read(t_lnr)
                t_rr = S.ac(
                    lambda e: e.activation(a_r1[0:1, TA:2 * TA], a_r1[0:1, 0:TA],
                                           AF.Exp, scale=-0.5), deps=[t_lnr], sig=True)
                bk7 = PB[R_UB[br]]
                t_rc = None
                for k4 in range(NTT):
                    t_rc = mm(bk7.ap[:, k4:k4 + 1],
                              a_r1[0:1, TA + k4 * 128:TA + (k4 + 1) * 128],
                              c_ones_rf[0:1, 0:1], True, True,
                              deps=[t_rr] + (bk7.wdeps() if k4 == 0 else []),
                              sig=(k4 == NTT - 1))
                bk7.wrote(t_rc)
                gtok = br * 16 + atl * 2
                t_rce = S.dv(
                    (lambda c_: lambda e: e.tensor_copy(
                        st_rstd2[:, c_:c_ + NTT], bk7.ap[:, 0:NTT]))(gtok),
                    deps=bk7.rdeps(), sig=True)
                bk7.read(t_rce)
                r1_free[0] = [t_rce, t_rr]
                t_sp = S.qs(
                    (lambda p_, g_: lambda e: e.dma_start(
                        out=on_hbm[g_, :, :], in_=a_o[p_][:, :]))(po, br * 8 + atl),
                    deps=st["nw_toks"], ch=f"on{po % 2}")
                o_free[po] = [t_sp]

    def row_stream(br):
        st = rowst[br]
        for nt in range(NTR):
            st["nt"] = nt
            pn = (2 * nt + br) % 2
            a2_loads(br, nt)
            for ah in range(2):
                atl = nt * 2 + ah
                st["atl"] = atl
                st["po"] = acnt[0] % 2
                acnt[0] += 1
                st["nw_toks"] = []
                tile_o[(br, atl)] = []
                for tt in range(NTT):
                    tg_row = atl * 2 + tt
                    emit_pre_a(br, pn, ah, tt)
                    yield
                    emit_pre_b(br, pn, ah, tt)
                    yield
                    emit_pre_c(br, pn, ah, tt, tg_row)
                    yield
                    for half in range(2):
                        c = 2 * tt + half
                        emit_cpre(br, pn, ah, c)
                        ccnt[0] += 1
                        prep_steps(1, quant=True)
                        yield
                        emit_scores(br, pn, ah, c)
                        prep_steps(1, quant=True)
                        yield
                        emit_omm(br, pn, ah, atl, c)
                        prep_steps(1, quant=True)
                        yield
                        emit_ustt(br, c)
                        prep_steps(1, quant=True)
                        yield
                emit_u1(br, atl, 0, 4)
                prep_steps(2, quant=True)
                yield
                emit_u1(br, atl, 4, 8)
                prep_steps(2, quant=True)
                yield
                emit_u1(br, atl, 8, 12)
                prep_steps(2, quant=True)
                yield
                emit_u1(br, atl, 12, KC)
                prep_steps(2, quant=True)
                yield
            qn_free[pn] = [("dv", S.cnt["dv"])]
            kn_free[pn] = [("pe", S.cnt["pe"])]
            vn_free[pn] = [("pe", S.cnt["pe"])]

    if DEV_A2:
        streams = [row_stream(br) for br in range(DEV_NROW)]
        alive = list(streams)
        while alive:
            for g in list(alive):
                try:
                    next(g)
                except StopIteration:
                    alive.remove(g)


    if DEV_PREP:
        while PS["wi"] < len(WO_UNITS):
            prep_wo_step()
        while PS["gi"] < len(G_UNITS):
            prep_gamma_step()
        if PS["inv"] is None:
            prep_gamma_finalize()

    def qdep(mi, ci):
        if not DEV_PREP:
            return []
        need = 2 if mi < 2 else 6
        while len(PS["qsdone"].get((mi, ci), ())) < need and PS["qi"] < len(Q_UNITS):
            prep_quant_step()
        return PS["qsdone"].get((mi, ci), [])

    FB = S.fence()
    bxn_free = []
    bhid_free = []
    bon_free = [FB]
    bq1_free = [FB, FB]
    bq2_free = [FB, FB]
    bq3_free = [FB, FB]
    bwo_free = [FB, FB]
    bsig_free = [[], []]
    bg_free = [[], []]
    bs_free = [FB, FB]
    bsg_free = [[], []]
    bx_free = [FB]
    bxt_free = [FB]
    bxc_free = [FB, FB]
    bos_free = [[], []]

    for bt in range(DEV_NBT):
        b = bt // (NBT // BPC)
        t0 = (bt % (NBT // BPC)) * TB
        g0 = bt * TB
        gtt0 = g0 // 128

        xn_evs2 = []
        for tt in range(TB // 128):
            bi = 0
            ttg = gtt0 + tt
            t_x = S.qs(
                (lambda b_, bb, r0: lambda e: e.dma_start(
                    out=b_x[b_][:, :], in_=x_hbm[bb, r0:r0 + 128, :]))(bi, b, t0 + tt * 128),
                deps=[bx_free[bi]] + FB, ch="bx")
            t_ap = S.ac(
                (lambda b_, c: lambda e: e.activation(
                    b_xt[b_][:, :], b_x[b_][:, :], AF.Identity,
                    scale=st_scale[:, c:c + 1], bias=st_bias[:, c:c + 1]))(bi, ttg),
                deps=[t_x, bxt_free[bi]] + FB, sig=True)
            bx_free[bi] = t_ap
            grp = []
            for g in range(4):
                bk = PB[6 + g % 2]
                t_tl = None
                for j in range(4):
                    fcb = g * 4 + j
                    t_tl = mm(bk.ap[:, j * 128:(j + 1) * 128],
                              b_xt[bi][:, fcb * 128:(fcb + 1) * 128],
                              c_i128b[:, :], True, True,
                              deps=[t_ap] + (bk.wdeps() if j == 0 else []) + FB,
                              sig=(j == 3))
                bk.wrote(t_tl)
                t_ev = S.ac(
                    (lambda bkap, g_, tt_: lambda e: e.copy(
                        view(b_xn, 0, 128, (g_ * 4) * TB + tt_ * 128, [[TB, 4], [1, 128]]),
                        bkap[:, :].rearrange("p (a b) -> p a b", a=4)))(bk.ap, g, tt),
                    deps=bk.rdeps() + bxn_free, sig=True)
                bk.read(t_ev)
                grp.append(t_ev)
            bxt_free[bi] = grp
            xn_evs2.extend(grp)
        bxn_free = []

        if bt == 0 and "xnb0" in tap_hbm:
            S.qa(lambda e: e.dma_start(out=tap_hbm["xnb0"][:, :], in_=b_xn[:, :]),
                 deps=xn_evs2, ch='tap')

        hid_toks = []
        for hc in range(HC):
            sbi = hc % 2
            if bt == 0:
                prep_steps(3, quant=True)
            t_w1 = S.qs(
                (lambda bb, h_: lambda e: e.dma_start(out=bb[:, :], in_=q1t[h_, :, :]))(
                    b_q1s[sbi], hc),
                deps=[bq1_free[sbi]] + FB + qdep(0, hc), ch=f"bq1_{sbi}")
            t_w2 = S.qs(
                (lambda bb, h_: lambda e: e.dma_start(out=bb[:, :], in_=q2t[h_, :, :]))(
                    b_q2s[sbi], hc),
                deps=[bq2_free[sbi]] + FB + qdep(1, hc), ch=f"bq2_{sbi}")
            pbase1 = 0 if hc % 2 == 0 else 4
            pbase2 = 2 if hc % 2 == 0 else 6
            sig_toks = []
            for half in range(2):
                bk1 = PB[pbase1 + half]
                t_l1 = None
                for kc in range(KC):
                    t_l1 = mm(bk1.ap[:, 0:512],
                              b_q1s[sbi][:, kc * 128:(kc + 1) * 128],
                              b_xn[:, kc * TB + half * 512: kc * TB + (half + 1) * 512],
                              kc == 0, kc == KC - 1,
                              deps=([t_w1] + xn_evs2 + (bk1.wdeps() if kc == 0 else [])),
                              sig=(kc == KC - 1))
                bk1.wrote(t_l1)
                t_sg = S.ac(
                    (lambda bkap, s_, h_: lambda e: e.activation(
                        b_sig[s_][:, h_ * 512:(h_ + 1) * 512], bkap[:, 0:512],
                        AF.Sigmoid))(bk1.ap, sbi, half),
                    deps=bk1.rdeps() + bsig_free[sbi], sig=True)
                bk1.read(t_sg)
                sig_toks.append(t_sg)
            bsig_free[sbi] = []
            t_w1_last = t_l1
            for half in range(2):
                bk2 = PB[pbase2 + half]
                t_l2 = None
                for kc in range(KC):
                    t_l2 = mm(bk2.ap[:, 0:512],
                              b_q2s[sbi][:, kc * 128:(kc + 1) * 128],
                              b_xn[:, kc * TB + half * 512: kc * TB + (half + 1) * 512],
                              kc == 0, kc == KC - 1,
                              deps=([t_w2] + xn_evs2 + (bk2.wdeps() if kc == 0 else [])),
                              sig=(kc == KC - 1))
                bk2.wrote(t_l2)
                t_h = S.dv(
                    (lambda bkap, s_, h_, hc_: lambda e: e.tensor_mul(
                        b_hid[:, hc_ * TB + h_ * 512: hc_ * TB + (h_ + 1) * 512],
                        b_sig[s_][:, h_ * 512:(h_ + 1) * 512],
                        bkap[:, 0:512]))(bk2.ap, sbi, half, hc),
                    deps=bk2.rdeps() + [sig_toks[half]] + bhid_free, sig=True)
                bk2.read(t_h)
                hid_toks.append(t_h)
                bsig_free[sbi].append(t_h)
            bq1_free[sbi] = t_w1_last
            bq2_free[sbi] = t_l2

        xn_done = [("pe", S.cnt["pe"])]
        on_evs = []
        for aq in range(4):
            on_evs.append(S.qs(
                (lambda a_, g_: lambda e: e.dma_start(
                    out=b_on[:, a_ * KC * TA:(a_ + 1) * KC * TA],
                    in_=on_hbm[g_, :, :]))(aq, bt * 4 + aq),
                deps=[bon_free[0]] + xn_done + FB, ch="bon"))

        if bt == 0 and "hid0" in tap_hbm:
            S.qa(lambda e: e.dma_start(out=tap_hbm["hid0"][:, :], in_=b_hid[:, :]),
                 deps=hid_toks, ch='tap')

        last_x3 = None
        for fc in range(KC):
            sbi = fc % 2
            bqa = fc % 2
            bqb = 1 - bqa
            if bt == 0:
                prep_steps(4, quant=True)
            t_w3 = S.qs(
                (lambda bb, f_: lambda e: e.dma_start(
                    out=bb[:, :], in_=q3t[f_, :, 0:FH // 2]))(b_q3s[bqa], fc),
                deps=[bq3_free[bqa]] + FB + qdep(2, fc), ch=f"bq3_{bqa}")
            t_w3b = S.qs(
                (lambda bb, f_: lambda e: e.dma_start(
                    out=bb[:, :], in_=q3t[f_, :, FH // 2:FH]))(b_q3s[bqb], fc),
                deps=[bq3_free[bqb]] + FB + qdep(2, fc), ch=f"bq3_{bqb}")
            t_wo = S.qs(
                (lambda bb, f_: lambda e: e.dma_start(out=bb[:, :], in_=wo_b[f_, :, :]))(
                    b_wob[0], fc),
                deps=[bwo_free[0]] + FB, ch="bwo0")
            t_xc = S.qg(
                (lambda bb, bb_, r0, f_: lambda e: e.dma_start(
                    out=bb[:, :].rearrange("p (tb f) -> p tb f", tb=8),
                    in_=x_hbm[bb_, r0:r0 + TB, f_ * 128:(f_ + 1) * 128].rearrange(
                        "(tb p) f -> p tb f", p=128)))(b_xc[sbi], b, t0, fc),
                deps=[bxc_free[sbi]] + FB, ch=f"bxcg{sbi}")
            s_toks = []
            t_sl = None
            for aq in range(4):
                bks = PB[6 + aq % 2]
                t_ls = None
                for kc in range(KC):
                    t_ls = mm(bks.ap[:, 0:TA],
                              b_wob[0][:, kc * 128:(kc + 1) * 128],
                              b_on[:, aq * KC * TA + kc * TA:aq * KC * TA + (kc + 1) * TA],
                              kc == 0, kc == KC - 1,
                              deps=([t_wo, on_evs[aq]] + (bks.wdeps() if kc == 0 else [])),
                              sig=(kc == KC - 1))
                bks.wrote(t_ls)
                t_sl = t_ls
                t_se = S.ac(
                    (lambda bkap, s_, a_: lambda e: e.copy(
                        b_s[s_][:, a_ * TA:(a_ + 1) * TA], bkap[:, 0:TA]))(bks.ap, sbi, aq),
                    deps=bks.rdeps() + ([bs_free[sbi]] + hid_toks[-1:]
                                        if aq == 0 else []), sig=True)
                bks.read(t_se)
                s_toks.append(t_se)
            bwo_free[0] = t_sl
            pb0 = 0 if fc % 2 == 0 else 2
            g_toks = []
            tok_lo = None
            for half in range(2):
                bk = PB[pb0 + half]
                t_l3 = None
                for hcp in range(HC):
                    hhalf = 0 if hcp < 22 else 1
                    hcl = hcp - 22 * hhalf
                    t_l3 = mm(bk.ap[:, 0:512],
                              b_q3s[bqa if hhalf == 0 else bqb][:, hcl * 128:(hcl + 1) * 128],
                              b_hid[:, hcp * TB + half * 512: hcp * TB + (half + 1) * 512],
                              hcp == 0, hcp == HC - 1,
                              deps=([t_w3, t_w3b] + hid_toks + (bk.wdeps() if hcp == 0 else [])),
                              sig=(hcp == 21 or hcp == HC - 1))
                    if hcp == 21 and half == 1:
                        tok_lo = t_l3
                bk.wrote(t_l3)
                last_x3 = t_l3
                t_g = S.ac(
                    (lambda bkap, s_, h_: lambda e: e.activation(
                        b_g[s_][:, h_ * 512:(h_ + 1) * 512], bkap[:, 0:512],
                        AF.Silu))(bk.ap, sbi, half),
                    deps=bk.rdeps() + bg_free[sbi] + xn_done, sig=True)
                bk.read(t_g)
                g_toks.append(t_g)
            bq3_free[bqa] = tok_lo
            bq3_free[bqb] = last_x3
            t_sgm = S.dv(
                (lambda s_: lambda e: e.tensor_mul(
                    b_g[s_][:, :], b_g[s_][:, :], b_s[s_][:, :]))(sbi),
                deps=g_toks + s_toks + bsg_free[sbi], sig=True)
            bg_free[sbi] = [t_sgm]
            bs_free[sbi] = t_sgm
            bsig_free[sbi].append(t_sgm)
            bxt_free[0] = (bxt_free[0] if isinstance(bxt_free[0], list)
                           else [bxt_free[0]]) + [t_sgm]
            st_toks = []
            for tb in range(8):
                bk = PB[4 + tb % 2]
                t_tp = mm(bk.ap[:, 0:128],
                          b_g[sbi][:, tb * 128:(tb + 1) * 128],
                          c_i128b[:, :], True, True,
                          deps=[t_sgm] + bk.wdeps(), sig=True)
                bk.wrote(t_tp)
                t_st = S.dv(
                    (lambda bkap, s_, tb_, c: lambda e: e.scalar_tensor_tensor(
                        b_os[s_][:, tb_ * 128:(tb_ + 1) * 128],
                        bkap[:, 0:128], st_rstd2[:, c:c + 1],
                        b_xc[s_][:, tb_ * 128:(tb_ + 1) * 128],
                        ALU.mult, ALU.add))(bk.ap, sbi, tb, gtt0 + tb),
                    deps=bk.rdeps() + [t_xc] + bos_free[sbi], sig=True)
                bk.read(t_st)
                st_toks.append(t_st)
            bos_free[sbi] = []
            bsg_free[sbi] = [("pe", S.cnt["pe"])]
            bxc_free[sbi] = st_toks[-1]
            t_od = S.qa(
                (lambda s_, bb_, r0, f_: lambda e: e.dma_start(
                    out=out_hbm[bb_, r0:r0 + TB, f_ * 128:(f_ + 1) * 128].rearrange(
                        "(tb p) f -> p tb f", p=128),
                    in_=b_os[s_][:, :].rearrange("p (tb f) -> p tb f", tb=8)))(
                    sbi, b, t0, fc),
                deps=st_toks, ch=f"out{sbi}")
            bos_free[sbi] = [t_od]
        bhid_free = [last_x3]
        bon_free = [t_sl]
        bxn_free = [t_sl]

    fin = [(ch, S.cnt[ch]) for ch in S.cnt if ch.startswith("out")]
    S.emit("qg", lambda e: e.nop(), deps=fin)

    with nc.Block() as block:
        S.replay(block)

    return nc


def _get_nc():
    if "nc" not in _CACHE:
        _CACHE["nc"] = build_program()
    return _CACHE["nc"]


_CACHE = {}


def kernel(**inputs):
    nc = _get_nc()
    lay = _host_layouts(inputs)
    x = np.asarray(inputs["L0_attn_x"], np.float32)
    in_maps = []
    for k in range(NCORE):
        m = {"x": np.ascontiguousarray(x[BPC * k:BPC * (k + 1)])}
        m.update(lay)
        in_maps.append(m)
    res = run_bass_kernel_spmd(nc, in_maps, core_ids=list(range(NCORE)))
    out = np.concatenate([r["out"] for r in res.results], axis=0)
    return out.astype(np.float32)
